# version 17
# baseline (speedup 1.0000x reference)
import numpy as np
import ml_dtypes
import concourse.bass as bass
import concourse.mybir as mybir
from concourse.bass_utils import run_bass_kernel_spmd

F32 = mybir.dt.float32
BF16 = mybir.dt.bfloat16
U8 = mybir.dt.uint8
AF = mybir.ActivationFunctionType
ALU = mybir.AluOpType
AX = mybir.AxisListType

DM = 1024
DI = 2048
S = 128
Q = 128
NCH = 16
BLK = 8
T = 2048
TW = 2176
NT = TW // 128
J = 4353
JT = 34
WF = 12288
E = 8
WIDTHS = [1024] * 4 + [2048] * 4
OFFS = np.concatenate([[0], np.cumsum(WIDTHS)]).astype(int)
CLIP = 8.0
EPS = 1e-5
TBATCH = [(0, 512), (512, 512), (1024, 512), (1536, 512), (2048, 128)]
MM2G = [(0, 3), (3, 3), (6, 3), (9, 3), (12, 3), (15, 1)]

ENGINES = ("tensor", "vector", "scalar", "gpsimd", "sync")
STRICT_SAME_ENGINE = True
DEBUG_PHASE = None


class Prog:

    def __init__(self):
        self.steps = []
        self.cnt = {e: 0 for e in ENGINES}
        self.cnt["cc"] = 0
        self.lastw = {}
        self.readers = {}

    def step(self, engine, emit, reads=(), writes=(), dma=False, tag="",
             done_sem=None, extra_waits=()):
        deps = {}

        def dep(sem, val, asyn):
            if sem == engine and not asyn and not STRICT_SAME_ENGINE:
                return
            if deps.get(sem, 0) < val:
                deps[sem] = val

        for t in reads:
            w = self.lastw.get(t)
            if w:
                dep(*w)
        for t in writes:
            w = self.lastw.get(t)
            if w:
                dep(*w)
            for r in self.readers.get(t, ()):
                dep(*r)
        for sem, val in extra_waits:
            if deps.get(sem, 0) < val:
                deps[sem] = val
        if done_sem is not None:
            sem, amt = done_sem
        else:
            sem, amt = engine, (16 if dma else 1)
        self.cnt[sem] = self.cnt.get(sem, 0) + amt
        rec = (sem, self.cnt[sem], dma or done_sem is not None)
        for t in reads:
            self.readers.setdefault(t, []).append(rec)
        for t in writes:
            self.lastw[t] = rec
            self.readers[t] = []
        self.steps.append((engine, emit, deps, (sem, amt), tag))
        return rec

    def barrier(self):
        snapshot = dict(self.cnt)
        for e in ENGINES:
            waits = {s: v for s, v in snapshot.items() if v > 0 and s != e}
            self.steps.append((e, None, waits, (e, 0), "barrier"))
        self.lastw.clear()
        self.readers.clear()

    def final_wait(self, engine="gpsimd"):
        waits = {s: v for s, v in self.cnt.items() if v > 0}
        self.steps.append((engine, None, waits, (engine, 0), "final"))

    def emit(self, nc, block, sems):
        per = {e: [] for e in ENGINES}
        for stp in self.steps:
            per[stp[0]].append(stp)

        def mk(engine, steps):
            def body(eng):
                satisfied = {}
                for _, emitfn, deps, (sem, amt), tag in steps:
                    for s9, v in sorted(deps.items()):
                        if satisfied.get(s9, 0) >= v:
                            continue
                        eng.wait_ge(sems[s9], v)
                        satisfied[s9] = v
                    if emitfn is None:
                        continue
                    ins = emitfn(eng)
                    if amt > 0:
                        if ins is None:
                            raise RuntimeError(f"step {tag} returned no instruction")
                        ins.then_inc(sems[sem], amt)
            return body

        for e in ENGINES:
            if per[e]:
                getattr(block, e)(mk(e, per[e]))


def build_program():
    nc = bass.Bass("TRN2", target_bir_lowering=False)
    p = Prog()

    def inp(name, shape, dtype=F32):
        return nc.dram_tensor(name, shape, dtype, kind="ExternalInput").ap()

    xin = inp("xin", [TW, DM])
    dp_b = inp("dp_b", [128, DI])
    WinT = inp("WinT", [DM, J])
    w1 = inp("w1", [DM, WF])
    w2 = inp("w2", [WF, DM])
    WoutT = inp("WoutT", [DI, DM])
    small9 = inp("small9", [DM, 9])
    A_log_row = inp("A_log_row", [1, S])
    cw_col = inp("cw_col", [128, 64])
    dp_col = inp("dp_col", [128, 16])
    nw_col = inp("nw_col", [128, 16])
    sel8 = inp("sel8", [128, 8])
    Mrows = inp("Mrows", [128, 512])
    maskoff = inp("maskoff", [128, 128])
    bandmask = inp("bandmask", [128, 8 * 128], U8)
    idf32 = inp("idf32", [128, 128])
    eyerow8 = inp("eyerow8", [8, 8 * 128])

    out = nc.dram_tensor("out", [T, DM], F32, kind="ExternalOutput").ap()

    xnhi_scr = nc.dram_tensor("xnhi_scr", [TW, DM], BF16, kind="Internal").ap()
    xnlo_scr = nc.dram_tensor("xnlo_scr", [TW, DM], BF16, kind="Internal").ap()
    xnT_scr = nc.dram_tensor("xnT_scr", [128, 8 * TW], BF16, kind="Internal").ap()
    z_scr = nc.dram_tensor("z_scr", [DI, TW], BF16, kind="Internal").ap()
    xm_scr = nc.dram_tensor("xm_scr", [NCH * 128, 2048], BF16, kind="Internal").ap()
    xmT_scr = nc.dram_tensor("xmT_scr", [T, DI], BF16, kind="Internal").ap()
    H_scr = nc.dram_tensor("H_scr", [WF, T], BF16, kind="Internal").ap()
    g_scr = nc.dram_tensor("g_scr", [T, DM], BF16, kind="Internal").ap()
    ag_in = nc.dram_tensor("ag_in", [S, DI], F32, kind="Internal")
    ag_out = nc.dram_tensor("ag_out", [8 * S, DI], F32, kind="Internal")

    PB = [nc.alloc_psum_tensor(f"PB{i}", [128, 512], F32) for i in range(8)]
    BT = [f"B{i}" for i in range(8)]

    def sball(name, shape, dtype=F32):
        return nc.alloc_sbuf_tensor(name, shape, dtype)

    dts_tok = sball("dts_tok", [128, NT])
    combine = sball("combine", [128, NT * 8])
    A_row = sball("A_row", [1, 128])
    ones1 = sball("ones1", [1, 128])
    onesc = sball("onesc", [128, 1])
    rows_sb = [sball(f"rows_sb{i}", [1, 512]) for i in range(2)]
    rQrow = [sball(f"rQrow{i}", [1, 128]) for i in range(2)]
    rstdrow = [sball(f"rstdrow{i}", [1, 128]) for i in range(2)]
    ck_sb = [sball(f"ck_sb{i}", [1, 1024]) for i in range(2)]
    l8row = [sball(f"l8row{i}", [1, 16]) for i in range(2)]
    Mrows_sb = sball("Mrows_sb", [128, 512])
    idb_sb = sball("idb_sb", [128, 128], BF16)
    maskoff_sb = sball("maskoff_sb", [128, 128])
    bandmask_sb = sball("bandmask_sb", [128, 8 * 128], U8)
    idf_sb = sball("idf_sb", [128, 128])
    eyerow8_sb = sball("eyerow8_sb", [8, 8 * 128])
    cw_sb = sball("cw_sb", [128, 64])
    dp_sb = sball("dp_sb", [128, 16])
    nw_sb = sball("nw_sb", [128, 16])
    sel8_sb = sball("sel8_sb", [128, 8])
    small_f = sball("small_f", [128, 8 * 9])
    small_hi = sball("small_hi", [128, 8 * 9], BF16)
    small_lo = sball("small_lo", [128, 8 * 9], BF16)
    smallhl = sball("smallhl", [128, 8 * 18], BF16)
    lnc = [sball(f"lnc{i}", [128, 16]) for i in range(2)]
    headf = sball("headf", [128, 16])
    sqc_t = sball("sqc_t", [128, 8])
    dccol = [sball(f"dccol{i}", [128, 1]) for i in range(2)]
    d8 = [sball(f"d8{i}", [128, 16]) for i in range(2)]
    srow_t = sball("srow_t", [1, 128])
    cbias = sball("cbias", [128, 2])
    rstd_col = [sball(f"rstd_col{i}", [128, 1]) for i in range(2)]

    arena_pp = 178 * 1024
    rem_pp = nc.sbuf_bytes_remaining // 128
    got = nc.bump_sbuf(arena_pp)
    assert got is not None, f"arena alloc failed rem={rem_pp}"
    abase = got[0]

    def at(name, off_pp, shape, dtype=F32):
        return nc.alloc_sbuf_tensor_at(name, shape, dtype, offset=abase + off_pp)

    K = 1024
    xnT_hi = at("xnT_hi", 0, [128, 8 * TW], BF16)
    w1buf = [at("w1buf0", 0, [128, 8 * 512], BF16),
             at("w1buf1", 8 * K, [128, 8 * 512], BF16)]
    Hrelu = at("Hrelu", 16 * K, [128, 2048], BF16)
    Hsqb = [at("Hsqb0", 20 * K, [128, 2048], BF16),
            at("Hsqb1", 24 * K, [128, 2048], BF16)]
    hbuf4 = [at("hbuf4_0", 0, [128, 8 * 1024], BF16),
             at("hbuf4_1", 16 * K, [128, 8 * 1024], BF16)]
    WoutT_sb = at("WoutT_sb", 0, [128, 16 * 1024], BF16)

    B0 = 34 * K
    xnT_lo = at("xnT_lo", B0, [128, 8 * TW], BF16)
    prebuf = [at("prebuf0", B0, [128, TW], BF16),
              at("prebuf1", B0 + 4352, [128, TW], BF16)]
    wbuf = [at("wbuf0", B0 + 8704, [128, 8 * 128], BF16),
            at("wbuf1", B0 + 10752, [128, 8 * 128], BF16)]
    zbuf = [at("zbuf0", B0 + 12800, [128, 512], BF16),
            at("zbuf1", B0 + 13824, [128, 512], BF16)]
    cvt0 = [at("cvt0a", B0 + 14848, [128, 512]),
            at("cvt0b", B0 + 16896, [128, 512])]
    cvt1 = [at("cvt1a", B0 + 19 * K, [128, 1024]),
            at("cvt1b", B0 + 23 * K, [128, 1024])]
    xnTs = [at("xnTs0", B0, [128, 2048], BF16),
            at("xnTs1", B0 + 4 * K, [128, 2048], BF16)]
    sq_fm = at("sq_fm", B0 + 8 * K, [8, 2048])
    sqb_sb = at("sqb_sb", B0 + 16 * K, [128, 2048], BF16)
    w2buf = [at("w2buf0", B0, [128, 8 * 1024], BF16),
             at("w2buf1", B0 + 16 * K, [128, 8 * 1024], BF16)]
    zc = [at("zc0", B0, [128, 2048], BF16),
          at("zc1", B0 + 4 * K, [128, 2048], BF16)]
    gbuf = [at("gbuf0", B0 + 8 * K, [128, 1024], BF16),
            at("gbuf1", B0 + 10 * K, [128, 1024], BF16)]
    xres = [at("xres0", B0 + 12 * K, [128, 1024]),
            at("xres1", B0 + 16 * K, [128, 1024])]
    outbuf = [at("outbuf0", B0 + 20 * K, [128, 1024]),
              at("outbuf1", B0 + 24 * K, [128, 1024])]
    dccol_all = at("dccol_all", B0 + 32 * K, [128, 16])

    C0 = 68 * K
    cvacc = [at("cvacc0", C0, [128, 2048]),
             at("cvacc1", C0 + 8 * K, [128, 2048])]
    cvt2 = [at("cvt2a", C0 + 16 * K, [128, 1024]),
            at("cvt2b", C0 + 20 * K, [128, 1024])]
    xmout = [at("xmout0", C0 + 24 * K, [128, 2048], BF16),
             at("xmout1", C0 + 28 * K, [128, 2048], BF16)]
    y_fm = [at("y_fm0", C0 + 32 * K, [128, 2048]),
            at("y_fm1", C0 + 40 * K, [128, 2048])]
    ysqh = [at("ysqh0", C0 + 16 * K, [128, 1024]),
            at("ysqh1", C0 + 20 * K, [128, 1024])]
    xmc = [at("xmc0", C0 + 24 * K, [128, 2048], BF16),
           at("xmc1", C0 + 28 * K, [128, 2048], BF16)]
    xm_tok = [at("xm_tok0", C0 + 48 * K, [128, 2048], BF16),
              at("xm_tok1", C0 + 52 * K, [128, 2048], BF16)]
    yg = [at("yg0", C0 + 0 * K, [128, 2048], BF16),
          at("yg1", C0 + 4 * K, [128, 2048], BF16)]
    sgbuf = [at("sgbuf0", C0 + 0 * K, [128, 1024]),
             at("sgbuf1", C0 + 4 * K, [128, 1024])]
    Zbuf = [at("Zbuf0", C0 + 56 * K, [128, 1024]),
            at("Zbuf1", C0 + 60 * K, [128, 1024])]
    Bendtok_all = at("Bendtok_all", C0 + 8 * K, [128, 2048], BF16)
    ckb = [at("ckb0", C0 + 12 * K, [128, 1024], BF16),
           at("ckb1", C0 + 14 * K, [128, 1024], BF16)]
    Wsb_all = at("Wsb_all", C0 + 16 * K, [128, 2048], BF16)
    Cs_all = at("Cs_all", C0 + 20 * K, [128, 2048], BF16)
    Dp_bcast = at("Dp_bcast", C0 + 24 * K, [128, 2048])
    xmdp = at("xmdp", C0 + 56 * K, [128, 2048])
    xmst = [at("xmst0", C0 + 32 * K, [128, 2048], BF16),
            at("xmst1", C0 + 36 * K, [128, 2048], BF16)]

    D0 = 132 * K
    S_run = at("S_run", D0, [128, 2048])
    B_fm = at("B_fm", D0 + 8 * K, [128, TW], BF16)
    C_fm = at("C_fm", D0 + 8 * K + 4352, [128, TW], BF16)
    S_run_bf = [at("S_run_bf0", D0 + 8 * K, [128, 2048], BF16),
                at("S_run_bf1", D0 + 12 * K, [128, 2048], BF16)]
    DS = D0 + 17 * K
    xnT_e = at("xnT_e", DS, [128, 8 * 2048], BF16)
    gout = [at("gout0", DS, [128, 1024], BF16),
            at("gout1", DS + 2 * K, [128, 1024], BF16)]
    MS = []
    for par in range(2):
        base = DS + par * 7 * K
        MS.append(dict(
            dtsb=at(f"dtsb{par}", base, [128, 128]),
            ebufA=at(f"ebufA{par}", base + 512, [128, 128]),
            ebufB=at(f"ebufB{par}", base + 1024, [128, 128]),
            F1=at(f"F1_{par}", base + 1536, [128, 128]),
            F2base=at(f"F2base{par}", base + 2048, [128, 128]),
            F2run=at(f"F2run{par}", base + 2560, [128, 128]),
            Btld=at(f"Btld{par}", base + 3072, [128, 128]),
            Bend=at(f"Bend{par}", base + 3584, [128, 128]),
            Bendtok=at(f"Bendtok{par}", base + 4096, [128, 128], BF16),
            Cs=at(f"Cs{par}", base + 4352, [128, 128], BF16),
            A_bar=at(f"A_bar{par}", base + 4608, [128, 128]),
            Ppp=[at(f"Ppp{par}_0", base + 5120, [128, 128]),
                 at(f"Ppp{par}_1", base + 5632, [128, 128])],
            zcw=at(f"zcw{par}", base + 6144, [128, 128]),
            Wsb=at(f"Wsb{par}", base + 6656, [128, 128], BF16),
        ))
    F2runB = [at("F2runB0", DS + 14 * K, [128, 128]),
              at("F2runB1", DS + 14 * K + 512, [128, 128])]
    ACC_OFFS = [D0, D0 + 4 * K, D0 + 8 * K, D0 + 12 * K,
                C0, C0 + 4 * K, C0 + 12 * K, C0 + 24 * K,
                C0 + 28 * K, C0 + 32 * K, C0 + 36 * K, C0 + 40 * K,
                C0 + 44 * K, C0 + 48 * K, C0 + 52 * K, C0 + 56 * K]
    accs = [at(f"accs{t}", off, [128, 1024]) for t, off in enumerate(ACC_OFFS)]
    PS1 = DS
    xq = [at("xq0", PS1, [128, 1024]), at("xq1", PS1 + 4 * K, [128, 1024])]
    lnjunk = at("lnjunk", PS1 + 8 * K, [128, 1024])
    xnf = [at("xnf0", PS1 + 12 * K, [128, 1024]),
           at("xnf1", PS1 + 16 * K, [128, 1024])]
    xnhib = [at("xnhib0", PS1 + 20 * K, [128, 1024], BF16),
             at("xnhib1", PS1 + 22 * K, [128, 1024], BF16)]
    xnlof = [at("xnlof0", PS1 + 24 * K, [128, 1024]),
             at("xnlof1", PS1 + 28 * K, [128, 1024])]

    def dma(queue, o, i, reads, writes, tag="", **kw):
        return p.step(queue, lambda e, o=o, i=i, kw=kw: e.dma_start(out=o, in_=i, **kw),
                      reads=reads, writes=writes, dma=True, tag=tag)

    def dmat(queue, o, i, reads, writes, tag=""):
        return p.step(queue, lambda e, o=o, i=i: e.dma_start_transpose(o, i),
                      reads=reads, writes=writes, dma=True, tag=tag)

    def st(engine, fn, reads=(), writes=(), tag="", extra_waits=()):
        return p.step(engine, fn, reads=reads, writes=writes, tag=tag,
                      extra_waits=extra_waits)

    dma("sync", Mrows_sb[:, :], Mrows, [], ["Mrows"])
    dma("sync", maskoff_sb[:, :], maskoff, [], ["maskoff"])
    dma("sync", bandmask_sb[:, :], bandmask, [], ["bandmask"])
    dma("sync", idf_sb[:, :], idf32, [], ["idf"])
    st("vector", lambda e: e.tensor_copy(idb_sb[:, :], idf_sb[:, :]),
       ["idf"], ["idb"])
    dma("sync", eyerow8_sb[:, :], eyerow8, [], ["eyerow8"])
    dma("sync", cw_sb[:, :], cw_col, [], ["cw"])
    dma("sync", dp_sb[:, :], dp_col, [], ["dp"])
    dma("sync", nw_sb[:, :], nw_col, [], ["nw"])
    dma("sync", sel8_sb[:, :], sel8, [], ["sel8"])
    dma("sync", small_f[:, :].rearrange("p (s e) -> p s e", s=8),
        small9.rearrange("(s p) e -> p s e", p=128),
        [], ["small_f"], allow_slow_non_contiguous=True)
    dma("sync", srow_t[:, :], A_log_row, [], ["alog"])
    st("vector", lambda e: e.memset(cbias[:, 0:1], EPS), [], ["cbias"])
    st("vector", lambda e: e.memset(cbias[:, 1:2], 1.0), [], ["cbias"])
    st("vector", lambda e: e.memset(ones1[:, :], 1.0), [], ["ones1"])
    st("vector", lambda e: e.memset(onesc[:, :], 1.0), [], ["onesc"])
    for par in range(2):
        st("vector", lambda e, par=par: e.memset(Zbuf[par][:, :], 0.0),
           [], [f"Zbuf{par}"])
    st("scalar", lambda e: e.activation(srow_t[:, :], srow_t[:, :], AF.Exp),
       ["alog"], ["alog"])
    st("vector", lambda e: e.tensor_scalar_mul(A_row[:, :], srow_t[:, :], -1.0),
       ["alog"], ["A_row"])
    st("vector", lambda e: e.tensor_copy(small_hi[:, :], small_f[:, :]),
       ["small_f"], ["small_hi"])
    st("vector", lambda e: e.tensor_sub(small_lo[:, :], small_f[:, :], small_hi[:, :]),
       ["small_f", "small_hi"], ["small_lo"])
    for s8 in range(8):
        st("vector", lambda e, s8=s8: e.tensor_copy(
            smallhl[:, s8 * 18:s8 * 18 + 9], small_hi[:, s8 * 9:(s8 + 1) * 9]),
           ["small_hi"], ["smallhl"])
        st("vector", lambda e, s8=s8: e.tensor_copy(
            smallhl[:, s8 * 18 + 9:s8 * 18 + 18], small_lo[:, s8 * 9:(s8 + 1) * 9]),
           ["small_lo"], ["smallhl"])

    for i in range(NT):
        pr = i % 2
        xb = xq[pr]
        xtok = f"xq{pr}"
        ln = lnc[pr]
        ltok = f"lnc{pr}"
        dma("sync", xb[:, :], xin[i * 128:(i + 1) * 128, :], [], [xtok], tag=f"ldx{i}")
        st("scalar", lambda e, xb=xb, ln=ln: e.activation(
            lnjunk[:, :], xb[:, :], AF.Square, accum_out=ln[:, 0:1]),
           [xtok, "lnjunk"], ["lnjunk", ltok], tag=f"ssq{i}")
        st("scalar", lambda e, xb=xb, ln=ln: e.activation(
            lnjunk[:, :], xb[:, :], AF.Copy, accum_out=ln[:, 1:2]),
           [xtok, "lnjunk"], ["lnjunk", ltok], tag=f"msum{i}")
        st("vector", lambda e, ln=ln: e.tensor_scalar_mul(ln[:, 2:3], ln[:, 1:2], 1.0 / DM),
           [ltok], [ltok])
        st("vector", lambda e, ln=ln: e.tensor_scalar_mul(ln[:, 3:4], ln[:, 0:1], 1.0 / DM),
           [ltok], [ltok])
        st("vector", lambda e, ln=ln: e.scalar_tensor_tensor(
            ln[:, 4:5], ln[:, 2:3], ln[:, 2:3], ln[:, 3:4], ALU.mult, ALU.subtract),
           [ltok], [ltok])
        st("scalar", lambda e, ln=ln: e.activation(ln[:, 5:6], ln[:, 4:5], AF.Ln,
                                                   scale=-1.0, bias=cbias[:, 0:1]),
           [ltok, "cbias"], [ltok])
        st("scalar", lambda e, ln=ln: e.activation(ln[:, 6:7], ln[:, 5:6], AF.Exp,
                                                   scale=-0.5),
           [ltok], [ltok])
        st("vector", lambda e, xb=xb, ln=ln, pr=pr: e.tensor_scalar(
            xnhib[pr][:, :], xb[:, :], ln[:, 2:3], ln[:, 6:7], ALU.subtract, ALU.mult),
           [xtok, ltok], [f"xnhib{pr}"], tag=f"xnhi{i}")
        st("vector", lambda e, xb=xb, ln=ln, pr=pr: e.tensor_scalar(
            xnf[pr][:, :], xb[:, :], ln[:, 2:3], ln[:, 6:7], ALU.subtract, ALU.mult),
           [xtok, ltok], [f"xnf{pr}"], tag=f"xnf{i}")
        st("vector", lambda e, pr=pr: e.tensor_sub(
            xnlof[pr][:, :], xnf[pr][:, :], xnhib[pr][:, :]),
           [f"xnf{pr}", f"xnhib{pr}"], [f"xnlof{pr}"], tag=f"xnlo{i}")
        for sl in range(8):
            bank = sl % 4
            st("tensor", lambda e, pr=pr, sl=sl, bank=bank: e.transpose(
                PB[bank][:, 0:128], xnf[pr][:, sl * 128:(sl + 1) * 128], idf_sb[:, :]),
               [f"xnf{pr}", "idf"], [BT[bank]], tag=f"trh{i}_{sl}")
            st("vector", lambda e, i=i, sl=sl, bank=bank: e.tensor_copy(
                xnT_hi[:, sl * TW + i * 128: sl * TW + (i + 1) * 128],
                PB[bank][:, 0:128]),
               [BT[bank], f"xnT_hi_{sl}"], [f"xnT_hi_{sl}"], tag=f"trhev{i}_{sl}")
            st("tensor", lambda e, pr=pr, sl=sl, bank=bank: e.transpose(
                PB[bank][:, 128:256], xnlof[pr][:, sl * 128:(sl + 1) * 128],
                idf_sb[:, :]),
               [f"xnlof{pr}", "idf"], [BT[bank]], tag=f"trl{i}_{sl}")
            st("vector", lambda e, i=i, sl=sl, bank=bank: e.tensor_copy(
                xnT_lo[:, sl * TW + i * 128: sl * TW + (i + 1) * 128],
                PB[bank][:, 128:256]),
               [BT[bank], f"xnT_lo_{sl}"], [f"xnT_lo_{sl}"], tag=f"trlev{i}_{sl}")

    dma("sync", xnT_scr, xnT_hi[:, :],
        [f"xnT_hi_{sl}" for sl in range(8)], ["xnT_scr"], tag="xnTscr")

    for i in range(1, NT):
        def head_mm(e, i=i):
            last = None
            for dm8 in range(8):
                lh = xnT_hi[:, dm8 * TW + i * 128: dm8 * TW + (i + 1) * 128]
                last = e.matmul(PB[6][:, 0:18], lh, smallhl[:, dm8 * 18:(dm8 + 1) * 18],
                                start=(dm8 == 0), stop=False)
            for dm8 in range(8):
                ll = xnT_lo[:, dm8 * TW + i * 128: dm8 * TW + (i + 1) * 128]
                last = e.matmul(PB[6][:, 0:9], ll, small_hi[:, dm8 * 9:(dm8 + 1) * 9],
                                start=False, stop=(dm8 == 7))
            return last
        st("tensor", head_mm,
           [f"xnT_hi_{d}" for d in range(8)] + [f"xnT_lo_{d}" for d in range(8)]
           + ["smallhl", "small_hi"], ["B6"], tag=f"headmm{i}")
        st("vector", lambda e: e.tensor_copy(lnjunk[:, 0:18], PB[6][:, 0:18]),
           ["B6", "lnjunk"], ["lnjunk"], tag=f"headcp{i}")
        st("vector", lambda e: e.tensor_add(headf[:, 0:9], lnjunk[:, 0:9],
                                            lnjunk[:, 9:18]),
           ["lnjunk", "headf"], ["headf"], tag=f"headadd{i}")
        st("scalar", lambda e: e.activation(headf[:, 9:10], headf[:, 8:9], AF.Abs),
           ["headf"], ["headf"])
        st("scalar", lambda e: e.activation(headf[:, 10:11], headf[:, 9:10], AF.Exp,
                                            scale=-1.0),
           ["headf"], ["headf"])
        st("scalar", lambda e: e.activation(headf[:, 11:12], headf[:, 10:11], AF.Ln,
                                            bias=cbias[:, 1:2]),
           ["headf", "cbias"], ["headf"])
        st("vector", lambda e: e.tensor_scalar_max(headf[:, 12:13], headf[:, 8:9], 0.0),
           ["headf"], ["headf"])
        st("vector", lambda e, i=i: e.tensor_add(
            dts_tok[:, i:i + 1], headf[:, 11:12], headf[:, 12:13]),
           ["headf"], ["dts_tok"], tag=f"dts{i}")
        csl = combine[:, i * 8:(i + 1) * 8]
        st("scalar", lambda e, csl=csl: e.activation(csl, headf[:, 0:8], AF.Sigmoid),
           ["headf"], ["probs"], tag=f"sig{i}")
        st("vector", lambda e, csl=csl: e.reduce_max(headf[:, 13:14], csl, AX.X),
           ["probs", "headf"], ["headf"])
        st("vector", lambda e, csl=csl: e.tensor_scalar(
            lnjunk[:, 0:8], csl, headf[:, 13:14], None, ALU.is_equal),
           ["probs", "headf", "lnjunk"], ["lnjunk"])
        st("vector", lambda e, csl=csl: e.scalar_tensor_tensor(
            lnjunk[:, 8:16], lnjunk[:, 0:8], -1e30, csl, ALU.mult, ALU.add),
           ["lnjunk", "probs"], ["lnjunk"])
        st("vector", lambda e: e.reduce_max(headf[:, 14:15], lnjunk[:, 8:16], AX.X),
           ["lnjunk", "headf"], ["headf"])
        st("vector", lambda e: e.scalar_tensor_tensor(
            headf[:, 15:16], headf[:, 13:14], 1.0, headf[:, 14:15], ALU.mult, ALU.add),
           ["headf"], ["headf"])
        st("vector", lambda e: e.tensor_scalar_add(headf[:, 15:16], headf[:, 15:16], 1e-20),
           ["headf"], ["headf"])
        st("vector", lambda e: e.reciprocal(headf[:, 13:14], headf[:, 15:16]),
           ["headf"], ["headf"])
        st("vector", lambda e, csl=csl: e.tensor_scalar(
            lnjunk[:, 0:8], csl, headf[:, 14:15], None, ALU.is_ge),
           ["probs", "headf", "lnjunk"], ["lnjunk"])
        st("vector", lambda e, csl=csl: e.tensor_mul(csl, csl, lnjunk[:, 0:8]),
           ["probs", "lnjunk"], ["probs"])
        st("vector", lambda e, csl=csl: e.tensor_scalar_mul(csl, csl, headf[:, 13:14]),
           ["probs", "headf"], ["probs"], tag=f"comb{i}")

    p.barrier()
    if DEBUG_PHASE == "p1":
        p.final_wait()
        return nc, p

    def chunk_rows(c):
        par = c % 2
        mb = 6 + par
        mt = BT[mb]
        rows = rows_sb[par]
        rt = f"rows{par}"
        st("tensor", lambda e, c=c, mb=mb: e.matmul(
            PB[mb][0:1, 0:512], dts_tok[:, c + 1:c + 2], Mrows_sb[:, :],
            start=True, stop=True),
           ["dts_tok", "Mrows"], [mt], tag=f"rows{c}")
        st("vector", lambda e, rows=rows, mb=mb: e.tensor_copy(
            rows[:, :], PB[mb][0:1, 0:512]),
           [mt], [rt], tag=f"rowsev{c}")
        st("vector", lambda e, rows=rows, par=par: e.tensor_scalar(
            rQrow[par][:, :], rows[:, 384:512], -1.0, rows[0:1, 511:512],
            ALU.mult, ALU.add),
           [rt], [f"rQ{par}"], tag=f"rq{c}")
        st("vector", lambda e, rows=rows, par=par: e.tensor_copy(
            l8row[par][:, :], rows[0:1, 135:256:8]),
           [rt], [f"l8{par}"], tag=f"l8{c}")
        st("tensor", lambda e, rows=rows, mb=mb: e.matmul(
            PB[mb][:, 0:128], ones1[:, :], rows[0:1, 0:128], start=True, stop=True),
           [rt, "ones1"], [mt], tag=f"dtsbmm{c}")
        st("vector", lambda e, par=par, mb=mb: e.tensor_copy(
            MS[par]["dtsb"][:, :], PB[mb][:, 0:128]),
           [mt], [f"dtsb{par}"], tag=f"dtsbev{c}")
        st("tensor", lambda e, par=par, mb=mb: e.matmul(
            PB[mb][:, 128:144], A_row[:, :], l8row[par][:, :], start=True, stop=True),
           [f"l8{par}", "A_row"], [mt], tag=f"d8mm{c}")
        st("scalar", lambda e, par=par, mb=mb: e.activation(
            d8[par][:, :], PB[mb][:, 128:144], AF.Exp),
           [mt], [f"d8{par}"], tag=f"d8e{c}")
        st("tensor", lambda e, rows=rows, mb=mb: e.matmul(
            PB[mb][:, 144:145], A_row[:, :], rows[0:1, 511:512], start=True, stop=True),
           [rt, "A_row"], [mt], tag=f"dcmm{c}")
        st("scalar", lambda e, c=c, mb=mb: e.activation(
            dccol_all[:, c:c + 1], PB[mb][:, 144:145], AF.Exp),
           [mt], [f"dcc{c}"], tag=f"dce{c}")

    def chunk_exp(c, rowap, rowtok, outsb, outtok, slot, tag):
        ab = 4 + c % 2
        abt = BT[ab]
        off = 0 if slot == 0 else 128
        st("tensor", lambda e, rowap=rowap, ab=ab, off=off: e.matmul(
            PB[ab][:, off:off + 128], A_row[:, :], rowap, start=True, stop=True),
           [rowtok, "A_row"], [abt], tag=f"{tag}mm{c}")
        st("scalar", lambda e, outsb=outsb, ab=ab, off=off: e.activation(
            outsb[:, :], PB[ab][:, off:off + 128], AF.Exp),
           [abt], [outtok], tag=f"{tag}e{c}")

    def chunk_btld(c):
        par = c % 2
        cc0 = 128 + c * 128
        st("vector", lambda e, cc0=cc0, par=par: e.tensor_mul(
            MS[par]["Btld"][:, :], B_fm[:, cc0:cc0 + 128], MS[par]["dtsb"][:, :]),
           ["B_fm", f"dtsb{par}"], [f"Btld{par}"], tag=f"btld{c}")

    def chunk_bend(c):
        par = c % 2
        mb = 6 + par
        mt = BT[mb]
        m = MS[par]
        chunk_exp(c, rQrow[par][:, :], f"rQ{par}", m["ebufA"], f"ebufA{par}", 0, "erq")
        st("vector", lambda e, m=m, par=par: e.tensor_mul(
            m["Bend"][:, :], m["Btld"][:, :], m["ebufA"][:, :]),
           [f"Btld{par}", f"ebufA{par}"], [f"Bend{par}"], tag=f"bend{c}")
        st("tensor", lambda e, m=m, mb=mb: e.transpose(
            PB[mb][:, 384:512], m["Bend"][:, :], idf_sb[:, :]),
           [f"Bend{par}", "idf"], [mt], tag=f"bendtr{c}")
        st("vector", lambda e, c=c, mb=mb: e.tensor_copy(
            Bendtok_all[:, c * 128:(c + 1) * 128], PB[mb][:, 384:512]),
           [mt], [f"Bet{c}", "cvacc1"], tag=f"bendev{c}")

    def chunk_xmtok(c):
        par = c % 2
        xf = y_fm[par]
        xftok = f"y_fm{par}"
        st("vector", lambda e, par=par, xf=xf: e.tensor_copy(xf[:, :], xmc[par][:, :]),
           [f"xmc{par}", xftok], [xftok], tag=f"xmf32_{c}")
        for sl in range(16):
            bk = sl % 4
            off = 128 * par
            st("tensor", lambda e, sl=sl, bk=bk, off=off, xf=xf: e.transpose(
                PB[bk][:, off:off + 128], xf[:, sl * 128:(sl + 1) * 128], idf_sb[:, :]),
               [xftok, "idf"], [BT[bk]], tag=f"xtr{c}_{sl}")
            st("vector", lambda e, sl=sl, bk=bk, off=off, par=par: e.tensor_copy(
                xm_tok[par][:, sl * 128:(sl + 1) * 128], PB[bk][:, off:off + 128]),
               [BT[bk], f"xm_tok{par}"], [f"xm_tok{par}"], tag=f"xtrev{c}_{sl}")

    def chunk_state(c):
        par = c % 2
        def smm(e, c=c, par=par):
            last = None
            for q4 in range(4):
                last = e.matmul(PB[q4][:, :], Bendtok_all[:, c * 128:(c + 1) * 128],
                                xm_tok[par][:, q4 * 512:(q4 + 1) * 512],
                                start=True, stop=True)
            return last
        st("tensor", smm, [f"Bet{c}", f"xm_tok{par}"], BT[0:4], tag=f"smm{c}")
        for q4 in range(4):
            st("vector", lambda e, q4=q4, c=c: e.scalar_tensor_tensor(
                S_run[:, q4 * 512:(q4 + 1) * 512], S_run[:, q4 * 512:(q4 + 1) * 512],
                dccol_all[:, c:c + 1], PB[q4][:, :], ALU.mult, ALU.add),
               ["S_run", f"dcc{c}", BT[q4]], ["S_run"], tag=f"sup{c}_{q4}")

    def prep_exps(c):
        par = c % 2
        m = MS[par]
        cc0 = 128 + c * 128
        chunk_exp(c, rows_sb[par][0:1, 0:128], f"rows{par}", m["A_bar"],
                  f"A_bar{par}", 0, "abar")
        chunk_exp(c, rows_sb[par][0:1, 128:256], f"rows{par}", m["ebufB"],
                  f"ebufB{par}", 1, "elc")
        st("vector", lambda e, cc0=cc0, m=m, par=par: e.tensor_mul(
            m["F1"][:, :], C_fm[:, cc0:cc0 + 128], m["ebufB"][:, :]),
           ["C_fm", f"ebufB{par}", f"F1{par}"], [f"F1{par}"], tag=f"f1_{c}")
        chunk_exp(c, rows_sb[par][0:1, 256:384], f"rows{par}", m["ebufA"],
                  f"ebufA{par}", 0, "er8")
        st("vector", lambda e, m=m, par=par: e.tensor_mul(
            m["F2base"][:, :], m["Btld"][:, :], m["ebufA"][:, :]),
           [f"Btld{par}", f"ebufA{par}", f"F2base{par}"], [f"F2base{par}"],
           tag=f"f2b{c}")
        chunk_exp(c, rows_sb[par][0:1, 384:512], f"rows{par}", m["ebufB"],
                  f"ebufB{par}", 1, "eg")
        st("vector", lambda e, cc0=cc0, m=m, par=par, c=c: e.tensor_mul(
            Cs_all[:, c * 128:(c + 1) * 128], C_fm[:, cc0:cc0 + 128],
            m["ebufB"][:, :]),
           ["C_fm", f"ebufB{par}", f"Cs{c}"], [f"Cs{c}", "cvt21"],
           tag=f"csx{c}")

    def prep_w(c):
        par = c % 2
        ab = 4 + par
        abt = BT[ab]
        m = MS[par]
        bufs = [m["F2run"], F2runB[par]]
        btoks = [f"F2run{par}", f"F2runB{par}"]
        for I in range(1, 16):
            src, dst = bufs[(I - 1) % 2], bufs[I % 2]
            stok, dtok = btoks[(I - 1) % 2], btoks[I % 2]
            st("vector", lambda e, src=src, dst=dst, I=I, par=par:
               e.tensor_scalar_mul(dst[:, :], src[:, :], d8[par][:, I - 1:I]),
               [stok, f"d8{par}", dtok], [dtok], tag=f"f2r{c}_{I}")
            st("vector", lambda e, dst=dst, I=I, m=m: e.tensor_copy(
                dst[:, (I - 1) * 8:I * 8], m["F2base"][:, (I - 1) * 8:I * 8]),
               [f"F2base{par}", dtok], [dtok], tag=f"f2c{c}_{I}")
            st("tensor", lambda e, I=I, dst=dst, m=m, ab=ab: e.matmul(
                PB[ab][:, 256 + I * 8:256 + (I + 1) * 8], dst[:, :],
                m["F1"][:, I * 8:(I + 1) * 8], start=True, stop=True),
               [dtok, f"F1{par}"], [abt], tag=f"wmm{c}_{I}")
        st("vector", lambda e, c=c, ab=ab: e.tensor_mul(
            Wsb_all[:, c * 128:(c + 1) * 128], PB[ab][:, 256:384],
            maskoff_sb[:, :]),
           [abt, "maskoff", f"Wsb{c}"], [f"Wsb{c}", "cvt20"], tag=f"wev{c}")

    def prep_band(c):
        par = c % 2
        ab = 4 + par
        abt = BT[ab]
        m = MS[par]
        cc0 = 128 + c * 128
        st("vector", lambda e, cc0=cc0, m=m, par=par: e.tensor_mul(
            Zbuf[par][:, 0:128], C_fm[:, cc0:cc0 + 128], m["Btld"][:, :]),
           ["C_fm", f"Btld{par}", f"Zbuf{par}"], [f"Zbuf{par}"], tag=f"z0_{c}")
        for k9 in range(1, BLK):
            src = m["Ppp"][(k9 - 1) % 2]
            dst = m["Ppp"][k9 % 2]
            stok, dtok = f"Ppp{par}_{(k9 - 1) % 2}", f"Ppp{par}_{k9 % 2}"
            if k9 == 1:
                st("vector", lambda e, dst=dst, m=m: e.tensor_copy(
                    dst[:, :], m["A_bar"][:, :]),
                   [f"A_bar{par}", dtok], [dtok], tag=f"p1_{c}")
            else:
                st("vector", lambda e, src=src, dst=dst, m=m: e.tensor_mul(
                    dst[:, 1:128], m["A_bar"][:, 1:128], src[:, 0:127]),
                   [f"A_bar{par}", stok, dtok], [dtok], tag=f"p{k9}_{c}")
            st("vector", lambda e, dst=dst, cc0=cc0, m=m: e.tensor_mul(
                m["zcw"][:, :], C_fm[:, cc0:cc0 + 128], dst[:, :]),
               ["C_fm", dtok, f"zcw{par}"], [f"zcw{par}"], tag=f"zcw{k9}_{c}")
            st("vector", lambda e, k9=k9, m=m, par=par: e.tensor_mul(
                Zbuf[par][:, k9 * 128 + k9:(k9 + 1) * 128],
                m["zcw"][:, k9:128], m["Btld"][:, 0:128 - k9]),
               [f"zcw{par}", f"Btld{par}", f"Zbuf{par}"], [f"Zbuf{par}"],
               tag=f"z{k9}_{c}")
        for h2 in range(2):
            st("tensor", lambda e, h2=h2, par=par, ab=ab: e.matmul(
                PB[ab][0:1, 0:512], onesc[:, :],
                Zbuf[par][:, h2 * 512:(h2 + 1) * 512], start=True, stop=True),
               [f"Zbuf{par}", "onesc"], [abt], tag=f"ckmm{c}_{h2}")
            st("vector", lambda e, h2=h2, par=par, ab=ab: e.tensor_copy(
                ck_sb[par][:, h2 * 512:(h2 + 1) * 512], PB[ab][0:1, 0:512]),
               [abt, f"ck{par}"], [f"ck{par}"], tag=f"ckev{c}_{h2}")
        for h2 in range(2):
            st("tensor", lambda e, h2=h2, par=par, ab=ab: e.matmul(
                PB[ab][:, 0:512], ones1[:, :],
                ck_sb[par][0:1, h2 * 512:(h2 + 1) * 512], start=True, stop=True),
               [f"ck{par}", "ones1"], [abt], tag=f"ckbmm{c}_{h2}")
            st("vector", lambda e, h2=h2, par=par, ab=ab: e.tensor_copy(
                ckb[par][:, h2 * 512:(h2 + 1) * 512], PB[ab][:, 0:512]),
               [abt, f"ckb{par}"], [f"ckb{par}", "cvacc1"],
               tag=f"ckbev{c}_{h2}")
        for k9 in range(BLK):
            st("vector", lambda e, k9=k9, par=par, c=c: e.copy_predicated(
                Wsb_all[:, c * 128:(c + 1) * 128],
                bandmask_sb[:, k9 * 128:(k9 + 1) * 128],
                ckb[par][:, k9 * 128:(k9 + 1) * 128]),
               ["bandmask", f"ckb{par}", f"Wsb{c}"], [f"Wsb{c}"],
               tag=f"band{c}_{k9}")

    st("vector", lambda e: e.memset(S_run[:, :], 0.0), [], ["S_run"])
    for par in range(2):
        st("vector", lambda e, par=par: e.memset(MS[par]["F2run"][:, :], 0.0),
           [f"F2run{par}"], [f"F2run{par}"], tag=f"msf2_{par}")
        st("vector", lambda e, par=par: e.memset(F2runB[par][:, :], 0.0),
           [f"F2runB{par}"], [f"F2runB{par}"], tag=f"msf2b_{par}")
        st("vector", lambda e, par=par: e.memset(MS[par]["Ppp"][0][:, :], 1.0),
           [f"Ppp{par}_0"], [f"Ppp{par}_0"], tag=f"msp0_{par}")
        st("vector", lambda e, par=par: e.memset(MS[par]["Ppp"][1][:, :], 1.0),
           [f"Ppp{par}_1"], [f"Ppp{par}_1"], tag=f"msp1_{par}")

    def evac_z(jt, tb_i, t0, tn, bank):
        pz = tb_i % 2
        zb = zbuf[pz]
        ztok = f"zbuf{pz}"
        ct = cvt0[pz]
        ctok = f"cvt0{pz}"
        st("scalar", lambda e, tn=tn, bank=bank, ct=ct: e.activation(
            ct[:, 0:tn], PB[bank][:, 0:tn], AF.Silu),
           [BT[bank], ctok], [ctok], tag=f"zsilu{jt}_{tb_i}")
        st("vector", lambda e, tn=tn, zb=zb, ct=ct: e.tensor_scalar_min(
            zb[:, 0:tn], ct[:, 0:tn], CLIP),
           [ctok], [ztok])
        dma("sync", z_scr[jt * 128:(jt + 1) * 128, t0:t0 + tn], zb[:, 0:tn],
            [ztok], [f"zscr{jt}_{tb_i}"], tag=f"zst{jt}_{tb_i}")

    def evac_xm(jt, tb_i, t0, tn, bank):
        sl = jt - 16
        pb = prebuf[sl % 2]
        ptok = f"prebuf{sl % 2}"
        ct = cvt1[tb_i % 2]
        ctok = f"cvt1{tb_i % 2}"
        st("scalar", lambda e, tn=tn, bank=bank, ct=ct: e.activation(
            ct[:, 0:tn], PB[bank][:, 0:tn], AF.Silu),
           [BT[bank], ctok], [ctok], tag=f"xmsilu{jt}_{tb_i}")
        st("vector", lambda e, t0=t0, tn=tn, pb=pb, ct=ct: e.tensor_scalar_min(
            pb[:, t0:t0 + tn], ct[:, 0:tn], CLIP),
           [ctok, ptok], [ptok])

    def evac_bc(jt, tb_i, t0, tn, bank):
        dst = B_fm if jt == 32 else C_fm
        tok = "B_fm" if jt == 32 else "C_fm"
        st("vector", lambda e, dst=dst, t0=t0, tn=tn, bank=bank: e.tensor_copy(
            dst[:, t0:t0 + tn], PB[bank][:, 0:tn]),
           [BT[bank]], [tok], tag=f"bc{jt}_{tb_i}")

    def conv_slot(sl):
        pb = prebuf[sl % 2]
        ptok = f"prebuf{sl % 2}"
        acc = cvacc[sl % 2]
        atok = f"cvacc{sl % 2}"
        xo = xmout[sl % 2]
        xotok = f"xmout{sl % 2}"
        st("vector", lambda e, sl=sl, pb=pb, acc=acc: e.tensor_scalar(
            acc[:, :], pb[:, 125:125 + T], cw_sb[:, sl * 4:sl * 4 + 1], None, ALU.mult),
           [ptok, "cw", atok], [atok], tag=f"cv0_{sl}")
        for k4 in range(1, 4):
            st("vector", lambda e, sl=sl, k4=k4, pb=pb, acc=acc: e.scalar_tensor_tensor(
                acc[:, :], pb[:, 125 + k4:125 + k4 + T],
                cw_sb[:, sl * 4 + k4:sl * 4 + k4 + 1], acc[:, :], ALU.mult, ALU.add),
               [ptok, "cw", atok], [atok], tag=f"cv{k4}_{sl}")
        for hh in range(2):
            cv = cvt2[hh]
            cvtok = f"cvt2{hh}"
            st("scalar", lambda e, acc=acc, hh=hh, cv=cv: e.activation(
                cv[:, :], acc[:, hh * 1024:(hh + 1) * 1024], AF.Silu),
               [atok, cvtok], [cvtok], tag=f"cs{hh}_{sl}")
        xs = xmst[sl % 2]
        xstok = f"xmst{sl % 2}"
        for cc in range(NCH):
            bk = 4 + cc % 4
            cv = cvt2[cc // 8]
            cvtok = f"cvt2{cc // 8}"
            st("tensor", lambda e, cc=cc, bk=bk, cv=cv: e.transpose(
                PB[bk][:, 0:128], cv[:, (cc % 8) * 128:(cc % 8 + 1) * 128],
                idf_sb[:, :]),
               [cvtok, "idf"], [BT[bk]], tag=f"cvtr{sl}_{cc}")
            st("vector", lambda e, cc=cc, bk=bk, xs=xs: e.tensor_scalar_min(
                xs[:, cc * 128:(cc + 1) * 128], PB[bk][:, 0:128], CLIP),
               [BT[bk], xstok], [xstok], tag=f"cvtrev{sl}_{cc}")
        dma("sync",
            xmT_scr[:, sl * 128:(sl + 1) * 128].rearrange("(c t) f -> t c f",
                                                          t=128),
            xs[:, :].rearrange("p (c f) -> p c f", c=16),
            [xstok], [f"xmT_{sl}"], tag=f"stxmT{sl}")

    ipq = 0
    for jti, jt in enumerate(list(range(16, 34)) + list(range(16))):
        wb = wbuf[jti % 2]
        wtok = f"wbuf{jti % 2}"
        dma("gpsimd", wb[:, :].rearrange("p (s j) -> p s j", s=8),
            WinT.rearrange("(s p) j -> p s j", p=128)[:, :, jt * 128:(jt + 1) * 128],
            [], [wtok], tag=f"ldw{jt}")
        for tb_i, (t0, tn) in enumerate(TBATCH):
            bank = ipq % 4
            ipq += 1
            def mmj(e, wb=wb, t0=t0, tn=tn, bank=bank):
                last = None
                for dm8 in range(8):
                    last = e.matmul(
                        PB[bank][:, 0:tn],
                        wb[:, dm8 * 128:(dm8 + 1) * 128],
                        xnT_hi[:, dm8 * TW + t0: dm8 * TW + t0 + tn],
                        start=(dm8 == 0), stop=(dm8 == 7))
                return last
            st("tensor", mmj, [wtok], [BT[bank]], tag=f"ip{jt}_{tb_i}")
            if jt < 16:
                evac_z(jt, tb_i, t0, tn, bank)
            elif jt < 32:
                evac_xm(jt, tb_i, t0, tn, bank)
            else:
                evac_bc(jt, tb_i, t0, tn, bank)
        if 16 <= jt < 32:
            conv_slot(jt - 16)
        c = jti - 18
        if 0 <= c < NCH:
            par = c % 2
            chunk_rows(c)
            chunk_btld(c)
            prep_exps(c)
            prep_w(c)
            prep_band(c)
            chunk_bend(c)
            dma("gpsimd", xm_tok[par][:, :], xmT_scr[c * 128:(c + 1) * 128, :],
                [f"xm_tok{par}"] + [f"xmT_{sl}" for sl in range(16)],
                [f"xm_tok{par}"], tag=f"ldxmT3a_{c}")
            chunk_state(c)

    dma("sync", ag_in.ap(), S_run[:, :], ["S_run"], ["ag_in"], tag="agst")
    p.step("gpsimd", lambda e: e.collective_compute(
        "AllGather", ALU.bypass, replica_groups=[list(range(8))],
        ins=[ag_in.ap().opt()], outs=[ag_out.ap().opt()]),
        reads=["ag_in"], writes=["ag_out"], done_sem=("cc", 1), tag="allgather")

    p.barrier()
    if DEBUG_PHASE == "p3a":
        p.final_wait()
        return nc, p

    for i in range(16):
        st("scalar", lambda e, i=i: e.activation(
            sqc_t[:, :], combine[:, (i + 1) * 8:(i + 2) * 8], AF.Sqrt),
           ["combine", "sqc"], ["sqc"], tag=f"sqrt{i}")
        st("tensor", lambda e: e.transpose(PB[6][0:8, 0:128], sqc_t[:, :], idf_sb[:, :]),
           ["sqc", "idf"], ["B6"], tag=f"sqtr{i}")
        st("vector", lambda e, i=i: e.tensor_copy(
            sq_fm[0:8, i * 128:(i + 1) * 128], PB[6][0:8, 0:128]),
           ["B6"], ["sq_fm"], tag=f"sqev{i}")

    wti = 0
    for e8 in range(E):
        we = WIDTHS[e8]
        def sqbmm(e, e8=e8):
            last = None
            for q4 in range(4):
                last = e.matmul(PB[q4][:, :], eyerow8_sb[:, e8 * 128:(e8 + 1) * 128],
                                sq_fm[0:8, q4 * 512:(q4 + 1) * 512],
                                start=True, stop=True)
            return last
        st("tensor", sqbmm, ["sq_fm", "eyerow8"], BT[0:4], tag=f"sqb{e8}")
        for q4 in range(4):
            st("vector", lambda e, q4=q4: e.tensor_copy(
                sqb_sb[:, q4 * 512:(q4 + 1) * 512], PB[q4][:, :]),
               [BT[q4], "sqb"], ["sqb"], tag=f"sqbev{e8}_{q4}")
        for dm8 in range(8):
            xs = xnTs[dm8 % 2]
            xst = f"xnTs{dm8 % 2}"
            dma("sync", xs[:, :], xnT_scr[:, dm8 * TW + 128: dm8 * TW + TW],
                [xst], [xst], tag=f"ldxnT{e8}_{dm8}")
            st("vector", lambda e, dm8=dm8, xs=xs: e.tensor_mul(
                xnT_e[:, dm8 * 2048:(dm8 + 1) * 2048], xs[:, :], sqb_sb[:, :]),
               [xst, "sqb", f"xnT_e{dm8}"], [f"xnT_e{dm8}"], tag=f"xe{e8}_{dm8}")
        for ch in range(we // 512):
            cw0 = int(OFFS[e8]) + ch * 512
            wb1 = w1buf[ch % 2]
            wt1 = f"w1buf{ch % 2}"
            dma("gpsimd", wb1[:, :].rearrange("p (s w) -> p s w", s=8),
                w1.rearrange("(s p) w -> p s w", p=128)[:, :, cw0:cw0 + 512],
                [wt1], [wt1], tag=f"ldw1_{e8}_{ch}")
            for wt4 in range(4):
                pbase = (wti % 2) * 4
                def mm1(e, wb1=wb1, wt4=wt4, pbase=pbase):
                    last = None
                    for q4 in range(4):
                        for dm8 in range(8):
                            last = e.matmul(
                                PB[pbase + q4][:, :],
                                wb1[:, dm8 * 512 + wt4 * 128: dm8 * 512 + (wt4 + 1) * 128],
                                xnT_e[:, dm8 * 2048 + q4 * 512: dm8 * 2048 + (q4 + 1) * 512],
                                start=(dm8 == 0), stop=(dm8 == 7))
                    return last
                st("tensor", mm1, [wt1] + [f"xnT_e{d}" for d in range(8)],
                   [BT[pbase + q] for q in range(4)], tag=f"mm1_{wti}")
                hb = Hsqb[wti % 2]
                htok = f"Hsqb{wti % 2}"
                for q4 in range(4):
                    st("scalar", lambda e, pbase=pbase, q4=q4: e.activation(
                        Hrelu[:, q4 * 512:(q4 + 1) * 512], PB[pbase + q4][:, :], AF.Relu),
                       [BT[pbase + q4], "Hrelu"], ["Hrelu"], tag=f"rel{wti}_{q4}")
                    st("vector", lambda e, q4=q4, hb=hb: e.tensor_mul(
                        hb[:, q4 * 512:(q4 + 1) * 512],
                        Hrelu[:, q4 * 512:(q4 + 1) * 512],
                        Hrelu[:, q4 * 512:(q4 + 1) * 512]),
                       ["Hrelu", htok], [htok], tag=f"hsq{wti}_{q4}")
                dma("sync", H_scr[wti * 128:(wti + 1) * 128, :], hb[:, :],
                    [htok], [f"Hrow{wti}"], tag=f"sth{wti}")
                wti += 1
    assert wti == 96

    p.barrier()

    bankc = 0
    for wb8 in range(12):
        wb2 = w2buf[wb8 % 2]
        wt2 = f"w2buf{wb8 % 2}"
        dma("gpsimd", wb2[:, :].rearrange("p (s d) -> p s d", s=8),
            w2[wb8 * 1024:(wb8 + 1) * 1024, :].rearrange("(s p) d -> p s d", p=128),
            [wt2], [wt2], tag=f"ldw2_{wb8}")
        for th in range(2):
            hb2 = hbuf4[(wb8 * 2 + th) % 2]
            ht2 = f"hbuf4_{(wb8 * 2 + th) % 2}"
            dma("sync", hb2[:, :].rearrange("p (s t) -> p s t", s=8),
                H_scr[wb8 * 1024:(wb8 + 1) * 1024,
                      th * 1024:(th + 1) * 1024].rearrange("(s p) t -> p s t", p=128),
                [ht2], [ht2], tag=f"ldh{wb8}_{th}")
            for tt in range(8):
                for hf in range(2):
                    bank = bankc % 8
                    bankc += 1
                    def mm2(e, hb2=hb2, wb2=wb2, tt=tt, hf=hf, bank=bank):
                        last = None
                        for sl8 in range(8):
                            last = e.matmul(
                                PB[bank][:, :],
                                hb2[:, sl8 * 1024 + tt * 128:
                                    sl8 * 1024 + (tt + 1) * 128],
                                wb2[:, sl8 * 1024 + hf * 512:
                                    sl8 * 1024 + (hf + 1) * 512],
                                start=(sl8 == 0), stop=(sl8 == 7))
                        return last
                    st("tensor", mm2, [ht2, wt2], [BT[bank]],
                       tag=f"mm2_{wb8}_{th}_{tt}_{hf}")
                    t16 = th * 8 + tt
                    ab = accs[t16]
                    atok = f"acc{t16}"
                    a0 = hf * 512
                    if wb8 == 0:
                        st("vector", lambda e, bank=bank, a0=a0, ab=ab:
                           e.tensor_copy(ab[:, a0:a0 + 512], PB[bank][:, :]),
                           [BT[bank]], [atok], tag=f"acccp{th}_{tt}_{hf}")
                    else:
                        st("vector", lambda e, bank=bank, a0=a0, ab=ab:
                           e.tensor_add(ab[:, a0:a0 + 512], ab[:, a0:a0 + 512],
                                        PB[bank][:, :]),
                           [BT[bank], atok], [atok],
                           tag=f"accad{wb8}_{th}_{tt}_{hf}")
    for t16 in range(16):
        gb = gout[t16 % 2]
        gtok = f"gout{t16 % 2}"
        st("scalar", lambda e, t16=t16, gb=gb: e.activation(
            gb[:, :], accs[t16][:, :], AF.Silu),
           [f"acc{t16}", gtok], [gtok], tag=f"gsilu{t16}")
        dma("sync", g_scr[t16 * 128:(t16 + 1) * 128, :], gb[:, :],
            [gtok], [f"gscr{t16}"], tag=f"stg{t16}")

    p.barrier()
    if DEBUG_PHASE == "p2":
        p.final_wait()
        return nc, p

    st("vector", lambda e: e.memset(S_run[:, :], 0.0), [], ["S_run"])
    cc_wait = [("cc", 1)]
    hb_bufs = [cvacc[0], xmdp]
    for r8 in range(8):
        bb = hb_bufs[r8 % 2]
        btok = f"hbb{r8 % 2}"
        p.step("sync" if r8 % 2 == 0 else "gpsimd",
               lambda e, r8=r8, bb=bb: e.dma_start(
                   out=bb[:, :], in_=ag_out.ap()[r8 * 128:(r8 + 1) * 128, :]),
               reads=[btok], writes=[btok], dma=True, tag=f"ldag{r8}",
               extra_waits=cc_wait if r8 < 2 else ())
        st("vector", lambda e, r8=r8, bb=bb: e.scalar_tensor_tensor(
            S_run[:, :], bb[:, :], sel8_sb[:, r8:r8 + 1], S_run[:, :],
            ALU.mult, ALU.add),
           [btok, "sel8", "S_run"], ["S_run"], tag=f"hinit{r8}")

    dma("gpsimd", WoutT_sb[:, :].rearrange("p (s j) -> p s j", s=16),
        WoutT.rearrange("(s p) j -> p s j", p=128), [], ["WoutT_raw"], tag="ldwo")
    for sl in range(16):
        st("vector", lambda e, sl=sl: e.tensor_scalar_mul(
            WoutT_sb[:, sl * 1024:(sl + 1) * 1024],
            WoutT_sb[:, sl * 1024:(sl + 1) * 1024], nw_sb[:, sl:sl + 1]),
           ["WoutT_raw", "nw", "WoutT"], ["WoutT"], tag=f"wosc{sl}")
    dma("scalar", Dp_bcast[:, :], dp_b, [], ["Dp_bcast"], tag="lddpb")

    def main_dmas(c):
        par = c % 2
        rc0 = c * 128
        cc0 = 128 + c * 128
        dma("sync", zc[par][:, :].rearrange("p (s t) -> p s t", s=16),
            z_scr[:, cc0:cc0 + 128].rearrange("(s p) t -> p s t", p=128),
            [f"zc{par}"], [f"zc{par}"], tag=f"ldz{c}")
        dma("sync", gbuf[par][:, :], g_scr[rc0:rc0 + 128, :], [f"gbuf{par}"],
            [f"gbuf{par}"], tag=f"ldg{c}")
        dma("sync", xres[par][:, :], xin[cc0:cc0 + 128, :], [f"xres{par}"],
            [f"xres{par}"], tag=f"ldxr{c}")
        dma("scalar", xm_tok[par][:, :], xmT_scr[rc0:rc0 + 128, :],
            [f"xm_tok{par}"], [f"xm_tok{par}"], tag=f"ldxmT{c}")

    def main_state(c):
        par = c % 2
        sbf = S_run_bf[par]
        sbtok = f"S_run_bf{par}"
        st("vector", lambda e, sbf=sbf: e.tensor_copy(sbf[:, :], S_run[:, :]),
           ["S_run", sbtok], [sbtok], tag=f"srbf{c}")
        chunk_state(c)

    def main_y(c):
        par = c % 2
        sbf = S_run_bf[par]
        sbtok = f"S_run_bf{par}"
        def ymm(e, c=c, par=par, sbf=sbf):
            last = None
            for q4 in range(4):
                e.matmul(PB[q4][:, :], Cs_all[:, c * 128:(c + 1) * 128],
                         sbf[:, q4 * 512:(q4 + 1) * 512], start=True, stop=False)
                last = e.matmul(PB[q4][:, :], Wsb_all[:, c * 128:(c + 1) * 128],
                                xm_tok[par][:, q4 * 512:(q4 + 1) * 512],
                                start=False, stop=True)
            return last
        st("tensor", ymm, [sbtok, f"Cs{c}", f"xm_tok{par}", f"Wsb{c}"],
           BT[0:4], tag=f"ymm{c}")
        st("vector", lambda e, par=par: e.tensor_mul(
            xmdp[:, :], xm_tok[par][:, :], Dp_bcast[:, :]),
           [f"xm_tok{par}", "Dp_bcast", "xmdp"], ["xmdp"], tag=f"xmdp{c}")
        yt = y_fm[par]
        yttok = f"y_fm{par}"
        for q4 in range(4):
            st("vector", lambda e, q4=q4, yt=yt: e.tensor_add(
                yt[:, q4 * 512:(q4 + 1) * 512], PB[q4][:, :],
                xmdp[:, q4 * 512:(q4 + 1) * 512]),
               [BT[q4], "xmdp", yttok], [yttok], tag=f"yev{c}_{q4}")
        st("scalar", lambda e, yt=yt, par=par: e.activation(
            xmdp[:, :], yt[:, :], AF.Square, accum_out=rstd_col[par][:, :]),
           [yttok, "xmdp", f"rstd_col{par}"], ["xmdp", f"rstd_col{par}"],
           tag=f"ysq{c}")
        st("scalar", lambda e, par=par: e.activation(
            rstd_col[par][:, :], rstd_col[par][:, :], AF.Ln,
            scale=1.0 / DI, bias=cbias[:, 0:1]),
           [f"rstd_col{par}", "cbias"], [f"rstd_col{par}"], tag=f"rln{c}")
        st("scalar", lambda e, par=par: e.activation(
            rstd_col[par][:, :], rstd_col[par][:, :], AF.Exp, scale=-0.5),
           [f"rstd_col{par}"], [f"rstd_col{par}"], tag=f"rexp{c}")

    def main_gate(c):
        par = c % 2
        mb = 6 + par
        yt = y_fm[par]
        yttok = f"y_fm{par}"
        for sl in range(16):
            off = 128 * (sl % 4)
            st("tensor", lambda e, sl=sl, mb=mb, off=off, yt=yt: e.transpose(
                PB[mb][:, off:off + 128], yt[:, sl * 128:(sl + 1) * 128],
                idf_sb[:, :]),
               [yttok, "idf"], [BT[mb]], tag=f"ytr{c}_{sl}")
            st("vector", lambda e, sl=sl, mb=mb, off=off, par=par:
               e.tensor_mul(
                   yg[par][:, sl * 128:(sl + 1) * 128], PB[mb][:, off:off + 128],
                   zc[par][:, sl * 128:(sl + 1) * 128]),
               [BT[mb], f"zc{par}", f"yg{par}"], [f"yg{par}"],
               tag=f"ygev{c}_{sl}")

    def main_out(c):
        par = c % 2
        rc0 = c * 128
        def omm(e, par=par):
            last = None
            for d16 in range(16):
                for hf in range(2):
                    last = e.matmul(
                        PB[4 + hf][:, :], yg[par][:, d16 * 128:(d16 + 1) * 128],
                        WoutT_sb[:, d16 * 1024 + hf * 512:
                                 d16 * 1024 + (hf + 1) * 512],
                        start=(d16 == 0), stop=(d16 == 15))
            return last
        st("tensor", omm, [f"yg{par}", "WoutT"], [BT[4], BT[5]], tag=f"omm{c}")
        ob = outbuf[par]
        otok = f"outbuf{par}"
        for hf in range(2):
            st("vector", lambda e, hf=hf, par=par: e.scalar_tensor_tensor(
                sgbuf[par][:, hf * 512:(hf + 1) * 512], PB[4 + hf][:, :],
                rstd_col[par][:, :],
                gbuf[par][:, hf * 512:(hf + 1) * 512], ALU.mult, ALU.mult),
               [BT[4 + hf], f"rstd_col{par}", f"gbuf{par}", f"yg{par}"],
               [f"yg{par}"], tag=f"sg{c}_{hf}")
        st("vector", lambda e, ob=ob, par=par: e.tensor_add(
            ob[:, :], sgbuf[par][:, :], xres[par][:, :]),
           [f"yg{par}", f"xres{par}", otok], [otok], tag=f"fin{c}")
        dma("sync", out[rc0:rc0 + 128, :], ob[:, :], [otok], [f"out{c}"],
            tag=f"stout{c}")

    for pr in range(NCH // 2):
        cs = (2 * pr, 2 * pr + 1)
        for c in cs:
            main_dmas(c)
        for c in cs:
            main_state(c)
        for c in cs:
            main_y(c)
        for c in cs:
            main_gate(c)
        for c in cs:
            main_out(c)

    p.final_wait()
    return nc, p


_nc_cache = {}


def _get_program():
    if "nc" not in _nc_cache:
        nc, p = build_program()
        with nc.Block() as block, \
             nc.semaphore("s_t") as s_t, nc.semaphore("s_v") as s_v, \
             nc.semaphore("s_s") as s_s, nc.semaphore("s_g") as s_g, \
             nc.semaphore("s_y") as s_y, nc.semaphore("s_cc") as s_cc:
            p.emit(nc, block, {"tensor": s_t, "vector": s_v, "scalar": s_s,
                               "gpsimd": s_g, "sync": s_y, "cc": s_cc})
        _nc_cache["nc"] = nc
    return _nc_cache["nc"]


def _host_inputs(L0_attn_x, L0_attn_in_proj_w, L0_attn_conv_w, L0_attn_A_log,
                 L0_attn_Dp, L0_attn_norm_w, L0_attn_out_proj_w,
                 L0_ffn_router_w, L0_ffn_w1, L0_ffn_w2):
    f32 = np.float32
    x = np.ascontiguousarray(np.asarray(L0_attn_x, dtype=f32))
    Win = np.ascontiguousarray(np.asarray(L0_attn_in_proj_w, dtype=f32))
    WinT = np.ascontiguousarray(Win.T)
    w1 = np.ascontiguousarray(np.asarray(L0_ffn_w1, dtype=f32))
    w2 = np.ascontiguousarray(np.asarray(L0_ffn_w2, dtype=f32))
    WoutT = np.ascontiguousarray(np.asarray(L0_attn_out_proj_w, dtype=f32).T)
    small9 = np.ascontiguousarray(
        np.concatenate([np.asarray(L0_ffn_router_w, f32).T,
                        Win[4352:4353, :].T], axis=1))
    A_log_row = np.ascontiguousarray(np.asarray(L0_attn_A_log, f32).reshape(1, S))
    cw = np.asarray(L0_attn_conv_w, f32).reshape(DI, 4)
    cw_col = np.ascontiguousarray(
        cw.reshape(16, 128, 4).transpose(1, 0, 2).reshape(128, 64))
    dp_col = np.ascontiguousarray(np.asarray(L0_attn_Dp, f32).reshape(16, 128).T)
    dp_b = np.ascontiguousarray(
        np.broadcast_to(np.asarray(L0_attn_Dp, f32).reshape(1, DI), (128, DI)))
    nw_col = np.ascontiguousarray(np.asarray(L0_attn_norm_w, f32).reshape(16, 128).T)

    tt = np.arange(128)
    M8 = ((tt[:, None] // BLK == tt[None, :] // BLK) & (tt[:, None] <= tt[None, :]))
    M8r = ((tt[:, None] // BLK == tt[None, :] // BLK) & (tt[:, None] > tt[None, :]))
    Mlow = tt[:, None] <= tt[None, :]
    Mrows = np.ascontiguousarray(
        np.concatenate([np.eye(128), M8, M8r, Mlow], axis=1).astype(f32))
    maskoff = np.ascontiguousarray(
        (tt[:, None] // BLK < tt[None, :] // BLK).astype(f32))
    bandmask = np.ascontiguousarray(np.concatenate(
        [((tt[None, :] - tt[:, None] == k) & (tt[:, None] // BLK == tt[None, :] // BLK))
         for k in range(BLK)], axis=1).astype(np.uint8))
    idf32 = np.eye(128, dtype=f32)
    eyerow8 = np.zeros((8, 8 * 128), f32)
    for e in range(8):
        eyerow8[e, e * 128:(e + 1) * 128] = 1.0

    shared = dict(WinT=WinT, w1=w1, w2=w2, WoutT=WoutT, small9=small9,
                  A_log_row=A_log_row, cw_col=cw_col, dp_col=dp_col,
                  dp_b=dp_b, nw_col=nw_col, Mrows=Mrows, maskoff=maskoff,
                  bandmask=bandmask, idf32=idf32, eyerow8=eyerow8)

    in_maps = []
    for core in range(8):
        b, h = core // 2, core % 2
        xin = np.zeros((TW, DM), f32)
        if h == 1:
            xin[125:128] = x[b, T - 3:T]
        xin[128:] = x[b, h * T:(h + 1) * T]
        sel = np.zeros((128, 8), f32)
        if h == 1:
            sel[:, 2 * b] = 1.0
        m = dict(shared)
        m["xin"] = np.ascontiguousarray(xin)
        m["sel8"] = sel
        in_maps.append(m)
    return in_maps


_LAST_RES = None


def kernel(**inputs):
    global _LAST_RES
    nc = _get_program()
    in_maps = _host_inputs(**inputs)
    res = run_bass_kernel_spmd(nc, in_maps, core_ids=list(range(8)))
    _LAST_RES = res
    outp = np.empty((4, 4096, 1024), np.float32)
    for core in range(8):
        b, h = core // 2, core % 2
        outp[b, h * T:(h + 1) * T] = np.asarray(res.results[core]["out"])
    return outp


# version 19
# speedup vs baseline: 1.0155x; 1.0155x over previous
import numpy as np
import ml_dtypes
import concourse.bass as bass
import concourse.mybir as mybir
from concourse.bass_utils import run_bass_kernel_spmd

F32 = mybir.dt.float32
BF16 = mybir.dt.bfloat16
U8 = mybir.dt.uint8
AF = mybir.ActivationFunctionType
ALU = mybir.AluOpType
AX = mybir.AxisListType

DM = 1024
DI = 2048
S = 128
Q = 128
NCH = 16
BLK = 8
T = 2048
TW = 2176
NT = TW // 128
J = 4353
JT = 34
WF = 12288
E = 8
WIDTHS = [1024] * 4 + [2048] * 4
OFFS = np.concatenate([[0], np.cumsum(WIDTHS)]).astype(int)
CLIP = 8.0
EPS = 1e-5
TBATCH = [(0, 512), (512, 512), (1024, 512), (1536, 512), (2048, 128)]
MM2G = [(0, 3), (3, 3), (6, 3), (9, 3), (12, 3), (15, 1)]

ENGINES = ("tensor", "vector", "scalar", "gpsimd", "sync")
STRICT_SAME_ENGINE = True
DEBUG_PHASE = None


class Prog:

    def __init__(self):
        self.steps = []
        self.cnt = {e: 0 for e in ENGINES}
        self.cnt["cc"] = 0
        self.lastw = {}
        self.readers = {}

    def step(self, engine, emit, reads=(), writes=(), dma=False, tag="",
             done_sem=None, extra_waits=()):
        deps = {}

        def dep(sem, val, asyn):
            if sem == engine and not asyn and not STRICT_SAME_ENGINE:
                return
            if deps.get(sem, 0) < val:
                deps[sem] = val

        for t in reads:
            w = self.lastw.get(t)
            if w:
                dep(*w)
        for t in writes:
            w = self.lastw.get(t)
            if w:
                dep(*w)
            for r in self.readers.get(t, ()):
                dep(*r)
        for sem, val in extra_waits:
            if deps.get(sem, 0) < val:
                deps[sem] = val
        if done_sem is not None:
            sem, amt = done_sem
        else:
            sem, amt = engine, (16 if dma else 1)
        self.cnt[sem] = self.cnt.get(sem, 0) + amt
        rec = (sem, self.cnt[sem], dma or done_sem is not None)
        for t in reads:
            self.readers.setdefault(t, []).append(rec)
        for t in writes:
            self.lastw[t] = rec
            self.readers[t] = []
        self.steps.append((engine, emit, deps, (sem, amt), tag))
        return rec

    def barrier(self):
        snapshot = dict(self.cnt)
        for e in ENGINES:
            waits = {s: v for s, v in snapshot.items() if v > 0 and s != e}
            self.steps.append((e, None, waits, (e, 0), "barrier"))
        self.lastw.clear()
        self.readers.clear()

    def final_wait(self, engine="gpsimd"):
        waits = {s: v for s, v in self.cnt.items() if v > 0}
        self.steps.append((engine, None, waits, (engine, 0), "final"))

    def emit(self, nc, block, sems):
        per = {e: [] for e in ENGINES}
        for stp in self.steps:
            per[stp[0]].append(stp)

        def mk(engine, steps):
            def body(eng):
                satisfied = {}
                for _, emitfn, deps, (sem, amt), tag in steps:
                    for s9, v in sorted(deps.items()):
                        if satisfied.get(s9, 0) >= v:
                            continue
                        eng.wait_ge(sems[s9], v)
                        satisfied[s9] = v
                    if emitfn is None:
                        continue
                    ins = emitfn(eng)
                    if amt > 0:
                        if ins is None:
                            raise RuntimeError(f"step {tag} returned no instruction")
                        ins.then_inc(sems[sem], amt)
            return body

        for e in ENGINES:
            if per[e]:
                getattr(block, e)(mk(e, per[e]))


def build_program():
    nc = bass.Bass("TRN2", target_bir_lowering=False)
    p = Prog()

    def inp(name, shape, dtype=F32):
        return nc.dram_tensor(name, shape, dtype, kind="ExternalInput").ap()

    xin = inp("xin", [TW, DM])
    dp_b = inp("dp_b", [128, DI])
    WinT = inp("WinT", [DM, J])
    w1 = inp("w1", [DM, WF])
    w2 = inp("w2", [WF, DM])
    WoutT = inp("WoutT", [DI, DM])
    small9 = inp("small9", [DM, 9])
    A_log_row = inp("A_log_row", [1, S])
    cw_col = inp("cw_col", [128, 64])
    dp_col = inp("dp_col", [128, 16])
    nw_col = inp("nw_col", [128, 16])
    sel8 = inp("sel8", [128, 8])
    Mrows = inp("Mrows", [128, 512])
    maskoff = inp("maskoff", [128, 128])
    bandmask = inp("bandmask", [128, 8 * 128], U8)
    idf32 = inp("idf32", [128, 128])
    eyerow8 = inp("eyerow8", [8, 8 * 128])

    out = nc.dram_tensor("out", [T, DM], F32, kind="ExternalOutput").ap()

    xnhi_scr = nc.dram_tensor("xnhi_scr", [TW, DM], BF16, kind="Internal").ap()
    xnlo_scr = nc.dram_tensor("xnlo_scr", [TW, DM], BF16, kind="Internal").ap()
    xnT_scr = nc.dram_tensor("xnT_scr", [128, 8 * TW], BF16, kind="Internal").ap()
    z_scr = nc.dram_tensor("z_scr", [DI, TW], BF16, kind="Internal").ap()
    xm_scr = nc.dram_tensor("xm_scr", [NCH * 128, 2048], BF16, kind="Internal").ap()
    xmT_scr = nc.dram_tensor("xmT_scr", [T, DI], BF16, kind="Internal").ap()
    H_scr = nc.dram_tensor("H_scr", [WF, T], BF16, kind="Internal").ap()
    g_scr = nc.dram_tensor("g_scr", [T, DM], BF16, kind="Internal").ap()
    ag_in = nc.dram_tensor("ag_in", [S, DI], F32, kind="Internal")
    ag_out = nc.dram_tensor("ag_out", [8 * S, DI], F32, kind="Internal")

    PB = [nc.alloc_psum_tensor(f"PB{i}", [128, 512], F32) for i in range(8)]
    BT = [f"B{i}" for i in range(8)]

    def sball(name, shape, dtype=F32):
        return nc.alloc_sbuf_tensor(name, shape, dtype)

    dts_tok = sball("dts_tok", [128, NT])
    combine = sball("combine", [128, NT * 8])
    A_row = sball("A_row", [1, 128])
    ones1 = sball("ones1", [1, 128])
    onesc = sball("onesc", [128, 1])
    rows_sb = [sball(f"rows_sb{i}", [1, 512]) for i in range(2)]
    rQrow = [sball(f"rQrow{i}", [1, 128]) for i in range(2)]
    rstdrow = [sball(f"rstdrow{i}", [1, 128]) for i in range(2)]
    ck_sb = [sball(f"ck_sb{i}", [1, 1024]) for i in range(2)]
    l8row = [sball(f"l8row{i}", [1, 16]) for i in range(2)]
    Mrows_sb = sball("Mrows_sb", [128, 512])
    idb_sb = sball("idb_sb", [128, 128], BF16)
    maskoff_sb = sball("maskoff_sb", [128, 128])
    bandmask_sb = sball("bandmask_sb", [128, 8 * 128], U8)
    idf_sb = sball("idf_sb", [128, 128])
    eyerow8_sb = sball("eyerow8_sb", [8, 8 * 128])
    cw_sb = sball("cw_sb", [128, 64])
    dp_sb = sball("dp_sb", [128, 16])
    nw_sb = sball("nw_sb", [128, 16])
    sel8_sb = sball("sel8_sb", [128, 8])
    small_f = sball("small_f", [128, 8 * 9])
    small_hi = sball("small_hi", [128, 8 * 9], BF16)
    small_lo = sball("small_lo", [128, 8 * 9], BF16)
    smallhl = sball("smallhl", [128, 8 * 18], BF16)
    lnc = [sball(f"lnc{i}", [128, 16]) for i in range(2)]
    headf = sball("headf", [128, 16])
    sqc_t = sball("sqc_t", [128, 8])
    dccol = [sball(f"dccol{i}", [128, 1]) for i in range(2)]
    d8 = [sball(f"d8{i}", [128, 16]) for i in range(2)]
    srow_t = sball("srow_t", [1, 128])
    cbias = sball("cbias", [128, 2])
    rstd_col = [sball(f"rstd_col{i}", [128, 1]) for i in range(2)]

    arena_pp = 178 * 1024
    rem_pp = nc.sbuf_bytes_remaining // 128
    got = nc.bump_sbuf(arena_pp)
    assert got is not None, f"arena alloc failed rem={rem_pp}"
    abase = got[0]

    def at(name, off_pp, shape, dtype=F32):
        return nc.alloc_sbuf_tensor_at(name, shape, dtype, offset=abase + off_pp)

    K = 1024
    xnT_hi = at("xnT_hi", 0, [128, 8 * TW], BF16)
    w1buf = [at("w1buf0", 0, [128, 8 * 512], BF16),
             at("w1buf1", 8 * K, [128, 8 * 512], BF16)]
    Hrelu = at("Hrelu", 16 * K, [128, 2048], BF16)
    Hsqb = [at("Hsqb0", 20 * K, [128, 2048], BF16),
            at("Hsqb1", 24 * K, [128, 2048], BF16)]
    hbuf4 = [at("hbuf4_0", 0, [128, 8 * 1024], BF16),
             at("hbuf4_1", 16 * K, [128, 8 * 1024], BF16)]
    WoutT_sb = at("WoutT_sb", 0, [128, 16 * 1024], BF16)

    B0 = 34 * K
    xnT_lo = at("xnT_lo", B0, [128, 8 * TW], BF16)
    prebuf = [at("prebuf0", B0, [128, TW], BF16),
              at("prebuf1", B0 + 4352, [128, TW], BF16)]
    wbuf = [at("wbuf0", B0 + 8704, [128, 8 * 128], BF16),
            at("wbuf1", B0 + 10752, [128, 8 * 128], BF16)]
    zbuf = [at("zbuf0", B0 + 12800, [128, 512], BF16),
            at("zbuf1", B0 + 13824, [128, 512], BF16)]
    cvt0 = [at("cvt0a", B0 + 14848, [128, 512]),
            at("cvt0b", B0 + 16896, [128, 512])]
    cvt1 = [at("cvt1a", B0 + 19 * K, [128, 1024]),
            at("cvt1b", B0 + 23 * K, [128, 1024])]
    xnTs = [at("xnTs0", B0, [128, 2048], BF16),
            at("xnTs1", B0 + 4 * K, [128, 2048], BF16)]
    sq_fm = at("sq_fm", B0 + 8 * K, [8, 2048])
    sqb_sb = at("sqb_sb", B0 + 16 * K, [128, 2048], BF16)
    w2buf = [at("w2buf0", B0, [128, 8 * 1024], BF16),
             at("w2buf1", B0 + 16 * K, [128, 8 * 1024], BF16)]
    zc = [at("zc0", B0, [128, 2048], BF16),
          at("zc1", B0 + 4 * K, [128, 2048], BF16)]
    gbuf = [at("gbuf0", B0 + 8 * K, [128, 1024], BF16),
            at("gbuf1", B0 + 10 * K, [128, 1024], BF16)]
    xres = [at("xres0", B0 + 12 * K, [128, 1024]),
            at("xres1", B0 + 16 * K, [128, 1024])]
    outbuf = [at("outbuf0", B0 + 20 * K, [128, 1024]),
              at("outbuf1", B0 + 24 * K, [128, 1024])]
    dccol_all = at("dccol_all", B0 + 32 * K, [128, 16])

    C0 = 68 * K
    cvacc = [at("cvacc0", C0, [128, 2048]),
             at("cvacc1", C0 + 8 * K, [128, 2048])]
    cvt2 = [at("cvt2a", C0 + 16 * K, [128, 1024]),
            at("cvt2b", C0 + 20 * K, [128, 1024])]
    xmout = [at("xmout0", C0 + 24 * K, [128, 2048], BF16),
             at("xmout1", C0 + 28 * K, [128, 2048], BF16)]
    y_fm = [at("y_fm0", C0 + 32 * K, [128, 2048]),
            at("y_fm1", C0 + 40 * K, [128, 2048])]
    ysqh = [at("ysqh0", C0 + 16 * K, [128, 1024]),
            at("ysqh1", C0 + 20 * K, [128, 1024])]
    xmc = [at("xmc0", C0 + 24 * K, [128, 2048], BF16),
           at("xmc1", C0 + 28 * K, [128, 2048], BF16)]
    xm_tok = [at("xm_tok0", C0 + 48 * K, [128, 2048], BF16),
              at("xm_tok1", C0 + 52 * K, [128, 2048], BF16)]
    yg = [at("yg0", C0 + 0 * K, [128, 2048], BF16),
          at("yg1", C0 + 4 * K, [128, 2048], BF16)]
    sgbuf = [at("sgbuf0", C0 + 0 * K, [128, 1024]),
             at("sgbuf1", C0 + 4 * K, [128, 1024])]
    Zbuf = [at("Zbuf0", C0 + 56 * K, [128, 1024]),
            at("Zbuf1", C0 + 60 * K, [128, 1024])]
    Bendtok_all = at("Bendtok_all", C0 + 8 * K, [128, 2048], BF16)
    ckb = [at("ckb0", C0 + 12 * K, [128, 1024], BF16),
           at("ckb1", C0 + 14 * K, [128, 1024], BF16)]
    Wsb_all = at("Wsb_all", C0 + 16 * K, [128, 2048], BF16)
    Cs_all = at("Cs_all", C0 + 20 * K, [128, 2048], BF16)
    Dp_bcast = at("Dp_bcast", C0 + 24 * K, [128, 2048])
    xmdp = at("xmdp", C0 + 56 * K, [128, 2048])
    xmst = [at("xmst0", C0 + 32 * K, [128, 2048], BF16),
            at("xmst1", C0 + 36 * K, [128, 2048], BF16)]

    D0 = 132 * K
    S_run = at("S_run", D0, [128, 2048])
    B_fm = at("B_fm", D0 + 8 * K, [128, TW], BF16)
    C_fm = at("C_fm", D0 + 8 * K + 4352, [128, TW], BF16)
    S_run_bf = [at("S_run_bf0", D0 + 8 * K, [128, 2048], BF16),
                at("S_run_bf1", D0 + 12 * K, [128, 2048], BF16)]
    DS = D0 + 17 * K
    xnT_e = at("xnT_e", DS, [128, 8 * 2048], BF16)
    gout = [at("gout0", DS, [128, 1024], BF16),
            at("gout1", DS + 2 * K, [128, 1024], BF16)]
    MS = []
    for par in range(2):
        base = DS + par * 7 * K
        MS.append(dict(
            dtsb=at(f"dtsb{par}", base, [128, 128]),
            ebufA=at(f"ebufA{par}", base + 512, [128, 128]),
            ebufB=at(f"ebufB{par}", base + 1024, [128, 128]),
            F1=at(f"F1_{par}", base + 1536, [128, 128]),
            F2base=at(f"F2base{par}", base + 2048, [128, 128]),
            F2run=at(f"F2run{par}", base + 2560, [128, 128]),
            Btld=at(f"Btld{par}", base + 3072, [128, 128]),
            Bend=at(f"Bend{par}", base + 3584, [128, 128]),
            Bendtok=at(f"Bendtok{par}", base + 4096, [128, 128], BF16),
            Cs=at(f"Cs{par}", base + 4352, [128, 128], BF16),
            A_bar=at(f"A_bar{par}", base + 4608, [128, 128]),
            Ppp=[at(f"Ppp{par}_0", base + 5120, [128, 128]),
                 at(f"Ppp{par}_1", base + 5632, [128, 128])],
            zcw=at(f"zcw{par}", base + 6144, [128, 128]),
            Wsb=at(f"Wsb{par}", base + 6656, [128, 128], BF16),
        ))
    F2runB = [at("F2runB0", DS + 14 * K, [128, 128]),
              at("F2runB1", DS + 14 * K + 512, [128, 128])]
    F2hist = [at("F2hist0", DS + 16 * K, [128, 15 * 128]),
              at("F2hist1", DS + 24 * K, [128, 15 * 128])]
    ACC_OFFS = [D0, D0 + 4 * K, D0 + 8 * K, D0 + 12 * K,
                C0, C0 + 4 * K, C0 + 12 * K, C0 + 24 * K,
                C0 + 28 * K, C0 + 32 * K, C0 + 36 * K, C0 + 40 * K,
                C0 + 44 * K, C0 + 48 * K, C0 + 52 * K, C0 + 56 * K]
    accs = [at(f"accs{t}", off, [128, 1024]) for t, off in enumerate(ACC_OFFS)]
    PS1 = DS
    xq = [at("xq0", PS1, [128, 1024]), at("xq1", PS1 + 4 * K, [128, 1024])]
    lnjunk = at("lnjunk", PS1 + 8 * K, [128, 1024])
    xnf = [at("xnf0", PS1 + 12 * K, [128, 1024]),
           at("xnf1", PS1 + 16 * K, [128, 1024])]
    xnhib = [at("xnhib0", PS1 + 20 * K, [128, 1024], BF16),
             at("xnhib1", PS1 + 22 * K, [128, 1024], BF16)]
    xnlof = [at("xnlof0", PS1 + 24 * K, [128, 1024]),
             at("xnlof1", PS1 + 28 * K, [128, 1024])]

    def dma(queue, o, i, reads, writes, tag="", **kw):
        return p.step(queue, lambda e, o=o, i=i, kw=kw: e.dma_start(out=o, in_=i, **kw),
                      reads=reads, writes=writes, dma=True, tag=tag)

    def dmat(queue, o, i, reads, writes, tag=""):
        return p.step(queue, lambda e, o=o, i=i: e.dma_start_transpose(o, i),
                      reads=reads, writes=writes, dma=True, tag=tag)

    def st(engine, fn, reads=(), writes=(), tag="", extra_waits=()):
        return p.step(engine, fn, reads=reads, writes=writes, tag=tag,
                      extra_waits=extra_waits)

    dma("sync", Mrows_sb[:, :], Mrows, [], ["Mrows"])
    dma("sync", maskoff_sb[:, :], maskoff, [], ["maskoff"])
    dma("sync", bandmask_sb[:, :], bandmask, [], ["bandmask"])
    dma("sync", idf_sb[:, :], idf32, [], ["idf"])
    st("vector", lambda e: e.tensor_copy(idb_sb[:, :], idf_sb[:, :]),
       ["idf"], ["idb"])
    dma("sync", eyerow8_sb[:, :], eyerow8, [], ["eyerow8"])
    dma("sync", cw_sb[:, :], cw_col, [], ["cw"])
    dma("sync", dp_sb[:, :], dp_col, [], ["dp"])
    dma("sync", nw_sb[:, :], nw_col, [], ["nw"])
    dma("sync", sel8_sb[:, :], sel8, [], ["sel8"])
    dma("sync", small_f[:, :].rearrange("p (s e) -> p s e", s=8),
        small9.rearrange("(s p) e -> p s e", p=128),
        [], ["small_f"], allow_slow_non_contiguous=True)
    dma("sync", srow_t[:, :], A_log_row, [], ["alog"])
    st("vector", lambda e: e.memset(cbias[:, 0:1], EPS), [], ["cbias"])
    st("vector", lambda e: e.memset(cbias[:, 1:2], 1.0), [], ["cbias"])
    st("vector", lambda e: e.memset(ones1[:, :], 1.0), [], ["ones1"])
    st("vector", lambda e: e.memset(onesc[:, :], 1.0), [], ["onesc"])
    for par in range(2):
        st("vector", lambda e, par=par: e.memset(Zbuf[par][:, :], 0.0),
           [], [f"Zbuf{par}"])
    st("scalar", lambda e: e.activation(srow_t[:, :], srow_t[:, :], AF.Exp),
       ["alog"], ["alog"])
    st("vector", lambda e: e.tensor_scalar_mul(A_row[:, :], srow_t[:, :], -1.0),
       ["alog"], ["A_row"])
    st("vector", lambda e: e.tensor_copy(small_hi[:, :], small_f[:, :]),
       ["small_f"], ["small_hi"])
    st("vector", lambda e: e.tensor_sub(small_lo[:, :], small_f[:, :], small_hi[:, :]),
       ["small_f", "small_hi"], ["small_lo"])
    for s8 in range(8):
        st("vector", lambda e, s8=s8: e.tensor_copy(
            smallhl[:, s8 * 18:s8 * 18 + 9], small_hi[:, s8 * 9:(s8 + 1) * 9]),
           ["small_hi"], ["smallhl"])
        st("vector", lambda e, s8=s8: e.tensor_copy(
            smallhl[:, s8 * 18 + 9:s8 * 18 + 18], small_lo[:, s8 * 9:(s8 + 1) * 9]),
           ["small_lo"], ["smallhl"])

    for i in range(NT):
        pr = i % 2
        xb = xq[pr]
        xtok = f"xq{pr}"
        ln = lnc[pr]
        ltok = f"lnc{pr}"
        dma("sync", xb[:, :], xin[i * 128:(i + 1) * 128, :], [], [xtok], tag=f"ldx{i}")
        st("scalar", lambda e, xb=xb, ln=ln: e.activation(
            lnjunk[:, :], xb[:, :], AF.Square, accum_out=ln[:, 0:1]),
           [xtok, "lnjunk"], ["lnjunk", ltok], tag=f"ssq{i}")
        st("scalar", lambda e, xb=xb, ln=ln: e.activation(
            lnjunk[:, :], xb[:, :], AF.Copy, accum_out=ln[:, 1:2]),
           [xtok, "lnjunk"], ["lnjunk", ltok], tag=f"msum{i}")
        st("vector", lambda e, ln=ln: e.tensor_scalar_mul(ln[:, 2:3], ln[:, 1:2], 1.0 / DM),
           [ltok], [ltok])
        st("vector", lambda e, ln=ln: e.tensor_scalar_mul(ln[:, 3:4], ln[:, 0:1], 1.0 / DM),
           [ltok], [ltok])
        st("vector", lambda e, ln=ln: e.scalar_tensor_tensor(
            ln[:, 4:5], ln[:, 2:3], ln[:, 2:3], ln[:, 3:4], ALU.mult, ALU.subtract),
           [ltok], [ltok])
        st("scalar", lambda e, ln=ln: e.activation(ln[:, 5:6], ln[:, 4:5], AF.Ln,
                                                   scale=-1.0, bias=cbias[:, 0:1]),
           [ltok, "cbias"], [ltok])
        st("scalar", lambda e, ln=ln: e.activation(ln[:, 6:7], ln[:, 5:6], AF.Exp,
                                                   scale=-0.5),
           [ltok], [ltok])
        st("vector", lambda e, xb=xb, ln=ln, pr=pr: e.tensor_scalar(
            xnhib[pr][:, :], xb[:, :], ln[:, 2:3], ln[:, 6:7], ALU.subtract, ALU.mult),
           [xtok, ltok], [f"xnhib{pr}"], tag=f"xnhi{i}")
        st("vector", lambda e, xb=xb, ln=ln, pr=pr: e.tensor_scalar(
            xnf[pr][:, :], xb[:, :], ln[:, 2:3], ln[:, 6:7], ALU.subtract, ALU.mult),
           [xtok, ltok], [f"xnf{pr}"], tag=f"xnf{i}")
        st("vector", lambda e, pr=pr: e.tensor_sub(
            xnlof[pr][:, :], xnf[pr][:, :], xnhib[pr][:, :]),
           [f"xnf{pr}", f"xnhib{pr}"], [f"xnlof{pr}"], tag=f"xnlo{i}")
        for sl in range(8):
            bank = sl % 4
            st("tensor", lambda e, pr=pr, sl=sl, bank=bank: e.transpose(
                PB[bank][:, 0:128], xnf[pr][:, sl * 128:(sl + 1) * 128], idf_sb[:, :]),
               [f"xnf{pr}", "idf"], [BT[bank]], tag=f"trh{i}_{sl}")
            st("vector", lambda e, i=i, sl=sl, bank=bank: e.tensor_copy(
                xnT_hi[:, sl * TW + i * 128: sl * TW + (i + 1) * 128],
                PB[bank][:, 0:128]),
               [BT[bank], f"xnT_hi_{sl}"], [f"xnT_hi_{sl}"], tag=f"trhev{i}_{sl}")
            st("tensor", lambda e, pr=pr, sl=sl, bank=bank: e.transpose(
                PB[bank][:, 128:256], xnlof[pr][:, sl * 128:(sl + 1) * 128],
                idf_sb[:, :]),
               [f"xnlof{pr}", "idf"], [BT[bank]], tag=f"trl{i}_{sl}")
            st("vector", lambda e, i=i, sl=sl, bank=bank: e.tensor_copy(
                xnT_lo[:, sl * TW + i * 128: sl * TW + (i + 1) * 128],
                PB[bank][:, 128:256]),
               [BT[bank], f"xnT_lo_{sl}"], [f"xnT_lo_{sl}"], tag=f"trlev{i}_{sl}")

    dma("sync", xnT_scr, xnT_hi[:, :],
        [f"xnT_hi_{sl}" for sl in range(8)], ["xnT_scr"], tag="xnTscr")

    for i in range(1, NT):
        def head_mm(e, i=i):
            last = None
            for dm8 in range(8):
                lh = xnT_hi[:, dm8 * TW + i * 128: dm8 * TW + (i + 1) * 128]
                last = e.matmul(PB[6][:, 0:18], lh, smallhl[:, dm8 * 18:(dm8 + 1) * 18],
                                start=(dm8 == 0), stop=False)
            for dm8 in range(8):
                ll = xnT_lo[:, dm8 * TW + i * 128: dm8 * TW + (i + 1) * 128]
                last = e.matmul(PB[6][:, 0:9], ll, small_hi[:, dm8 * 9:(dm8 + 1) * 9],
                                start=False, stop=(dm8 == 7))
            return last
        st("tensor", head_mm,
           [f"xnT_hi_{d}" for d in range(8)] + [f"xnT_lo_{d}" for d in range(8)]
           + ["smallhl", "small_hi"], ["B6"], tag=f"headmm{i}")
        st("vector", lambda e: e.tensor_copy(lnjunk[:, 0:18], PB[6][:, 0:18]),
           ["B6", "lnjunk"], ["lnjunk"], tag=f"headcp{i}")
        st("vector", lambda e: e.tensor_add(headf[:, 0:9], lnjunk[:, 0:9],
                                            lnjunk[:, 9:18]),
           ["lnjunk", "headf"], ["headf"], tag=f"headadd{i}")
        st("scalar", lambda e: e.activation(headf[:, 9:10], headf[:, 8:9], AF.Abs),
           ["headf"], ["headf"])
        st("scalar", lambda e: e.activation(headf[:, 10:11], headf[:, 9:10], AF.Exp,
                                            scale=-1.0),
           ["headf"], ["headf"])
        st("scalar", lambda e: e.activation(headf[:, 11:12], headf[:, 10:11], AF.Ln,
                                            bias=cbias[:, 1:2]),
           ["headf", "cbias"], ["headf"])
        st("vector", lambda e: e.tensor_scalar_max(headf[:, 12:13], headf[:, 8:9], 0.0),
           ["headf"], ["headf"])
        st("vector", lambda e, i=i: e.tensor_add(
            dts_tok[:, i:i + 1], headf[:, 11:12], headf[:, 12:13]),
           ["headf"], ["dts_tok"], tag=f"dts{i}")
        csl = combine[:, i * 8:(i + 1) * 8]
        st("scalar", lambda e, csl=csl: e.activation(csl, headf[:, 0:8], AF.Sigmoid),
           ["headf"], ["probs"], tag=f"sig{i}")
        st("vector", lambda e, csl=csl: e.reduce_max(headf[:, 13:14], csl, AX.X),
           ["probs", "headf"], ["headf"])
        st("vector", lambda e, csl=csl: e.tensor_scalar(
            lnjunk[:, 0:8], csl, headf[:, 13:14], None, ALU.is_equal),
           ["probs", "headf", "lnjunk"], ["lnjunk"])
        st("vector", lambda e, csl=csl: e.scalar_tensor_tensor(
            lnjunk[:, 8:16], lnjunk[:, 0:8], -1e30, csl, ALU.mult, ALU.add),
           ["lnjunk", "probs"], ["lnjunk"])
        st("vector", lambda e: e.reduce_max(headf[:, 14:15], lnjunk[:, 8:16], AX.X),
           ["lnjunk", "headf"], ["headf"])
        st("vector", lambda e: e.scalar_tensor_tensor(
            headf[:, 15:16], headf[:, 13:14], 1.0, headf[:, 14:15], ALU.mult, ALU.add),
           ["headf"], ["headf"])
        st("vector", lambda e: e.tensor_scalar_add(headf[:, 15:16], headf[:, 15:16], 1e-20),
           ["headf"], ["headf"])
        st("vector", lambda e: e.reciprocal(headf[:, 13:14], headf[:, 15:16]),
           ["headf"], ["headf"])
        st("vector", lambda e, csl=csl: e.tensor_scalar(
            lnjunk[:, 0:8], csl, headf[:, 14:15], None, ALU.is_ge),
           ["probs", "headf", "lnjunk"], ["lnjunk"])
        st("vector", lambda e, csl=csl: e.tensor_mul(csl, csl, lnjunk[:, 0:8]),
           ["probs", "lnjunk"], ["probs"])
        st("vector", lambda e, csl=csl: e.tensor_scalar_mul(csl, csl, headf[:, 13:14]),
           ["probs", "headf"], ["probs"], tag=f"comb{i}")

    p.barrier()
    if DEBUG_PHASE == "p1":
        p.final_wait()
        return nc, p

    def chunk_rows(c):
        par = c % 2
        mb = 6 + par
        mt = BT[mb]
        rows = rows_sb[par]
        rt = f"rows{par}"
        st("tensor", lambda e, c=c, mb=mb: e.matmul(
            PB[mb][0:1, 0:512], dts_tok[:, c + 1:c + 2], Mrows_sb[:, :],
            start=True, stop=True),
           ["dts_tok", "Mrows"], [mt], tag=f"rows{c}")
        st("vector", lambda e, rows=rows, mb=mb: e.tensor_copy(
            rows[:, :], PB[mb][0:1, 0:512]),
           [mt], [rt], tag=f"rowsev{c}")
        st("vector", lambda e, rows=rows, par=par: e.tensor_scalar(
            rQrow[par][:, :], rows[:, 384:512], -1.0, rows[0:1, 511:512],
            ALU.mult, ALU.add),
           [rt], [f"rQ{par}"], tag=f"rq{c}")
        st("vector", lambda e, rows=rows, par=par: e.tensor_copy(
            l8row[par][:, :], rows[0:1, 135:256:8]),
           [rt], [f"l8{par}"], tag=f"l8{c}")
        st("tensor", lambda e, rows=rows, mb=mb: e.matmul(
            PB[mb][:, 0:128], ones1[:, :], rows[0:1, 0:128], start=True, stop=True),
           [rt, "ones1"], [mt], tag=f"dtsbmm{c}")
        st("vector", lambda e, par=par, mb=mb: e.tensor_copy(
            MS[par]["dtsb"][:, :], PB[mb][:, 0:128]),
           [mt], [f"dtsb{par}"], tag=f"dtsbev{c}")
        st("tensor", lambda e, par=par, mb=mb: e.matmul(
            PB[mb][:, 128:144], A_row[:, :], l8row[par][:, :], start=True, stop=True),
           [f"l8{par}", "A_row"], [mt], tag=f"d8mm{c}")
        st("scalar", lambda e, par=par, mb=mb: e.activation(
            d8[par][:, :], PB[mb][:, 128:144], AF.Exp),
           [mt], [f"d8{par}"], tag=f"d8e{c}")
        st("tensor", lambda e, rows=rows, mb=mb: e.matmul(
            PB[mb][:, 144:145], A_row[:, :], rows[0:1, 511:512], start=True, stop=True),
           [rt, "A_row"], [mt], tag=f"dcmm{c}")
        st("scalar", lambda e, c=c, mb=mb: e.activation(
            dccol_all[:, c:c + 1], PB[mb][:, 144:145], AF.Exp),
           [mt], [f"dcc{c}"], tag=f"dce{c}")

    def chunk_exp(c, rowap, rowtok, outsb, outtok, slot, tag):
        ab = 4 + c % 2
        abt = BT[ab]
        off = 0 if slot == 0 else 128
        st("tensor", lambda e, rowap=rowap, ab=ab, off=off: e.matmul(
            PB[ab][:, off:off + 128], A_row[:, :], rowap, start=True, stop=True),
           [rowtok, "A_row"], [abt], tag=f"{tag}mm{c}")
        st("scalar", lambda e, outsb=outsb, ab=ab, off=off: e.activation(
            outsb[:, :], PB[ab][:, off:off + 128], AF.Exp),
           [abt], [outtok], tag=f"{tag}e{c}")

    def chunk_btld(c):
        par = c % 2
        cc0 = 128 + c * 128
        st("vector", lambda e, cc0=cc0, par=par: e.tensor_mul(
            MS[par]["Btld"][:, :], B_fm[:, cc0:cc0 + 128], MS[par]["dtsb"][:, :]),
           ["B_fm", f"dtsb{par}"], [f"Btld{par}"], tag=f"btld{c}")

    def chunk_bend(c):
        par = c % 2
        mb = 6 + par
        mt = BT[mb]
        m = MS[par]
        chunk_exp(c, rQrow[par][:, :], f"rQ{par}", m["ebufA"], f"ebufA{par}", 0, "erq")
        st("vector", lambda e, m=m, par=par: e.tensor_mul(
            m["Bend"][:, :], m["Btld"][:, :], m["ebufA"][:, :]),
           [f"Btld{par}", f"ebufA{par}"], [f"Bend{par}"], tag=f"bend{c}")
        st("tensor", lambda e, m=m, mb=mb: e.transpose(
            PB[mb][:, 384:512], m["Bend"][:, :], idf_sb[:, :]),
           [f"Bend{par}", "idf"], [mt], tag=f"bendtr{c}")
        st("vector", lambda e, c=c, mb=mb: e.tensor_copy(
            Bendtok_all[:, c * 128:(c + 1) * 128], PB[mb][:, 384:512]),
           [mt], [f"Bet{c}", "cvacc1"], tag=f"bendev{c}")

    def chunk_xmtok(c):
        par = c % 2
        xf = y_fm[par]
        xftok = f"y_fm{par}"
        st("vector", lambda e, par=par, xf=xf: e.tensor_copy(xf[:, :], xmc[par][:, :]),
           [f"xmc{par}", xftok], [xftok], tag=f"xmf32_{c}")
        for sl in range(16):
            bk = sl % 4
            off = 128 * par
            st("tensor", lambda e, sl=sl, bk=bk, off=off, xf=xf: e.transpose(
                PB[bk][:, off:off + 128], xf[:, sl * 128:(sl + 1) * 128], idf_sb[:, :]),
               [xftok, "idf"], [BT[bk]], tag=f"xtr{c}_{sl}")
            st("vector", lambda e, sl=sl, bk=bk, off=off, par=par: e.tensor_copy(
                xm_tok[par][:, sl * 128:(sl + 1) * 128], PB[bk][:, off:off + 128]),
               [BT[bk], f"xm_tok{par}"], [f"xm_tok{par}"], tag=f"xtrev{c}_{sl}")

    def chunk_state(c):
        par = c % 2
        def smm(e, c=c, par=par):
            last = None
            for q4 in range(4):
                last = e.matmul(PB[q4][:, :], Bendtok_all[:, c * 128:(c + 1) * 128],
                                xm_tok[par][:, q4 * 512:(q4 + 1) * 512],
                                start=True, stop=True)
            return last
        st("tensor", smm, [f"Bet{c}", f"xm_tok{par}"], BT[0:4], tag=f"smm{c}")
        for q4 in range(4):
            st("vector", lambda e, q4=q4, c=c: e.scalar_tensor_tensor(
                S_run[:, q4 * 512:(q4 + 1) * 512], S_run[:, q4 * 512:(q4 + 1) * 512],
                dccol_all[:, c:c + 1], PB[q4][:, :], ALU.mult, ALU.add),
               ["S_run", f"dcc{c}", BT[q4]], ["S_run"], tag=f"sup{c}_{q4}")

    def prep_exps(c):
        par = c % 2
        m = MS[par]
        cc0 = 128 + c * 128
        chunk_exp(c, rows_sb[par][0:1, 0:128], f"rows{par}", m["A_bar"],
                  f"A_bar{par}", 0, "abar")
        chunk_exp(c, rows_sb[par][0:1, 128:256], f"rows{par}", m["ebufB"],
                  f"ebufB{par}", 1, "elc")
        st("vector", lambda e, cc0=cc0, m=m, par=par: e.tensor_mul(
            m["F1"][:, :], C_fm[:, cc0:cc0 + 128], m["ebufB"][:, :]),
           ["C_fm", f"ebufB{par}", f"F1{par}"], [f"F1{par}"], tag=f"f1_{c}")
        chunk_exp(c, rows_sb[par][0:1, 256:384], f"rows{par}", m["ebufA"],
                  f"ebufA{par}", 0, "er8")
        st("vector", lambda e, m=m, par=par: e.tensor_mul(
            m["F2base"][:, :], m["Btld"][:, :], m["ebufA"][:, :]),
           [f"Btld{par}", f"ebufA{par}", f"F2base{par}"], [f"F2base{par}"],
           tag=f"f2b{c}")
        chunk_exp(c, rows_sb[par][0:1, 384:512], f"rows{par}", m["ebufB"],
                  f"ebufB{par}", 1, "eg")
        st("vector", lambda e, cc0=cc0, m=m, par=par, c=c: e.tensor_mul(
            Cs_all[:, c * 128:(c + 1) * 128], C_fm[:, cc0:cc0 + 128],
            m["ebufB"][:, :]),
           ["C_fm", f"ebufB{par}", f"Cs{c}"], [f"Cs{c}", "cvt21"],
           tag=f"csx{c}")

    def f2_chain(c):
        par = c % 2
        m = MS[par]
        h = F2hist[par]
        htok = f"F2hist{par}"
        st("vector", lambda e, h=h: e.memset(h[:, 8:128], 0.0),
           [htok], [htok], tag=f"f2z{c}")
        st("vector", lambda e, h=h, m=m: e.tensor_copy(
            h[:, 0:8], m["F2base"][:, 0:8]),
           [f"F2base{par}", htok], [htok], tag=f"f2c{c}_1")
        for I in range(2, 16):
            st("vector", lambda e, h=h, I=I, par=par:
               e.tensor_scalar_mul(h[:, (I - 1) * 128:I * 128],
                                   h[:, (I - 2) * 128:(I - 1) * 128],
                                   d8[par][:, I - 1:I]),
               [htok, f"d8{par}"], [htok], tag=f"f2r{c}_{I}")
            st("vector", lambda e, h=h, I=I, m=m: e.tensor_copy(
                h[:, (I - 1) * 128 + (I - 1) * 8:(I - 1) * 128 + I * 8],
                m["F2base"][:, (I - 1) * 8:I * 8]),
               [f"F2base{par}", htok], [htok], tag=f"f2c{c}_{I}")

    def w_fin(c):
        par = c % 2
        ab = 4 + par
        abt = BT[ab]
        m = MS[par]
        h = F2hist[par]
        htok = f"F2hist{par}"
        for I in range(1, 16):
            st("tensor", lambda e, I=I, h=h, m=m, ab=ab: e.matmul(
                PB[ab][:, 256 + I * 8:256 + (I + 1) * 8],
                h[:, (I - 1) * 128:I * 128],
                m["F1"][:, I * 8:(I + 1) * 8], start=True, stop=True),
               [htok, f"F1{par}"], [abt], tag=f"wmm{c}_{I}")
        st("vector", lambda e, c=c, ab=ab: e.tensor_mul(
            Wsb_all[:, c * 128:(c + 1) * 128], PB[ab][:, 256:384],
            maskoff_sb[:, :]),
           [abt, "maskoff", f"Wsb{c}"], [f"Wsb{c}", "cvt20"], tag=f"wev{c}")

    def band_chain(c):
        par = c % 2
        ab = 4 + par
        abt = BT[ab]
        m = MS[par]
        cc0 = 128 + c * 128
        st("vector", lambda e, cc0=cc0, m=m, par=par: e.tensor_mul(
            Zbuf[par][:, 0:128], C_fm[:, cc0:cc0 + 128], m["Btld"][:, :]),
           ["C_fm", f"Btld{par}", f"Zbuf{par}"], [f"Zbuf{par}"], tag=f"z0_{c}")
        for k9 in range(1, BLK):
            src = m["Ppp"][(k9 - 1) % 2]
            dst = m["Ppp"][k9 % 2]
            stok, dtok = f"Ppp{par}_{(k9 - 1) % 2}", f"Ppp{par}_{k9 % 2}"
            if k9 == 1:
                st("vector", lambda e, dst=dst, m=m: e.tensor_copy(
                    dst[:, :], m["A_bar"][:, :]),
                   [f"A_bar{par}", dtok], [dtok], tag=f"p1_{c}")
            else:
                st("vector", lambda e, src=src, dst=dst, m=m: e.tensor_mul(
                    dst[:, 1:128], m["A_bar"][:, 1:128], src[:, 0:127]),
                   [f"A_bar{par}", stok, dtok], [dtok], tag=f"p{k9}_{c}")
            st("vector", lambda e, dst=dst, cc0=cc0, m=m: e.tensor_mul(
                m["zcw"][:, :], C_fm[:, cc0:cc0 + 128], dst[:, :]),
               ["C_fm", dtok, f"zcw{par}"], [f"zcw{par}"], tag=f"zcw{k9}_{c}")
            st("vector", lambda e, k9=k9, m=m, par=par: e.tensor_mul(
                Zbuf[par][:, k9 * 128 + k9:(k9 + 1) * 128],
                m["zcw"][:, k9:128], m["Btld"][:, 0:128 - k9]),
               [f"zcw{par}", f"Btld{par}", f"Zbuf{par}"], [f"Zbuf{par}"],
               tag=f"z{k9}_{c}")

    def band_fin(c):
        par = c % 2
        ab = 4 + par
        abt = BT[ab]
        for h2 in range(2):
            st("tensor", lambda e, h2=h2, par=par, ab=ab: e.matmul(
                PB[ab][0:1, 0:512], onesc[:, :],
                Zbuf[par][:, h2 * 512:(h2 + 1) * 512], start=True, stop=True),
               [f"Zbuf{par}", "onesc"], [abt], tag=f"ckmm{c}_{h2}")
            st("vector", lambda e, h2=h2, par=par, ab=ab: e.tensor_copy(
                ck_sb[par][:, h2 * 512:(h2 + 1) * 512], PB[ab][0:1, 0:512]),
               [abt, f"ck{par}"], [f"ck{par}"], tag=f"ckev{c}_{h2}")
        for h2 in range(2):
            st("tensor", lambda e, h2=h2, par=par, ab=ab: e.matmul(
                PB[ab][:, 0:512], ones1[:, :],
                ck_sb[par][0:1, h2 * 512:(h2 + 1) * 512], start=True, stop=True),
               [f"ck{par}", "ones1"], [abt], tag=f"ckbmm{c}_{h2}")
            st("vector", lambda e, h2=h2, par=par, ab=ab: e.tensor_copy(
                ckb[par][:, h2 * 512:(h2 + 1) * 512], PB[ab][:, 0:512]),
               [abt, f"ckb{par}"], [f"ckb{par}", "cvacc1"],
               tag=f"ckbev{c}_{h2}")
        for k9 in range(BLK):
            st("vector", lambda e, k9=k9, par=par, c=c: e.copy_predicated(
                Wsb_all[:, c * 128:(c + 1) * 128],
                bandmask_sb[:, k9 * 128:(k9 + 1) * 128],
                ckb[par][:, k9 * 128:(k9 + 1) * 128]),
               ["bandmask", f"ckb{par}", f"Wsb{c}"], [f"Wsb{c}"],
               tag=f"band{c}_{k9}")

    st("vector", lambda e: e.memset(S_run[:, :], 0.0), [], ["S_run"])
    for par in range(2):
        st("vector", lambda e, par=par: e.memset(MS[par]["F2run"][:, :], 0.0),
           [f"F2run{par}"], [f"F2run{par}"], tag=f"msf2_{par}")
        st("vector", lambda e, par=par: e.memset(F2runB[par][:, :], 0.0),
           [f"F2runB{par}"], [f"F2runB{par}"], tag=f"msf2b_{par}")
        st("vector", lambda e, par=par: e.memset(MS[par]["Ppp"][0][:, :], 1.0),
           [f"Ppp{par}_0"], [f"Ppp{par}_0"], tag=f"msp0_{par}")
        st("vector", lambda e, par=par: e.memset(MS[par]["Ppp"][1][:, :], 1.0),
           [f"Ppp{par}_1"], [f"Ppp{par}_1"], tag=f"msp1_{par}")

    def evac_z(jt, tb_i, t0, tn, bank):
        pz = tb_i % 2
        zb = zbuf[pz]
        ztok = f"zbuf{pz}"
        ct = cvt0[pz]
        ctok = f"cvt0{pz}"
        st("scalar", lambda e, tn=tn, bank=bank, ct=ct: e.activation(
            ct[:, 0:tn], PB[bank][:, 0:tn], AF.Silu),
           [BT[bank], ctok], [ctok], tag=f"zsilu{jt}_{tb_i}")
        st("vector", lambda e, tn=tn, zb=zb, ct=ct: e.tensor_scalar_min(
            zb[:, 0:tn], ct[:, 0:tn], CLIP),
           [ctok], [ztok])
        dma("sync", z_scr[jt * 128:(jt + 1) * 128, t0:t0 + tn], zb[:, 0:tn],
            [ztok], [f"zscr{jt}_{tb_i}"], tag=f"zst{jt}_{tb_i}")

    def evac_xm(jt, tb_i, t0, tn, bank):
        sl = jt - 16
        pb = prebuf[sl % 2]
        ptok = f"prebuf{sl % 2}"
        ct = cvt1[tb_i % 2]
        ctok = f"cvt1{tb_i % 2}"
        st("scalar", lambda e, tn=tn, bank=bank, ct=ct: e.activation(
            ct[:, 0:tn], PB[bank][:, 0:tn], AF.Silu),
           [BT[bank], ctok], [ctok], tag=f"xmsilu{jt}_{tb_i}")
        st("vector", lambda e, t0=t0, tn=tn, pb=pb, ct=ct: e.tensor_scalar_min(
            pb[:, t0:t0 + tn], ct[:, 0:tn], CLIP),
           [ctok, ptok], [ptok])

    def evac_bc(jt, tb_i, t0, tn, bank):
        dst = B_fm if jt == 32 else C_fm
        tok = "B_fm" if jt == 32 else "C_fm"
        st("vector", lambda e, dst=dst, t0=t0, tn=tn, bank=bank: e.tensor_copy(
            dst[:, t0:t0 + tn], PB[bank][:, 0:tn]),
           [BT[bank]], [tok], tag=f"bc{jt}_{tb_i}")

    def conv_slot(sl):
        pb = prebuf[sl % 2]
        ptok = f"prebuf{sl % 2}"
        acc = cvacc[sl % 2]
        atok = f"cvacc{sl % 2}"
        xo = xmout[sl % 2]
        xotok = f"xmout{sl % 2}"
        st("vector", lambda e, sl=sl, pb=pb, acc=acc: e.tensor_scalar(
            acc[:, :], pb[:, 125:125 + T], cw_sb[:, sl * 4:sl * 4 + 1], None, ALU.mult),
           [ptok, "cw", atok], [atok], tag=f"cv0_{sl}")
        for k4 in range(1, 4):
            st("vector", lambda e, sl=sl, k4=k4, pb=pb, acc=acc: e.scalar_tensor_tensor(
                acc[:, :], pb[:, 125 + k4:125 + k4 + T],
                cw_sb[:, sl * 4 + k4:sl * 4 + k4 + 1], acc[:, :], ALU.mult, ALU.add),
               [ptok, "cw", atok], [atok], tag=f"cv{k4}_{sl}")
        for hh in range(2):
            cv = cvt2[hh]
            cvtok = f"cvt2{hh}"
            st("scalar", lambda e, acc=acc, hh=hh, cv=cv: e.activation(
                cv[:, :], acc[:, hh * 1024:(hh + 1) * 1024], AF.Silu),
               [atok, cvtok], [cvtok], tag=f"cs{hh}_{sl}")
        xs = xmst[sl % 2]
        xstok = f"xmst{sl % 2}"
        for cc in range(NCH):
            bk = 4 + cc % 4
            cv = cvt2[cc // 8]
            cvtok = f"cvt2{cc // 8}"
            st("tensor", lambda e, cc=cc, bk=bk, cv=cv: e.transpose(
                PB[bk][:, 0:128], cv[:, (cc % 8) * 128:(cc % 8 + 1) * 128],
                idf_sb[:, :]),
               [cvtok, "idf"], [BT[bk]], tag=f"cvtr{sl}_{cc}")
            st("vector", lambda e, cc=cc, bk=bk, xs=xs: e.tensor_scalar_min(
                xs[:, cc * 128:(cc + 1) * 128], PB[bk][:, 0:128], CLIP),
               [BT[bk], xstok], [xstok], tag=f"cvtrev{sl}_{cc}")
        dma("sync",
            xmT_scr[:, sl * 128:(sl + 1) * 128].rearrange("(c t) f -> t c f",
                                                          t=128),
            xs[:, :].rearrange("p (c f) -> p c f", c=16),
            [xstok], [f"xmT_{sl}"], tag=f"stxmT{sl}")

    ipq = 0
    for jti, jt in enumerate(list(range(16, 34)) + list(range(16))):
        wb = wbuf[jti % 2]
        wtok = f"wbuf{jti % 2}"
        dma("gpsimd", wb[:, :].rearrange("p (s j) -> p s j", s=8),
            WinT.rearrange("(s p) j -> p s j", p=128)[:, :, jt * 128:(jt + 1) * 128],
            [], [wtok], tag=f"ldw{jt}")
        for tb_i, (t0, tn) in enumerate(TBATCH):
            bank = ipq % 4
            ipq += 1
            def mmj(e, wb=wb, t0=t0, tn=tn, bank=bank):
                last = None
                for dm8 in range(8):
                    last = e.matmul(
                        PB[bank][:, 0:tn],
                        wb[:, dm8 * 128:(dm8 + 1) * 128],
                        xnT_hi[:, dm8 * TW + t0: dm8 * TW + t0 + tn],
                        start=(dm8 == 0), stop=(dm8 == 7))
                return last
            st("tensor", mmj, [wtok], [BT[bank]], tag=f"ip{jt}_{tb_i}")
            if jt < 16:
                evac_z(jt, tb_i, t0, tn, bank)
            elif jt < 32:
                evac_xm(jt, tb_i, t0, tn, bank)
            else:
                evac_bc(jt, tb_i, t0, tn, bank)
        if 16 <= jt < 32:
            conv_slot(jt - 16)
        def stage_B(c):
            chunk_rows(c)
            chunk_btld(c)
            prep_exps(c)
            f2_chain(c)
            band_chain(c)

        def stage_C(c):
            w_fin(c)
            band_fin(c)
            chunk_bend(c)

        def stage_D(c):
            par = c % 2
            dma("gpsimd", xm_tok[par][:, :], xmT_scr[c * 128:(c + 1) * 128, :],
                [f"xm_tok{par}"] + [f"xmT_{sl}" for sl in range(16)],
                [f"xm_tok{par}"], tag=f"ldxmT3a_{c}")
            chunk_state(c)

        if 0 <= jti - 18 < NCH:
            stage_B(jti - 18)
        if 0 <= jti - 19 < NCH:
            stage_C(jti - 19)
        if 0 <= jti - 20 < NCH:
            stage_D(jti - 20)

    stage_C(15)
    stage_D(14)
    stage_D(15)

    dma("sync", ag_in.ap(), S_run[:, :], ["S_run"], ["ag_in"], tag="agst")
    p.step("gpsimd", lambda e: e.collective_compute(
        "AllGather", ALU.bypass, replica_groups=[list(range(8))],
        ins=[ag_in.ap().opt()], outs=[ag_out.ap().opt()]),
        reads=["ag_in"], writes=["ag_out"], done_sem=("cc", 1), tag="allgather")

    p.barrier()
    if DEBUG_PHASE == "p3a":
        p.final_wait()
        return nc, p

    for i in range(16):
        st("scalar", lambda e, i=i: e.activation(
            sqc_t[:, :], combine[:, (i + 1) * 8:(i + 2) * 8], AF.Sqrt),
           ["combine", "sqc"], ["sqc"], tag=f"sqrt{i}")
        st("tensor", lambda e: e.transpose(PB[6][0:8, 0:128], sqc_t[:, :], idf_sb[:, :]),
           ["sqc", "idf"], ["B6"], tag=f"sqtr{i}")
        st("vector", lambda e, i=i: e.tensor_copy(
            sq_fm[0:8, i * 128:(i + 1) * 128], PB[6][0:8, 0:128]),
           ["B6"], ["sq_fm"], tag=f"sqev{i}")

    wti = 0
    for e8 in range(E):
        we = WIDTHS[e8]
        def sqbmm(e, e8=e8):
            last = None
            for q4 in range(4):
                last = e.matmul(PB[q4][:, :], eyerow8_sb[:, e8 * 128:(e8 + 1) * 128],
                                sq_fm[0:8, q4 * 512:(q4 + 1) * 512],
                                start=True, stop=True)
            return last
        st("tensor", sqbmm, ["sq_fm", "eyerow8"], BT[0:4], tag=f"sqb{e8}")
        for q4 in range(4):
            st("vector", lambda e, q4=q4: e.tensor_copy(
                sqb_sb[:, q4 * 512:(q4 + 1) * 512], PB[q4][:, :]),
               [BT[q4], "sqb"], ["sqb"], tag=f"sqbev{e8}_{q4}")
        for dm8 in range(8):
            xs = xnTs[dm8 % 2]
            xst = f"xnTs{dm8 % 2}"
            dma("sync", xs[:, :], xnT_scr[:, dm8 * TW + 128: dm8 * TW + TW],
                [xst], [xst], tag=f"ldxnT{e8}_{dm8}")
            st("vector", lambda e, dm8=dm8, xs=xs: e.tensor_mul(
                xnT_e[:, dm8 * 2048:(dm8 + 1) * 2048], xs[:, :], sqb_sb[:, :]),
               [xst, "sqb", f"xnT_e{dm8}"], [f"xnT_e{dm8}"], tag=f"xe{e8}_{dm8}")
        for ch in range(we // 512):
            cw0 = int(OFFS[e8]) + ch * 512
            wb1 = w1buf[ch % 2]
            wt1 = f"w1buf{ch % 2}"
            dma("gpsimd", wb1[:, :].rearrange("p (s w) -> p s w", s=8),
                w1.rearrange("(s p) w -> p s w", p=128)[:, :, cw0:cw0 + 512],
                [wt1], [wt1], tag=f"ldw1_{e8}_{ch}")
            for wt4 in range(4):
                pbase = (wti % 2) * 4
                def mm1(e, wb1=wb1, wt4=wt4, pbase=pbase):
                    last = None
                    for q4 in range(4):
                        for dm8 in range(8):
                            last = e.matmul(
                                PB[pbase + q4][:, :],
                                wb1[:, dm8 * 512 + wt4 * 128: dm8 * 512 + (wt4 + 1) * 128],
                                xnT_e[:, dm8 * 2048 + q4 * 512: dm8 * 2048 + (q4 + 1) * 512],
                                start=(dm8 == 0), stop=(dm8 == 7))
                    return last
                st("tensor", mm1, [wt1] + [f"xnT_e{d}" for d in range(8)],
                   [BT[pbase + q] for q in range(4)], tag=f"mm1_{wti}")
                hb = Hsqb[wti % 2]
                htok = f"Hsqb{wti % 2}"
                for q4 in range(4):
                    st("scalar", lambda e, pbase=pbase, q4=q4: e.activation(
                        Hrelu[:, q4 * 512:(q4 + 1) * 512], PB[pbase + q4][:, :], AF.Relu),
                       [BT[pbase + q4], "Hrelu"], ["Hrelu"], tag=f"rel{wti}_{q4}")
                    st("vector", lambda e, q4=q4, hb=hb: e.tensor_mul(
                        hb[:, q4 * 512:(q4 + 1) * 512],
                        Hrelu[:, q4 * 512:(q4 + 1) * 512],
                        Hrelu[:, q4 * 512:(q4 + 1) * 512]),
                       ["Hrelu", htok], [htok], tag=f"hsq{wti}_{q4}")
                dma("sync", H_scr[wti * 128:(wti + 1) * 128, :], hb[:, :],
                    [htok], [f"Hrow{wti}"], tag=f"sth{wti}")
                wti += 1
    assert wti == 96

    p.barrier()

    bankc = 0
    for wb8 in range(12):
        wb2 = w2buf[wb8 % 2]
        wt2 = f"w2buf{wb8 % 2}"
        dma("gpsimd", wb2[:, :].rearrange("p (s d) -> p s d", s=8),
            w2[wb8 * 1024:(wb8 + 1) * 1024, :].rearrange("(s p) d -> p s d", p=128),
            [wt2], [wt2], tag=f"ldw2_{wb8}")
        for th in range(2):
            hb2 = hbuf4[(wb8 * 2 + th) % 2]
            ht2 = f"hbuf4_{(wb8 * 2 + th) % 2}"
            dma("sync", hb2[:, :].rearrange("p (s t) -> p s t", s=8),
                H_scr[wb8 * 1024:(wb8 + 1) * 1024,
                      th * 1024:(th + 1) * 1024].rearrange("(s p) t -> p s t", p=128),
                [ht2], [ht2], tag=f"ldh{wb8}_{th}")
            for tt in range(8):
                for hf in range(2):
                    bank = bankc % 8
                    bankc += 1
                    def mm2(e, hb2=hb2, wb2=wb2, tt=tt, hf=hf, bank=bank):
                        last = None
                        for sl8 in range(8):
                            last = e.matmul(
                                PB[bank][:, :],
                                hb2[:, sl8 * 1024 + tt * 128:
                                    sl8 * 1024 + (tt + 1) * 128],
                                wb2[:, sl8 * 1024 + hf * 512:
                                    sl8 * 1024 + (hf + 1) * 512],
                                start=(sl8 == 0), stop=(sl8 == 7))
                        return last
                    st("tensor", mm2, [ht2, wt2], [BT[bank]],
                       tag=f"mm2_{wb8}_{th}_{tt}_{hf}")
                    t16 = th * 8 + tt
                    ab = accs[t16]
                    atok = f"acc{t16}"
                    a0 = hf * 512
                    if wb8 == 0:
                        st("vector", lambda e, bank=bank, a0=a0, ab=ab:
                           e.tensor_copy(ab[:, a0:a0 + 512], PB[bank][:, :]),
                           [BT[bank]], [atok], tag=f"acccp{th}_{tt}_{hf}")
                    else:
                        st("vector", lambda e, bank=bank, a0=a0, ab=ab:
                           e.tensor_add(ab[:, a0:a0 + 512], ab[:, a0:a0 + 512],
                                        PB[bank][:, :]),
                           [BT[bank], atok], [atok],
                           tag=f"accad{wb8}_{th}_{tt}_{hf}")
    for t16 in range(16):
        gb = gout[t16 % 2]
        gtok = f"gout{t16 % 2}"
        st("scalar", lambda e, t16=t16, gb=gb: e.activation(
            gb[:, :], accs[t16][:, :], AF.Silu),
           [f"acc{t16}", gtok], [gtok], tag=f"gsilu{t16}")
        dma("sync", g_scr[t16 * 128:(t16 + 1) * 128, :], gb[:, :],
            [gtok], [f"gscr{t16}"], tag=f"stg{t16}")

    p.barrier()
    if DEBUG_PHASE == "p2":
        p.final_wait()
        return nc, p

    st("vector", lambda e: e.memset(S_run[:, :], 0.0), [], ["S_run"])
    cc_wait = [("cc", 1)]
    hb_bufs = [cvacc[0], xmdp]
    for r8 in range(8):
        bb = hb_bufs[r8 % 2]
        btok = f"hbb{r8 % 2}"
        p.step("sync" if r8 % 2 == 0 else "gpsimd",
               lambda e, r8=r8, bb=bb: e.dma_start(
                   out=bb[:, :], in_=ag_out.ap()[r8 * 128:(r8 + 1) * 128, :]),
               reads=[btok], writes=[btok], dma=True, tag=f"ldag{r8}",
               extra_waits=cc_wait if r8 < 2 else ())
        st("vector", lambda e, r8=r8, bb=bb: e.scalar_tensor_tensor(
            S_run[:, :], bb[:, :], sel8_sb[:, r8:r8 + 1], S_run[:, :],
            ALU.mult, ALU.add),
           [btok, "sel8", "S_run"], ["S_run"], tag=f"hinit{r8}")

    dma("gpsimd", WoutT_sb[:, :].rearrange("p (s j) -> p s j", s=16),
        WoutT.rearrange("(s p) j -> p s j", p=128), [], ["WoutT_raw"], tag="ldwo")
    for sl in range(16):
        st("vector", lambda e, sl=sl: e.tensor_scalar_mul(
            WoutT_sb[:, sl * 1024:(sl + 1) * 1024],
            WoutT_sb[:, sl * 1024:(sl + 1) * 1024], nw_sb[:, sl:sl + 1]),
           ["WoutT_raw", "nw", "WoutT"], ["WoutT"], tag=f"wosc{sl}")
    dma("scalar", Dp_bcast[:, :], dp_b, [], ["Dp_bcast"], tag="lddpb")

    def main_dmas(c):
        par = c % 2
        rc0 = c * 128
        cc0 = 128 + c * 128
        dma("sync", zc[par][:, :].rearrange("p (s t) -> p s t", s=16),
            z_scr[:, cc0:cc0 + 128].rearrange("(s p) t -> p s t", p=128),
            [f"zc{par}"], [f"zc{par}"], tag=f"ldz{c}")
        dma("sync", gbuf[par][:, :], g_scr[rc0:rc0 + 128, :], [f"gbuf{par}"],
            [f"gbuf{par}"], tag=f"ldg{c}")
        dma("sync", xres[par][:, :], xin[cc0:cc0 + 128, :], [f"xres{par}"],
            [f"xres{par}"], tag=f"ldxr{c}")
        dma("scalar", xm_tok[par][:, :], xmT_scr[rc0:rc0 + 128, :],
            [f"xm_tok{par}"], [f"xm_tok{par}"], tag=f"ldxmT{c}")

    def main_state(c):
        par = c % 2
        sbf = S_run_bf[par]
        sbtok = f"S_run_bf{par}"
        st("vector", lambda e, sbf=sbf: e.tensor_copy(sbf[:, :], S_run[:, :]),
           ["S_run", sbtok], [sbtok], tag=f"srbf{c}")
        chunk_state(c)

    def main_y(c):
        par = c % 2
        sbf = S_run_bf[par]
        sbtok = f"S_run_bf{par}"
        def ymm(e, c=c, par=par, sbf=sbf):
            last = None
            for q4 in range(4):
                e.matmul(PB[q4][:, :], Cs_all[:, c * 128:(c + 1) * 128],
                         sbf[:, q4 * 512:(q4 + 1) * 512], start=True, stop=False)
                last = e.matmul(PB[q4][:, :], Wsb_all[:, c * 128:(c + 1) * 128],
                                xm_tok[par][:, q4 * 512:(q4 + 1) * 512],
                                start=False, stop=True)
            return last
        st("tensor", ymm, [sbtok, f"Cs{c}", f"xm_tok{par}", f"Wsb{c}"],
           BT[0:4], tag=f"ymm{c}")
        st("vector", lambda e, par=par: e.tensor_mul(
            xmdp[:, :], xm_tok[par][:, :], Dp_bcast[:, :]),
           [f"xm_tok{par}", "Dp_bcast", "xmdp"], ["xmdp"], tag=f"xmdp{c}")
        yt = y_fm[par]
        yttok = f"y_fm{par}"
        for q4 in range(4):
            st("vector", lambda e, q4=q4, yt=yt: e.tensor_add(
                yt[:, q4 * 512:(q4 + 1) * 512], PB[q4][:, :],
                xmdp[:, q4 * 512:(q4 + 1) * 512]),
               [BT[q4], "xmdp", yttok], [yttok], tag=f"yev{c}_{q4}")
        st("scalar", lambda e, yt=yt, par=par: e.activation(
            xmdp[:, :], yt[:, :], AF.Square, accum_out=rstd_col[par][:, :]),
           [yttok, "xmdp", f"rstd_col{par}"], ["xmdp", f"rstd_col{par}"],
           tag=f"ysq{c}")
        st("scalar", lambda e, par=par: e.activation(
            rstd_col[par][:, :], rstd_col[par][:, :], AF.Ln,
            scale=1.0 / DI, bias=cbias[:, 0:1]),
           [f"rstd_col{par}", "cbias"], [f"rstd_col{par}"], tag=f"rln{c}")
        st("scalar", lambda e, par=par: e.activation(
            rstd_col[par][:, :], rstd_col[par][:, :], AF.Exp, scale=-0.5),
           [f"rstd_col{par}"], [f"rstd_col{par}"], tag=f"rexp{c}")

    def main_gate(c):
        par = c % 2
        mb = 6 + par
        yt = y_fm[par]
        yttok = f"y_fm{par}"
        for sl in range(16):
            off = 128 * (sl % 4)
            st("tensor", lambda e, sl=sl, mb=mb, off=off, yt=yt: e.transpose(
                PB[mb][:, off:off + 128], yt[:, sl * 128:(sl + 1) * 128],
                idf_sb[:, :]),
               [yttok, "idf"], [BT[mb]], tag=f"ytr{c}_{sl}")
            st("vector", lambda e, sl=sl, mb=mb, off=off, par=par:
               e.tensor_mul(
                   yg[par][:, sl * 128:(sl + 1) * 128], PB[mb][:, off:off + 128],
                   zc[par][:, sl * 128:(sl + 1) * 128]),
               [BT[mb], f"zc{par}", f"yg{par}"], [f"yg{par}"],
               tag=f"ygev{c}_{sl}")

    def main_out(c):
        par = c % 2
        rc0 = c * 128
        def omm(e, par=par):
            last = None
            for d16 in range(16):
                for hf in range(2):
                    last = e.matmul(
                        PB[4 + hf][:, :], yg[par][:, d16 * 128:(d16 + 1) * 128],
                        WoutT_sb[:, d16 * 1024 + hf * 512:
                                 d16 * 1024 + (hf + 1) * 512],
                        start=(d16 == 0), stop=(d16 == 15))
            return last
        st("tensor", omm, [f"yg{par}", "WoutT"], [BT[4], BT[5]], tag=f"omm{c}")
        ob = outbuf[par]
        otok = f"outbuf{par}"
        for hf in range(2):
            st("vector", lambda e, hf=hf, par=par: e.scalar_tensor_tensor(
                sgbuf[par][:, hf * 512:(hf + 1) * 512], PB[4 + hf][:, :],
                rstd_col[par][:, :],
                gbuf[par][:, hf * 512:(hf + 1) * 512], ALU.mult, ALU.mult),
               [BT[4 + hf], f"rstd_col{par}", f"gbuf{par}", f"yg{par}"],
               [f"yg{par}"], tag=f"sg{c}_{hf}")
        st("vector", lambda e, ob=ob, par=par: e.tensor_add(
            ob[:, :], sgbuf[par][:, :], xres[par][:, :]),
           [f"yg{par}", f"xres{par}", otok], [otok], tag=f"fin{c}")
        dma("sync", out[rc0:rc0 + 128, :], ob[:, :], [otok], [f"out{c}"],
            tag=f"stout{c}")

    for pr in range(NCH // 2):
        cs = (2 * pr, 2 * pr + 1)
        for c in cs:
            main_dmas(c)
        for c in cs:
            main_state(c)
        for c in cs:
            main_y(c)
        for c in cs:
            main_gate(c)
        for c in cs:
            main_out(c)

    p.final_wait()
    return nc, p


_nc_cache = {}


def _get_program():
    if "nc" not in _nc_cache:
        nc, p = build_program()
        with nc.Block() as block, \
             nc.semaphore("s_t") as s_t, nc.semaphore("s_v") as s_v, \
             nc.semaphore("s_s") as s_s, nc.semaphore("s_g") as s_g, \
             nc.semaphore("s_y") as s_y, nc.semaphore("s_cc") as s_cc:
            p.emit(nc, block, {"tensor": s_t, "vector": s_v, "scalar": s_s,
                               "gpsimd": s_g, "sync": s_y, "cc": s_cc})
        _nc_cache["nc"] = nc
    return _nc_cache["nc"]


def _host_inputs(L0_attn_x, L0_attn_in_proj_w, L0_attn_conv_w, L0_attn_A_log,
                 L0_attn_Dp, L0_attn_norm_w, L0_attn_out_proj_w,
                 L0_ffn_router_w, L0_ffn_w1, L0_ffn_w2):
    f32 = np.float32
    x = np.ascontiguousarray(np.asarray(L0_attn_x, dtype=f32))
    Win = np.ascontiguousarray(np.asarray(L0_attn_in_proj_w, dtype=f32))
    WinT = np.ascontiguousarray(Win.T)
    w1 = np.ascontiguousarray(np.asarray(L0_ffn_w1, dtype=f32))
    w2 = np.ascontiguousarray(np.asarray(L0_ffn_w2, dtype=f32))
    WoutT = np.ascontiguousarray(np.asarray(L0_attn_out_proj_w, dtype=f32).T)
    small9 = np.ascontiguousarray(
        np.concatenate([np.asarray(L0_ffn_router_w, f32).T,
                        Win[4352:4353, :].T], axis=1))
    A_log_row = np.ascontiguousarray(np.asarray(L0_attn_A_log, f32).reshape(1, S))
    cw = np.asarray(L0_attn_conv_w, f32).reshape(DI, 4)
    cw_col = np.ascontiguousarray(
        cw.reshape(16, 128, 4).transpose(1, 0, 2).reshape(128, 64))
    dp_col = np.ascontiguousarray(np.asarray(L0_attn_Dp, f32).reshape(16, 128).T)
    dp_b = np.ascontiguousarray(
        np.broadcast_to(np.asarray(L0_attn_Dp, f32).reshape(1, DI), (128, DI)))
    nw_col = np.ascontiguousarray(np.asarray(L0_attn_norm_w, f32).reshape(16, 128).T)

    tt = np.arange(128)
    M8 = ((tt[:, None] // BLK == tt[None, :] // BLK) & (tt[:, None] <= tt[None, :]))
    M8r = ((tt[:, None] // BLK == tt[None, :] // BLK) & (tt[:, None] > tt[None, :]))
    Mlow = tt[:, None] <= tt[None, :]
    Mrows = np.ascontiguousarray(
        np.concatenate([np.eye(128), M8, M8r, Mlow], axis=1).astype(f32))
    maskoff = np.ascontiguousarray(
        (tt[:, None] // BLK < tt[None, :] // BLK).astype(f32))
    bandmask = np.ascontiguousarray(np.concatenate(
        [((tt[None, :] - tt[:, None] == k) & (tt[:, None] // BLK == tt[None, :] // BLK))
         for k in range(BLK)], axis=1).astype(np.uint8))
    idf32 = np.eye(128, dtype=f32)
    eyerow8 = np.zeros((8, 8 * 128), f32)
    for e in range(8):
        eyerow8[e, e * 128:(e + 1) * 128] = 1.0

    shared = dict(WinT=WinT, w1=w1, w2=w2, WoutT=WoutT, small9=small9,
                  A_log_row=A_log_row, cw_col=cw_col, dp_col=dp_col,
                  dp_b=dp_b, nw_col=nw_col, Mrows=Mrows, maskoff=maskoff,
                  bandmask=bandmask, idf32=idf32, eyerow8=eyerow8)

    in_maps = []
    for core in range(8):
        b, h = core // 2, core % 2
        xin = np.zeros((TW, DM), f32)
        if h == 1:
            xin[125:128] = x[b, T - 3:T]
        xin[128:] = x[b, h * T:(h + 1) * T]
        sel = np.zeros((128, 8), f32)
        if h == 1:
            sel[:, 2 * b] = 1.0
        m = dict(shared)
        m["xin"] = np.ascontiguousarray(xin)
        m["sel8"] = sel
        in_maps.append(m)
    return in_maps


_LAST_RES = None


def kernel(**inputs):
    global _LAST_RES
    nc = _get_program()
    in_maps = _host_inputs(**inputs)
    res = run_bass_kernel_spmd(nc, in_maps, core_ids=list(range(8)))
    _LAST_RES = res
    outp = np.empty((4, 4096, 1024), np.float32)
    for core in range(8):
        b, h = core // 2, core % 2
        outp[b, h * T:(h + 1) * T] = np.asarray(res.results[core]["out"])
    return outp


# version 20
# speedup vs baseline: 1.0435x; 1.0276x over previous
import numpy as np
import ml_dtypes
import concourse.bass as bass
import concourse.mybir as mybir
from concourse.bass_utils import run_bass_kernel_spmd

F32 = mybir.dt.float32
BF16 = mybir.dt.bfloat16
U8 = mybir.dt.uint8
AF = mybir.ActivationFunctionType
ALU = mybir.AluOpType
AX = mybir.AxisListType

DM = 1024
DI = 2048
S = 128
Q = 128
NCH = 16
BLK = 8
T = 2048
TW = 2176
NT = TW // 128
J = 4353
JT = 34
WF = 12288
E = 8
WIDTHS = [1024] * 4 + [2048] * 4
OFFS = np.concatenate([[0], np.cumsum(WIDTHS)]).astype(int)
CLIP = 8.0
EPS = 1e-5
TBATCH = [(0, 512), (512, 512), (1024, 512), (1536, 512), (2048, 128)]
MM2G = [(0, 3), (3, 3), (6, 3), (9, 3), (12, 3), (15, 1)]

ENGINES = ("tensor", "vector", "scalar", "gpsimd", "sync")
STRICT_SAME_ENGINE = True
DEBUG_PHASE = None


class Prog:

    def __init__(self):
        self.steps = []
        self.cnt = {e: 0 for e in ENGINES}
        self.cnt["cc"] = 0
        self.lastw = {}
        self.readers = {}

    def step(self, engine, emit, reads=(), writes=(), dma=False, tag="",
             done_sem=None, extra_waits=()):
        deps = {}

        def dep(sem, val, asyn):
            if sem == engine and not asyn and not STRICT_SAME_ENGINE:
                return
            if deps.get(sem, 0) < val:
                deps[sem] = val

        for t in reads:
            w = self.lastw.get(t)
            if w:
                dep(*w)
        for t in writes:
            w = self.lastw.get(t)
            if w:
                dep(*w)
            for r in self.readers.get(t, ()):
                dep(*r)
        for sem, val in extra_waits:
            if deps.get(sem, 0) < val:
                deps[sem] = val
        if done_sem is not None:
            sem, amt = done_sem
        else:
            sem, amt = engine, (16 if dma else 1)
        self.cnt[sem] = self.cnt.get(sem, 0) + amt
        rec = (sem, self.cnt[sem], dma or done_sem is not None)
        for t in reads:
            self.readers.setdefault(t, []).append(rec)
        for t in writes:
            self.lastw[t] = rec
            self.readers[t] = []
        self.steps.append((engine, emit, deps, (sem, amt), tag))
        return rec

    def barrier(self):
        snapshot = dict(self.cnt)
        for e in ENGINES:
            waits = {s: v for s, v in snapshot.items() if v > 0 and s != e}
            self.steps.append((e, None, waits, (e, 0), "barrier"))
        self.lastw.clear()
        self.readers.clear()

    def final_wait(self, engine="gpsimd"):
        waits = {s: v for s, v in self.cnt.items() if v > 0}
        self.steps.append((engine, None, waits, (engine, 0), "final"))

    def emit(self, nc, block, sems):
        per = {e: [] for e in ENGINES}
        for stp in self.steps:
            per[stp[0]].append(stp)

        def mk(engine, steps):
            def body(eng):
                satisfied = {}
                for _, emitfn, deps, (sem, amt), tag in steps:
                    for s9, v in sorted(deps.items()):
                        if satisfied.get(s9, 0) >= v:
                            continue
                        eng.wait_ge(sems[s9], v)
                        satisfied[s9] = v
                    if emitfn is None:
                        continue
                    ins = emitfn(eng)
                    if amt > 0:
                        if ins is None:
                            raise RuntimeError(f"step {tag} returned no instruction")
                        ins.then_inc(sems[sem], amt)
            return body

        for e in ENGINES:
            if per[e]:
                getattr(block, e)(mk(e, per[e]))


def build_program():
    nc = bass.Bass("TRN2", target_bir_lowering=False)
    p = Prog()

    def inp(name, shape, dtype=F32):
        return nc.dram_tensor(name, shape, dtype, kind="ExternalInput").ap()

    xin = inp("xin", [TW, DM])
    dp_b = inp("dp_b", [128, DI])
    WinT = inp("WinT", [DM, J])
    w1 = inp("w1", [DM, WF])
    w2 = inp("w2", [WF, DM])
    WoutT = inp("WoutT", [DI, DM])
    small9 = inp("small9", [DM, 9])
    A_log_row = inp("A_log_row", [1, S])
    cw_col = inp("cw_col", [128, 64])
    dp_col = inp("dp_col", [128, 16])
    nw_col = inp("nw_col", [128, 16])
    sel8 = inp("sel8", [128, 8])
    Mrows = inp("Mrows", [128, 512])
    maskoff = inp("maskoff", [128, 128])
    bandmask = inp("bandmask", [128, 8 * 128], U8)
    idf32 = inp("idf32", [128, 128])
    eyerow8 = inp("eyerow8", [8, 8 * 128])

    out = nc.dram_tensor("out", [T, DM], F32, kind="ExternalOutput").ap()

    xnhi_scr = nc.dram_tensor("xnhi_scr", [TW, DM], BF16, kind="Internal").ap()
    xnlo_scr = nc.dram_tensor("xnlo_scr", [TW, DM], BF16, kind="Internal").ap()
    xnT_scr = nc.dram_tensor("xnT_scr", [128, 8 * TW], BF16, kind="Internal").ap()
    z_scr = nc.dram_tensor("z_scr", [DI, TW], BF16, kind="Internal").ap()
    xm_scr = nc.dram_tensor("xm_scr", [NCH * 128, 2048], BF16, kind="Internal").ap()
    xmT_scr = nc.dram_tensor("xmT_scr", [T, DI], BF16, kind="Internal").ap()
    H_scr = nc.dram_tensor("H_scr", [WF, T], BF16, kind="Internal").ap()
    g_scr = nc.dram_tensor("g_scr", [T, DM], BF16, kind="Internal").ap()
    ag_in = nc.dram_tensor("ag_in", [S, DI], F32, kind="Internal")
    ag_out = nc.dram_tensor("ag_out", [8 * S, DI], F32, kind="Internal")

    PB = [nc.alloc_psum_tensor(f"PB{i}", [128, 512], F32) for i in range(8)]
    BT = [f"B{i}" for i in range(8)]

    def sball(name, shape, dtype=F32):
        return nc.alloc_sbuf_tensor(name, shape, dtype)

    dts_tok = sball("dts_tok", [128, NT])
    combine = sball("combine", [128, NT * 8])
    A_row = sball("A_row", [1, 128])
    ones1 = sball("ones1", [1, 128])
    onesc = sball("onesc", [128, 1])
    ones1b = sball("ones1b", [1, 128], BF16)
    onescb = sball("onescb", [128, 1], BF16)
    rows_sb = [sball(f"rows_sb{i}", [1, 512]) for i in range(2)]
    rQrow = [sball(f"rQrow{i}", [1, 128]) for i in range(2)]
    rstdrow = [sball(f"rstdrow{i}", [1, 128]) for i in range(2)]
    ck_sb = [sball(f"ck_sb{i}", [1, 1024], BF16) for i in range(2)]
    l8row = [sball(f"l8row{i}", [1, 16]) for i in range(2)]
    Mrows_sb = sball("Mrows_sb", [128, 512])
    idb_sb = sball("idb_sb", [128, 128], BF16)
    maskoff_sb = sball("maskoff_sb", [128, 128])
    bandmask_sb = sball("bandmask_sb", [128, 8 * 128], U8)
    idf_sb = sball("idf_sb", [128, 128])
    eyerow8_sb = sball("eyerow8_sb", [8, 8 * 128])
    cw_sb = sball("cw_sb", [128, 64])
    dp_sb = sball("dp_sb", [128, 16])
    nw_sb = sball("nw_sb", [128, 16])
    sel8_sb = sball("sel8_sb", [128, 8])
    small_f = sball("small_f", [128, 8 * 9])
    small_hi = sball("small_hi", [128, 8 * 9], BF16)
    small_lo = sball("small_lo", [128, 8 * 9], BF16)
    smallhl = sball("smallhl", [128, 8 * 18], BF16)
    lnc = [sball(f"lnc{i}", [128, 16]) for i in range(2)]
    headf = sball("headf", [128, 16])
    sqc_t = sball("sqc_t", [128, 8])
    dccol = [sball(f"dccol{i}", [128, 1]) for i in range(2)]
    d8 = [sball(f"d8{i}", [128, 16]) for i in range(2)]
    srow_t = sball("srow_t", [1, 128])
    cbias = sball("cbias", [128, 2])
    rstd_col = [sball(f"rstd_col{i}", [128, 1]) for i in range(2)]

    arena_pp = 178 * 1024
    rem_pp = nc.sbuf_bytes_remaining // 128
    got = nc.bump_sbuf(arena_pp)
    assert got is not None, f"arena alloc failed rem={rem_pp}"
    abase = got[0]

    def at(name, off_pp, shape, dtype=F32):
        return nc.alloc_sbuf_tensor_at(name, shape, dtype, offset=abase + off_pp)

    K = 1024
    xnT_hi = at("xnT_hi", 0, [128, 8 * TW], BF16)
    w1buf = [at("w1buf0", 0, [128, 8 * 512], BF16),
             at("w1buf1", 8 * K, [128, 8 * 512], BF16)]
    Hrelu = at("Hrelu", 16 * K, [128, 2048], BF16)
    Hsqb = [at("Hsqb0", 20 * K, [128, 2048], BF16),
            at("Hsqb1", 24 * K, [128, 2048], BF16)]
    hbuf4 = [at("hbuf4_0", 0, [128, 8 * 1024], BF16),
             at("hbuf4_1", 16 * K, [128, 8 * 1024], BF16)]
    WoutT_sb = at("WoutT_sb", 0, [128, 16 * 1024], BF16)

    B0 = 34 * K
    xnT_lo = at("xnT_lo", B0, [128, 8 * TW], BF16)
    prebuf = [at("prebuf0", B0, [128, TW], BF16),
              at("prebuf1", B0 + 4352, [128, TW], BF16)]
    wbuf = [at("wbuf0", B0 + 8704, [128, 8 * 128], BF16),
            at("wbuf1", B0 + 10752, [128, 8 * 128], BF16)]
    zbuf = [at("zbuf0", B0 + 12800, [128, 512], BF16),
            at("zbuf1", B0 + 13824, [128, 512], BF16)]
    cvt0 = [at("cvt0a", B0 + 14848, [128, 512]),
            at("cvt0b", B0 + 16896, [128, 512])]
    cvt1 = [at("cvt1a", B0 + 19 * K, [128, 1024]),
            at("cvt1b", B0 + 23 * K, [128, 1024])]
    xnTs = [at("xnTs0", B0, [128, 2048], BF16),
            at("xnTs1", B0 + 4 * K, [128, 2048], BF16)]
    sq_fm = at("sq_fm", B0 + 8 * K, [8, 2048])
    sqb_sb = at("sqb_sb", B0 + 16 * K, [128, 2048], BF16)
    w2buf = [at("w2buf0", B0, [128, 8 * 1024], BF16),
             at("w2buf1", B0 + 16 * K, [128, 8 * 1024], BF16)]
    zc = [at("zc0", B0, [128, 2048], BF16),
          at("zc1", B0 + 4 * K, [128, 2048], BF16)]
    gbuf = [at("gbuf0", B0 + 8 * K, [128, 1024], BF16),
            at("gbuf1", B0 + 10 * K, [128, 1024], BF16)]
    xres = [at("xres0", B0 + 12 * K, [128, 1024]),
            at("xres1", B0 + 16 * K, [128, 1024])]
    outbuf = [at("outbuf0", B0 + 20 * K, [128, 1024]),
              at("outbuf1", B0 + 24 * K, [128, 1024])]
    dccol_all = at("dccol_all", B0 + 32 * K, [128, 16])

    C0 = 68 * K
    cvacc = [at("cvacc0", C0, [128, 2048]),
             at("cvacc1", C0 + 8 * K, [128, 2048])]
    cvt2 = [at("cvt2a", C0 + 16 * K, [128, 1024]),
            at("cvt2b", C0 + 20 * K, [128, 1024])]
    xmout = [at("xmout0", C0 + 24 * K, [128, 2048], BF16),
             at("xmout1", C0 + 28 * K, [128, 2048], BF16)]
    y_fm = [at("y_fm0", C0 + 32 * K, [128, 2048]),
            at("y_fm1", C0 + 40 * K, [128, 2048])]
    ysqh = [at("ysqh0", C0 + 16 * K, [128, 1024]),
            at("ysqh1", C0 + 20 * K, [128, 1024])]
    xmc = [at("xmc0", C0 + 24 * K, [128, 2048], BF16),
           at("xmc1", C0 + 28 * K, [128, 2048], BF16)]
    xm_tok = [at("xm_tok0", C0 + 48 * K, [128, 2048], BF16),
              at("xm_tok1", C0 + 52 * K, [128, 2048], BF16)]
    yg = [at("yg0", C0 + 0 * K, [128, 2048], BF16),
          at("yg1", C0 + 4 * K, [128, 2048], BF16)]
    sgbuf = [at("sgbuf0", C0 + 0 * K, [128, 1024]),
             at("sgbuf1", C0 + 4 * K, [128, 1024])]
    Zbuf = [at("Zbuf0", C0 + 56 * K, [128, 1024], BF16),
            at("Zbuf1", C0 + 58 * K, [128, 1024], BF16)]
    Bendtok_all = at("Bendtok_all", C0 + 8 * K, [128, 2048], BF16)
    ckb = [at("ckb0", C0 + 12 * K, [128, 1024], BF16),
           at("ckb1", C0 + 14 * K, [128, 1024], BF16)]
    Wsb_all = at("Wsb_all", C0 + 16 * K, [128, 2048], BF16)
    Cs_all = at("Cs_all", C0 + 20 * K, [128, 2048], BF16)
    Dp_bcast = at("Dp_bcast", C0 + 24 * K, [128, 2048])
    xmdp = at("xmdp", C0 + 56 * K, [128, 2048])
    xmst = [at("xmst0", C0 + 32 * K, [128, 2048], BF16),
            at("xmst1", C0 + 36 * K, [128, 2048], BF16)]

    D0 = 132 * K
    S_run = at("S_run", D0, [128, 2048])
    B_fm = at("B_fm", D0 + 8 * K, [128, TW], BF16)
    C_fm = at("C_fm", D0 + 8 * K + 4352, [128, TW], BF16)
    S_run_bf = [at("S_run_bf0", D0 + 8 * K, [128, 2048], BF16),
                at("S_run_bf1", D0 + 12 * K, [128, 2048], BF16)]
    DS = D0 + 17 * K
    xnT_e = at("xnT_e", DS, [128, 8 * 2048], BF16)
    gout = [at("gout0", DS, [128, 1024], BF16),
            at("gout1", DS + 2 * K, [128, 1024], BF16)]
    MS = []
    for par in range(2):
        base = DS + par * 7 * K
        MS.append(dict(
            dtsb=at(f"dtsb{par}", base, [128, 128]),
            ebufA=at(f"ebufA{par}", base + 512, [128, 128]),
            ebufB=at(f"ebufB{par}", base + 1024, [128, 128]),
            F1=at(f"F1_{par}", base + 1536, [128, 128], BF16),
            F2base=at(f"F2base{par}", base + 2048, [128, 128]),
            F2run=at(f"F2run{par}", base + 2560, [128, 128]),
            Btld=at(f"Btld{par}", base + 3072, [128, 128]),
            Bend=at(f"Bend{par}", base + 3584, [128, 128]),
            Bendtok=at(f"Bendtok{par}", base + 4096, [128, 128], BF16),
            Cs=at(f"Cs{par}", base + 4352, [128, 128], BF16),
            A_bar=at(f"A_bar{par}", base + 4608, [128, 128]),
            Ppp=[at(f"Ppp{par}_0", base + 5120, [128, 128]),
                 at(f"Ppp{par}_1", base + 5632, [128, 128])],
            zcw=at(f"zcw{par}", base + 6144, [128, 128]),
            Wsb=at(f"Wsb{par}", base + 6656, [128, 128], BF16),
        ))
    F2runB = [at("F2runB0", DS + 14 * K, [128, 128]),
              at("F2runB1", DS + 14 * K + 512, [128, 128])]
    F2hist = [at("F2hist0", DS + 16 * K, [128, 15 * 128], BF16),
              at("F2hist1", DS + 20 * K, [128, 15 * 128], BF16)]
    ACC_OFFS = [D0, D0 + 4 * K, D0 + 8 * K, D0 + 12 * K,
                C0, C0 + 4 * K, C0 + 12 * K, C0 + 24 * K,
                C0 + 28 * K, C0 + 32 * K, C0 + 36 * K, C0 + 40 * K,
                C0 + 44 * K, C0 + 48 * K, C0 + 52 * K, C0 + 56 * K]
    accs = [at(f"accs{t}", off, [128, 1024]) for t, off in enumerate(ACC_OFFS)]
    PS1 = DS
    xq = [at("xq0", PS1, [128, 1024]), at("xq1", PS1 + 4 * K, [128, 1024])]
    lnjunk = at("lnjunk", PS1 + 8 * K, [128, 1024])
    xnf = [at("xnf0", PS1 + 12 * K, [128, 1024]),
           at("xnf1", PS1 + 16 * K, [128, 1024])]
    xnhib = [at("xnhib0", PS1 + 20 * K, [128, 1024], BF16),
             at("xnhib1", PS1 + 22 * K, [128, 1024], BF16)]
    xnlof = [at("xnlof0", PS1 + 24 * K, [128, 1024]),
             at("xnlof1", PS1 + 28 * K, [128, 1024])]

    def dma(queue, o, i, reads, writes, tag="", **kw):
        return p.step(queue, lambda e, o=o, i=i, kw=kw: e.dma_start(out=o, in_=i, **kw),
                      reads=reads, writes=writes, dma=True, tag=tag)

    def dmat(queue, o, i, reads, writes, tag=""):
        return p.step(queue, lambda e, o=o, i=i: e.dma_start_transpose(o, i),
                      reads=reads, writes=writes, dma=True, tag=tag)

    def st(engine, fn, reads=(), writes=(), tag="", extra_waits=()):
        return p.step(engine, fn, reads=reads, writes=writes, tag=tag,
                      extra_waits=extra_waits)

    dma("sync", Mrows_sb[:, :], Mrows, [], ["Mrows"])
    dma("sync", maskoff_sb[:, :], maskoff, [], ["maskoff"])
    dma("sync", bandmask_sb[:, :], bandmask, [], ["bandmask"])
    dma("sync", idf_sb[:, :], idf32, [], ["idf"])
    st("vector", lambda e: e.tensor_copy(idb_sb[:, :], idf_sb[:, :]),
       ["idf"], ["idb"])
    dma("sync", eyerow8_sb[:, :], eyerow8, [], ["eyerow8"])
    dma("sync", cw_sb[:, :], cw_col, [], ["cw"])
    dma("sync", dp_sb[:, :], dp_col, [], ["dp"])
    dma("sync", nw_sb[:, :], nw_col, [], ["nw"])
    dma("sync", sel8_sb[:, :], sel8, [], ["sel8"])
    dma("sync", small_f[:, :].rearrange("p (s e) -> p s e", s=8),
        small9.rearrange("(s p) e -> p s e", p=128),
        [], ["small_f"], allow_slow_non_contiguous=True)
    dma("sync", srow_t[:, :], A_log_row, [], ["alog"])
    st("vector", lambda e: e.memset(cbias[:, 0:1], EPS), [], ["cbias"])
    st("vector", lambda e: e.memset(cbias[:, 1:2], 1.0), [], ["cbias"])
    st("vector", lambda e: e.memset(ones1[:, :], 1.0), [], ["ones1"])
    st("vector", lambda e: e.memset(onesc[:, :], 1.0), [], ["onesc"])
    st("vector", lambda e: e.memset(ones1b[:, :], 1.0), [], ["ones1b"])
    st("vector", lambda e: e.memset(onescb[:, :], 1.0), [], ["onescb"])
    for par in range(2):
        st("vector", lambda e, par=par: e.memset(Zbuf[par][:, :], 0.0),
           [], [f"Zbuf{par}"])
    st("scalar", lambda e: e.activation(srow_t[:, :], srow_t[:, :], AF.Exp),
       ["alog"], ["alog"])
    st("vector", lambda e: e.tensor_scalar_mul(A_row[:, :], srow_t[:, :], -1.0),
       ["alog"], ["A_row"])
    st("vector", lambda e: e.tensor_copy(small_hi[:, :], small_f[:, :]),
       ["small_f"], ["small_hi"])
    st("vector", lambda e: e.tensor_sub(small_lo[:, :], small_f[:, :], small_hi[:, :]),
       ["small_f", "small_hi"], ["small_lo"])
    for s8 in range(8):
        st("vector", lambda e, s8=s8: e.tensor_copy(
            smallhl[:, s8 * 18:s8 * 18 + 9], small_hi[:, s8 * 9:(s8 + 1) * 9]),
           ["small_hi"], ["smallhl"])
        st("vector", lambda e, s8=s8: e.tensor_copy(
            smallhl[:, s8 * 18 + 9:s8 * 18 + 18], small_lo[:, s8 * 9:(s8 + 1) * 9]),
           ["small_lo"], ["smallhl"])

    for i in range(NT):
        pr = i % 2
        xb = xq[pr]
        xtok = f"xq{pr}"
        ln = lnc[pr]
        ltok = f"lnc{pr}"
        dma("sync", xb[:, :], xin[i * 128:(i + 1) * 128, :], [], [xtok], tag=f"ldx{i}")
        st("scalar", lambda e, xb=xb, ln=ln: e.activation(
            lnjunk[:, :], xb[:, :], AF.Square, accum_out=ln[:, 0:1]),
           [xtok, "lnjunk"], ["lnjunk", ltok], tag=f"ssq{i}")
        st("scalar", lambda e, xb=xb, ln=ln: e.activation(
            lnjunk[:, :], xb[:, :], AF.Copy, accum_out=ln[:, 1:2]),
           [xtok, "lnjunk"], ["lnjunk", ltok], tag=f"msum{i}")
        st("vector", lambda e, ln=ln: e.tensor_scalar_mul(ln[:, 2:3], ln[:, 1:2], 1.0 / DM),
           [ltok], [ltok])
        st("vector", lambda e, ln=ln: e.tensor_scalar_mul(ln[:, 3:4], ln[:, 0:1], 1.0 / DM),
           [ltok], [ltok])
        st("vector", lambda e, ln=ln: e.scalar_tensor_tensor(
            ln[:, 4:5], ln[:, 2:3], ln[:, 2:3], ln[:, 3:4], ALU.mult, ALU.subtract),
           [ltok], [ltok])
        st("scalar", lambda e, ln=ln: e.activation(ln[:, 5:6], ln[:, 4:5], AF.Ln,
                                                   scale=-1.0, bias=cbias[:, 0:1]),
           [ltok, "cbias"], [ltok])
        st("scalar", lambda e, ln=ln: e.activation(ln[:, 6:7], ln[:, 5:6], AF.Exp,
                                                   scale=-0.5),
           [ltok], [ltok])
        st("vector", lambda e, xb=xb, ln=ln, pr=pr: e.tensor_scalar(
            xnhib[pr][:, :], xb[:, :], ln[:, 2:3], ln[:, 6:7], ALU.subtract, ALU.mult),
           [xtok, ltok], [f"xnhib{pr}"], tag=f"xnhi{i}")
        st("vector", lambda e, xb=xb, ln=ln, pr=pr: e.tensor_scalar(
            xnf[pr][:, :], xb[:, :], ln[:, 2:3], ln[:, 6:7], ALU.subtract, ALU.mult),
           [xtok, ltok], [f"xnf{pr}"], tag=f"xnf{i}")
        st("vector", lambda e, pr=pr: e.tensor_sub(
            xnlof[pr][:, :], xnf[pr][:, :], xnhib[pr][:, :]),
           [f"xnf{pr}", f"xnhib{pr}"], [f"xnlof{pr}"], tag=f"xnlo{i}")
        for sl in range(8):
            bank = sl % 4
            st("tensor", lambda e, pr=pr, sl=sl, bank=bank: e.transpose(
                PB[bank][:, 0:128], xnf[pr][:, sl * 128:(sl + 1) * 128], idf_sb[:, :]),
               [f"xnf{pr}", "idf"], [BT[bank]], tag=f"trh{i}_{sl}")
            st("vector", lambda e, i=i, sl=sl, bank=bank: e.tensor_copy(
                xnT_hi[:, sl * TW + i * 128: sl * TW + (i + 1) * 128],
                PB[bank][:, 0:128]),
               [BT[bank], f"xnT_hi_{sl}"], [f"xnT_hi_{sl}"], tag=f"trhev{i}_{sl}")
            st("tensor", lambda e, pr=pr, sl=sl, bank=bank: e.transpose(
                PB[bank][:, 128:256], xnlof[pr][:, sl * 128:(sl + 1) * 128],
                idf_sb[:, :]),
               [f"xnlof{pr}", "idf"], [BT[bank]], tag=f"trl{i}_{sl}")
            st("vector", lambda e, i=i, sl=sl, bank=bank: e.tensor_copy(
                xnT_lo[:, sl * TW + i * 128: sl * TW + (i + 1) * 128],
                PB[bank][:, 128:256]),
               [BT[bank], f"xnT_lo_{sl}"], [f"xnT_lo_{sl}"], tag=f"trlev{i}_{sl}")

    dma("sync", xnT_scr, xnT_hi[:, :],
        [f"xnT_hi_{sl}" for sl in range(8)], ["xnT_scr"], tag="xnTscr")

    for i in range(1, NT):
        def head_mm(e, i=i):
            last = None
            for dm8 in range(8):
                lh = xnT_hi[:, dm8 * TW + i * 128: dm8 * TW + (i + 1) * 128]
                last = e.matmul(PB[6][:, 0:18], lh, smallhl[:, dm8 * 18:(dm8 + 1) * 18],
                                start=(dm8 == 0), stop=False)
            for dm8 in range(8):
                ll = xnT_lo[:, dm8 * TW + i * 128: dm8 * TW + (i + 1) * 128]
                last = e.matmul(PB[6][:, 0:9], ll, small_hi[:, dm8 * 9:(dm8 + 1) * 9],
                                start=False, stop=(dm8 == 7))
            return last
        st("tensor", head_mm,
           [f"xnT_hi_{d}" for d in range(8)] + [f"xnT_lo_{d}" for d in range(8)]
           + ["smallhl", "small_hi"], ["B6"], tag=f"headmm{i}")
        st("vector", lambda e: e.tensor_copy(lnjunk[:, 0:18], PB[6][:, 0:18]),
           ["B6", "lnjunk"], ["lnjunk"], tag=f"headcp{i}")
        st("vector", lambda e: e.tensor_add(headf[:, 0:9], lnjunk[:, 0:9],
                                            lnjunk[:, 9:18]),
           ["lnjunk", "headf"], ["headf"], tag=f"headadd{i}")
        st("scalar", lambda e: e.activation(headf[:, 9:10], headf[:, 8:9], AF.Abs),
           ["headf"], ["headf"])
        st("scalar", lambda e: e.activation(headf[:, 10:11], headf[:, 9:10], AF.Exp,
                                            scale=-1.0),
           ["headf"], ["headf"])
        st("scalar", lambda e: e.activation(headf[:, 11:12], headf[:, 10:11], AF.Ln,
                                            bias=cbias[:, 1:2]),
           ["headf", "cbias"], ["headf"])
        st("vector", lambda e: e.tensor_scalar_max(headf[:, 12:13], headf[:, 8:9], 0.0),
           ["headf"], ["headf"])
        st("vector", lambda e, i=i: e.tensor_add(
            dts_tok[:, i:i + 1], headf[:, 11:12], headf[:, 12:13]),
           ["headf"], ["dts_tok"], tag=f"dts{i}")
        csl = combine[:, i * 8:(i + 1) * 8]
        st("scalar", lambda e, csl=csl: e.activation(csl, headf[:, 0:8], AF.Sigmoid),
           ["headf"], ["probs"], tag=f"sig{i}")
        st("vector", lambda e, csl=csl: e.reduce_max(headf[:, 13:14], csl, AX.X),
           ["probs", "headf"], ["headf"])
        st("vector", lambda e, csl=csl: e.tensor_scalar(
            lnjunk[:, 0:8], csl, headf[:, 13:14], None, ALU.is_equal),
           ["probs", "headf", "lnjunk"], ["lnjunk"])
        st("vector", lambda e, csl=csl: e.scalar_tensor_tensor(
            lnjunk[:, 8:16], lnjunk[:, 0:8], -1e30, csl, ALU.mult, ALU.add),
           ["lnjunk", "probs"], ["lnjunk"])
        st("vector", lambda e: e.reduce_max(headf[:, 14:15], lnjunk[:, 8:16], AX.X),
           ["lnjunk", "headf"], ["headf"])
        st("vector", lambda e: e.scalar_tensor_tensor(
            headf[:, 15:16], headf[:, 13:14], 1.0, headf[:, 14:15], ALU.mult, ALU.add),
           ["headf"], ["headf"])
        st("vector", lambda e: e.tensor_scalar_add(headf[:, 15:16], headf[:, 15:16], 1e-20),
           ["headf"], ["headf"])
        st("vector", lambda e: e.reciprocal(headf[:, 13:14], headf[:, 15:16]),
           ["headf"], ["headf"])
        st("vector", lambda e, csl=csl: e.tensor_scalar(
            lnjunk[:, 0:8], csl, headf[:, 14:15], None, ALU.is_ge),
           ["probs", "headf", "lnjunk"], ["lnjunk"])
        st("vector", lambda e, csl=csl: e.tensor_mul(csl, csl, lnjunk[:, 0:8]),
           ["probs", "lnjunk"], ["probs"])
        st("vector", lambda e, csl=csl: e.tensor_scalar_mul(csl, csl, headf[:, 13:14]),
           ["probs", "headf"], ["probs"], tag=f"comb{i}")

    p.barrier()
    if DEBUG_PHASE == "p1":
        p.final_wait()
        return nc, p

    def chunk_rows(c):
        par = c % 2
        mb = 6 + par
        mt = BT[mb]
        rows = rows_sb[par]
        rt = f"rows{par}"
        st("tensor", lambda e, c=c, mb=mb: e.matmul(
            PB[mb][0:1, 0:512], dts_tok[:, c + 1:c + 2], Mrows_sb[:, :],
            start=True, stop=True),
           ["dts_tok", "Mrows"], [mt], tag=f"rows{c}")
        st("vector", lambda e, rows=rows, mb=mb: e.tensor_copy(
            rows[:, :], PB[mb][0:1, 0:512]),
           [mt], [rt], tag=f"rowsev{c}")
        st("vector", lambda e, rows=rows, par=par: e.tensor_scalar(
            rQrow[par][:, :], rows[:, 384:512], -1.0, rows[0:1, 511:512],
            ALU.mult, ALU.add),
           [rt], [f"rQ{par}"], tag=f"rq{c}")
        st("vector", lambda e, rows=rows, par=par: e.tensor_copy(
            l8row[par][:, :], rows[0:1, 135:256:8]),
           [rt], [f"l8{par}"], tag=f"l8{c}")
        st("tensor", lambda e, rows=rows, mb=mb: e.matmul(
            PB[mb][:, 0:128], ones1[:, :], rows[0:1, 0:128], start=True, stop=True),
           [rt, "ones1"], [mt], tag=f"dtsbmm{c}")
        st("vector", lambda e, par=par, mb=mb: e.tensor_copy(
            MS[par]["dtsb"][:, :], PB[mb][:, 0:128]),
           [mt], [f"dtsb{par}"], tag=f"dtsbev{c}")
        st("tensor", lambda e, par=par, mb=mb: e.matmul(
            PB[mb][:, 128:144], A_row[:, :], l8row[par][:, :], start=True, stop=True),
           [f"l8{par}", "A_row"], [mt], tag=f"d8mm{c}")
        st("scalar", lambda e, par=par, mb=mb: e.activation(
            d8[par][:, :], PB[mb][:, 128:144], AF.Exp),
           [mt], [f"d8{par}"], tag=f"d8e{c}")
        st("tensor", lambda e, rows=rows, mb=mb: e.matmul(
            PB[mb][:, 144:145], A_row[:, :], rows[0:1, 511:512], start=True, stop=True),
           [rt, "A_row"], [mt], tag=f"dcmm{c}")
        st("scalar", lambda e, c=c, mb=mb: e.activation(
            dccol_all[:, c:c + 1], PB[mb][:, 144:145], AF.Exp),
           [mt], [f"dcc{c}"], tag=f"dce{c}")

    def chunk_exp(c, rowap, rowtok, outsb, outtok, slot, tag):
        ab = 4 + c % 2
        abt = BT[ab]
        off = 0 if slot == 0 else 128
        st("tensor", lambda e, rowap=rowap, ab=ab, off=off: e.matmul(
            PB[ab][:, off:off + 128], A_row[:, :], rowap, start=True, stop=True),
           [rowtok, "A_row"], [abt], tag=f"{tag}mm{c}")
        st("scalar", lambda e, outsb=outsb, ab=ab, off=off: e.activation(
            outsb[:, :], PB[ab][:, off:off + 128], AF.Exp),
           [abt], [outtok], tag=f"{tag}e{c}")

    def chunk_btld(c):
        par = c % 2
        cc0 = 128 + c * 128
        st("vector", lambda e, cc0=cc0, par=par: e.tensor_mul(
            MS[par]["Btld"][:, :], B_fm[:, cc0:cc0 + 128], MS[par]["dtsb"][:, :]),
           ["B_fm", f"dtsb{par}"], [f"Btld{par}"], tag=f"btld{c}")

    def chunk_bend(c):
        par = c % 2
        mb = 6 + par
        mt = BT[mb]
        m = MS[par]
        chunk_exp(c, rQrow[par][:, :], f"rQ{par}", m["ebufA"], f"ebufA{par}", 0, "erq")
        st("vector", lambda e, m=m, par=par: e.tensor_mul(
            m["Bend"][:, :], m["Btld"][:, :], m["ebufA"][:, :]),
           [f"Btld{par}", f"ebufA{par}"], [f"Bend{par}"], tag=f"bend{c}")
        st("tensor", lambda e, m=m, mb=mb: e.transpose(
            PB[mb][:, 384:512], m["Bend"][:, :], idf_sb[:, :]),
           [f"Bend{par}", "idf"], [mt], tag=f"bendtr{c}")
        st("vector", lambda e, c=c, mb=mb: e.tensor_copy(
            Bendtok_all[:, c * 128:(c + 1) * 128], PB[mb][:, 384:512]),
           [mt], [f"Bet{c}", "cvacc1"], tag=f"bendev{c}")

    def chunk_xmtok(c):
        par = c % 2
        xf = y_fm[par]
        xftok = f"y_fm{par}"
        st("vector", lambda e, par=par, xf=xf: e.tensor_copy(xf[:, :], xmc[par][:, :]),
           [f"xmc{par}", xftok], [xftok], tag=f"xmf32_{c}")
        for sl in range(16):
            bk = sl % 4
            off = 128 * par
            st("tensor", lambda e, sl=sl, bk=bk, off=off, xf=xf: e.transpose(
                PB[bk][:, off:off + 128], xf[:, sl * 128:(sl + 1) * 128], idf_sb[:, :]),
               [xftok, "idf"], [BT[bk]], tag=f"xtr{c}_{sl}")
            st("vector", lambda e, sl=sl, bk=bk, off=off, par=par: e.tensor_copy(
                xm_tok[par][:, sl * 128:(sl + 1) * 128], PB[bk][:, off:off + 128]),
               [BT[bk], f"xm_tok{par}"], [f"xm_tok{par}"], tag=f"xtrev{c}_{sl}")

    def chunk_state(c):
        par = c % 2
        def smm(e, c=c, par=par):
            last = None
            for q4 in range(4):
                last = e.matmul(PB[q4][:, :], Bendtok_all[:, c * 128:(c + 1) * 128],
                                xm_tok[par][:, q4 * 512:(q4 + 1) * 512],
                                start=True, stop=True)
            return last
        st("tensor", smm, [f"Bet{c}", f"xm_tok{par}"], BT[0:4], tag=f"smm{c}")
        for q4 in range(4):
            st("vector", lambda e, q4=q4, c=c: e.scalar_tensor_tensor(
                S_run[:, q4 * 512:(q4 + 1) * 512], S_run[:, q4 * 512:(q4 + 1) * 512],
                dccol_all[:, c:c + 1], PB[q4][:, :], ALU.mult, ALU.add),
               ["S_run", f"dcc{c}", BT[q4]], ["S_run"], tag=f"sup{c}_{q4}")

    def prep_exps(c):
        par = c % 2
        m = MS[par]
        cc0 = 128 + c * 128
        chunk_exp(c, rows_sb[par][0:1, 0:128], f"rows{par}", m["A_bar"],
                  f"A_bar{par}", 0, "abar")
        chunk_exp(c, rows_sb[par][0:1, 128:256], f"rows{par}", m["ebufB"],
                  f"ebufB{par}", 1, "elc")
        st("vector", lambda e, cc0=cc0, m=m, par=par: e.tensor_mul(
            m["F1"][:, :], C_fm[:, cc0:cc0 + 128], m["ebufB"][:, :]),
           ["C_fm", f"ebufB{par}", f"F1{par}"], [f"F1{par}"], tag=f"f1_{c}")
        chunk_exp(c, rows_sb[par][0:1, 256:384], f"rows{par}", m["ebufA"],
                  f"ebufA{par}", 0, "er8")
        st("vector", lambda e, m=m, par=par: e.tensor_mul(
            m["F2base"][:, :], m["Btld"][:, :], m["ebufA"][:, :]),
           [f"Btld{par}", f"ebufA{par}", f"F2base{par}"], [f"F2base{par}"],
           tag=f"f2b{c}")
        chunk_exp(c, rows_sb[par][0:1, 384:512], f"rows{par}", m["ebufB"],
                  f"ebufB{par}", 1, "eg")
        st("vector", lambda e, cc0=cc0, m=m, par=par, c=c: e.tensor_mul(
            Cs_all[:, c * 128:(c + 1) * 128], C_fm[:, cc0:cc0 + 128],
            m["ebufB"][:, :]),
           ["C_fm", f"ebufB{par}", f"Cs{c}"], [f"Cs{c}", "cvt21"],
           tag=f"csx{c}")

    def f2_chain(c):
        par = c % 2
        m = MS[par]
        h = F2hist[par]
        htok = f"F2hist{par}"
        st("vector", lambda e, h=h: e.memset(h[:, 8:128], 0.0),
           [htok], [htok], tag=f"f2z{c}")
        st("vector", lambda e, h=h, m=m: e.tensor_copy(
            h[:, 0:8], m["F2base"][:, 0:8]),
           [f"F2base{par}", htok], [htok], tag=f"f2c{c}_1")
        for I in range(2, 16):
            st("vector", lambda e, h=h, I=I, par=par:
               e.tensor_scalar_mul(h[:, (I - 1) * 128:I * 128],
                                   h[:, (I - 2) * 128:(I - 1) * 128],
                                   d8[par][:, I - 1:I]),
               [htok, f"d8{par}"], [htok], tag=f"f2r{c}_{I}")
            st("vector", lambda e, h=h, I=I, m=m: e.tensor_copy(
                h[:, (I - 1) * 128 + (I - 1) * 8:(I - 1) * 128 + I * 8],
                m["F2base"][:, (I - 1) * 8:I * 8]),
               [f"F2base{par}", htok], [htok], tag=f"f2c{c}_{I}")

    def w_fin(c):
        par = c % 2
        ab = 4 + par
        abt = BT[ab]
        m = MS[par]
        h = F2hist[par]
        htok = f"F2hist{par}"
        for I in range(1, 16):
            st("tensor", lambda e, I=I, h=h, m=m, ab=ab: e.matmul(
                PB[ab][:, 256 + I * 8:256 + (I + 1) * 8],
                h[:, (I - 1) * 128:I * 128],
                m["F1"][:, I * 8:(I + 1) * 8], start=True, stop=True),
               [htok, f"F1{par}"], [abt], tag=f"wmm{c}_{I}")
        st("vector", lambda e, c=c, ab=ab: e.tensor_mul(
            Wsb_all[:, c * 128:(c + 1) * 128], PB[ab][:, 256:384],
            maskoff_sb[:, :]),
           [abt, "maskoff", f"Wsb{c}"], [f"Wsb{c}", "cvt20"], tag=f"wev{c}")

    def band_chain(c):
        par = c % 2
        ab = 4 + par
        abt = BT[ab]
        m = MS[par]
        cc0 = 128 + c * 128
        st("vector", lambda e, cc0=cc0, m=m, par=par: e.tensor_mul(
            Zbuf[par][:, 0:128], C_fm[:, cc0:cc0 + 128], m["Btld"][:, :]),
           ["C_fm", f"Btld{par}", f"Zbuf{par}"], [f"Zbuf{par}"], tag=f"z0_{c}")
        for k9 in range(1, BLK):
            src = m["Ppp"][(k9 - 1) % 2]
            dst = m["Ppp"][k9 % 2]
            stok, dtok = f"Ppp{par}_{(k9 - 1) % 2}", f"Ppp{par}_{k9 % 2}"
            if k9 == 1:
                st("vector", lambda e, dst=dst, m=m: e.tensor_copy(
                    dst[:, :], m["A_bar"][:, :]),
                   [f"A_bar{par}", dtok], [dtok], tag=f"p1_{c}")
            else:
                st("vector", lambda e, src=src, dst=dst, m=m: e.tensor_mul(
                    dst[:, 1:128], m["A_bar"][:, 1:128], src[:, 0:127]),
                   [f"A_bar{par}", stok, dtok], [dtok], tag=f"p{k9}_{c}")
            st("vector", lambda e, dst=dst, cc0=cc0, m=m: e.tensor_mul(
                m["zcw"][:, :], C_fm[:, cc0:cc0 + 128], dst[:, :]),
               ["C_fm", dtok, f"zcw{par}"], [f"zcw{par}"], tag=f"zcw{k9}_{c}")
            st("vector", lambda e, k9=k9, m=m, par=par: e.tensor_mul(
                Zbuf[par][:, k9 * 128 + k9:(k9 + 1) * 128],
                m["zcw"][:, k9:128], m["Btld"][:, 0:128 - k9]),
               [f"zcw{par}", f"Btld{par}", f"Zbuf{par}"], [f"Zbuf{par}"],
               tag=f"z{k9}_{c}")

    def band_fin(c):
        par = c % 2
        ab = 4 + par
        abt = BT[ab]
        for h2 in range(2):
            st("tensor", lambda e, h2=h2, par=par, ab=ab: e.matmul(
                PB[ab][0:1, 0:512], onescb[:, :],
                Zbuf[par][:, h2 * 512:(h2 + 1) * 512], start=True, stop=True),
               [f"Zbuf{par}", "onescb"], [abt], tag=f"ckmm{c}_{h2}")
            st("vector", lambda e, h2=h2, par=par, ab=ab: e.tensor_copy(
                ck_sb[par][:, h2 * 512:(h2 + 1) * 512], PB[ab][0:1, 0:512]),
               [abt, f"ck{par}"], [f"ck{par}"], tag=f"ckev{c}_{h2}")
        for h2 in range(2):
            st("tensor", lambda e, h2=h2, par=par, ab=ab: e.matmul(
                PB[ab][:, 0:512], ones1b[:, :],
                ck_sb[par][0:1, h2 * 512:(h2 + 1) * 512], start=True, stop=True),
               [f"ck{par}", "ones1b"], [abt], tag=f"ckbmm{c}_{h2}")
            st("vector", lambda e, h2=h2, par=par, ab=ab: e.tensor_copy(
                ckb[par][:, h2 * 512:(h2 + 1) * 512], PB[ab][:, 0:512]),
               [abt, f"ckb{par}"], [f"ckb{par}", "cvacc1"],
               tag=f"ckbev{c}_{h2}")
        for k9 in range(BLK):
            st("vector", lambda e, k9=k9, par=par, c=c: e.copy_predicated(
                Wsb_all[:, c * 128:(c + 1) * 128],
                bandmask_sb[:, k9 * 128:(k9 + 1) * 128],
                ckb[par][:, k9 * 128:(k9 + 1) * 128]),
               ["bandmask", f"ckb{par}", f"Wsb{c}"], [f"Wsb{c}"],
               tag=f"band{c}_{k9}")

    st("vector", lambda e: e.memset(S_run[:, :], 0.0), [], ["S_run"])
    for par in range(2):
        st("vector", lambda e, par=par: e.memset(MS[par]["F2run"][:, :], 0.0),
           [f"F2run{par}"], [f"F2run{par}"], tag=f"msf2_{par}")
        st("vector", lambda e, par=par: e.memset(F2runB[par][:, :], 0.0),
           [f"F2runB{par}"], [f"F2runB{par}"], tag=f"msf2b_{par}")
        st("vector", lambda e, par=par: e.memset(MS[par]["Ppp"][0][:, :], 1.0),
           [f"Ppp{par}_0"], [f"Ppp{par}_0"], tag=f"msp0_{par}")
        st("vector", lambda e, par=par: e.memset(MS[par]["Ppp"][1][:, :], 1.0),
           [f"Ppp{par}_1"], [f"Ppp{par}_1"], tag=f"msp1_{par}")

    def evac_z(jt, tb_i, t0, tn, bank):
        pz = tb_i % 2
        zb = zbuf[pz]
        ztok = f"zbuf{pz}"
        ct = cvt0[pz]
        ctok = f"cvt0{pz}"
        st("scalar", lambda e, tn=tn, bank=bank, ct=ct: e.activation(
            ct[:, 0:tn], PB[bank][:, 0:tn], AF.Silu),
           [BT[bank], ctok], [ctok], tag=f"zsilu{jt}_{tb_i}")
        st("vector", lambda e, tn=tn, zb=zb, ct=ct: e.tensor_scalar_min(
            zb[:, 0:tn], ct[:, 0:tn], CLIP),
           [ctok], [ztok])
        dma("sync", z_scr[jt * 128:(jt + 1) * 128, t0:t0 + tn], zb[:, 0:tn],
            [ztok], [f"zscr{jt}_{tb_i}"], tag=f"zst{jt}_{tb_i}")

    def evac_xm(jt, tb_i, t0, tn, bank):
        sl = jt - 16
        pb = prebuf[sl % 2]
        ptok = f"prebuf{sl % 2}"
        ct = cvt1[tb_i % 2]
        ctok = f"cvt1{tb_i % 2}"
        st("scalar", lambda e, tn=tn, bank=bank, ct=ct: e.activation(
            ct[:, 0:tn], PB[bank][:, 0:tn], AF.Silu),
           [BT[bank], ctok], [ctok], tag=f"xmsilu{jt}_{tb_i}")
        st("vector", lambda e, t0=t0, tn=tn, pb=pb, ct=ct: e.tensor_scalar_min(
            pb[:, t0:t0 + tn], ct[:, 0:tn], CLIP),
           [ctok, ptok], [ptok])

    def evac_bc(jt, tb_i, t0, tn, bank):
        dst = B_fm if jt == 32 else C_fm
        tok = "B_fm" if jt == 32 else "C_fm"
        st("vector", lambda e, dst=dst, t0=t0, tn=tn, bank=bank: e.tensor_copy(
            dst[:, t0:t0 + tn], PB[bank][:, 0:tn]),
           [BT[bank]], [tok], tag=f"bc{jt}_{tb_i}")

    def conv_slot(sl):
        pb = prebuf[sl % 2]
        ptok = f"prebuf{sl % 2}"
        acc = cvacc[sl % 2]
        atok = f"cvacc{sl % 2}"
        xo = xmout[sl % 2]
        xotok = f"xmout{sl % 2}"
        st("vector", lambda e, sl=sl, pb=pb, acc=acc: e.tensor_scalar(
            acc[:, :], pb[:, 125:125 + T], cw_sb[:, sl * 4:sl * 4 + 1], None, ALU.mult),
           [ptok, "cw", atok], [atok], tag=f"cv0_{sl}")
        for k4 in range(1, 4):
            st("vector", lambda e, sl=sl, k4=k4, pb=pb, acc=acc: e.scalar_tensor_tensor(
                acc[:, :], pb[:, 125 + k4:125 + k4 + T],
                cw_sb[:, sl * 4 + k4:sl * 4 + k4 + 1], acc[:, :], ALU.mult, ALU.add),
               [ptok, "cw", atok], [atok], tag=f"cv{k4}_{sl}")
        for hh in range(2):
            cv = cvt2[hh]
            cvtok = f"cvt2{hh}"
            st("scalar", lambda e, acc=acc, hh=hh, cv=cv: e.activation(
                cv[:, :], acc[:, hh * 1024:(hh + 1) * 1024], AF.Silu),
               [atok, cvtok], [cvtok], tag=f"cs{hh}_{sl}")
        xs = xmst[sl % 2]
        xstok = f"xmst{sl % 2}"
        for cc in range(NCH):
            bk = 4 + cc % 4
            cv = cvt2[cc // 8]
            cvtok = f"cvt2{cc // 8}"
            st("tensor", lambda e, cc=cc, bk=bk, cv=cv: e.transpose(
                PB[bk][:, 0:128], cv[:, (cc % 8) * 128:(cc % 8 + 1) * 128],
                idf_sb[:, :]),
               [cvtok, "idf"], [BT[bk]], tag=f"cvtr{sl}_{cc}")
            st("vector", lambda e, cc=cc, bk=bk, xs=xs: e.tensor_scalar_min(
                xs[:, cc * 128:(cc + 1) * 128], PB[bk][:, 0:128], CLIP),
               [BT[bk], xstok], [xstok], tag=f"cvtrev{sl}_{cc}")
        dma("sync",
            xmT_scr[:, sl * 128:(sl + 1) * 128].rearrange("(c t) f -> t c f",
                                                          t=128),
            xs[:, :].rearrange("p (c f) -> p c f", c=16),
            [xstok], [f"xmT_{sl}"], tag=f"stxmT{sl}")

    ipq = 0
    for jti, jt in enumerate(list(range(16, 34)) + list(range(16))):
        wb = wbuf[jti % 2]
        wtok = f"wbuf{jti % 2}"
        dma("gpsimd", wb[:, :].rearrange("p (s j) -> p s j", s=8),
            WinT.rearrange("(s p) j -> p s j", p=128)[:, :, jt * 128:(jt + 1) * 128],
            [], [wtok], tag=f"ldw{jt}")
        for tb_i, (t0, tn) in enumerate(TBATCH):
            bank = ipq % 4
            ipq += 1
            def mmj(e, wb=wb, t0=t0, tn=tn, bank=bank):
                last = None
                for dm8 in range(8):
                    last = e.matmul(
                        PB[bank][:, 0:tn],
                        wb[:, dm8 * 128:(dm8 + 1) * 128],
                        xnT_hi[:, dm8 * TW + t0: dm8 * TW + t0 + tn],
                        start=(dm8 == 0), stop=(dm8 == 7))
                return last
            st("tensor", mmj, [wtok], [BT[bank]], tag=f"ip{jt}_{tb_i}")
            if jt < 16:
                evac_z(jt, tb_i, t0, tn, bank)
            elif jt < 32:
                evac_xm(jt, tb_i, t0, tn, bank)
            else:
                evac_bc(jt, tb_i, t0, tn, bank)
        if 16 <= jt < 32:
            conv_slot(jt - 16)
        def stage_B(c):
            chunk_rows(c)
            chunk_btld(c)
            prep_exps(c)
            f2_chain(c)
            band_chain(c)

        def stage_C(c):
            w_fin(c)
            band_fin(c)
            chunk_bend(c)

        def stage_D(c):
            par = c % 2
            dma("gpsimd", xm_tok[par][:, :], xmT_scr[c * 128:(c + 1) * 128, :],
                [f"xm_tok{par}"] + [f"xmT_{sl}" for sl in range(16)],
                [f"xm_tok{par}"], tag=f"ldxmT3a_{c}")
            chunk_state(c)

        if 0 <= jti - 18 < NCH:
            stage_B(jti - 18)
        if 0 <= jti - 19 < NCH:
            stage_C(jti - 19)
        if 0 <= jti - 20 < NCH:
            stage_D(jti - 20)

    stage_C(15)
    stage_D(14)
    stage_D(15)

    dma("sync", ag_in.ap(), S_run[:, :], ["S_run"], ["ag_in"], tag="agst")
    p.step("gpsimd", lambda e: e.collective_compute(
        "AllGather", ALU.bypass, replica_groups=[list(range(8))],
        ins=[ag_in.ap().opt()], outs=[ag_out.ap().opt()]),
        reads=["ag_in"], writes=["ag_out"], done_sem=("cc", 1), tag="allgather")

    p.barrier()
    if DEBUG_PHASE == "p3a":
        p.final_wait()
        return nc, p

    for i in range(16):
        st("scalar", lambda e, i=i: e.activation(
            sqc_t[:, :], combine[:, (i + 1) * 8:(i + 2) * 8], AF.Sqrt),
           ["combine", "sqc"], ["sqc"], tag=f"sqrt{i}")
        st("tensor", lambda e: e.transpose(PB[6][0:8, 0:128], sqc_t[:, :], idf_sb[:, :]),
           ["sqc", "idf"], ["B6"], tag=f"sqtr{i}")
        st("vector", lambda e, i=i: e.tensor_copy(
            sq_fm[0:8, i * 128:(i + 1) * 128], PB[6][0:8, 0:128]),
           ["B6"], ["sq_fm"], tag=f"sqev{i}")

    wti = 0
    for e8 in range(E):
        we = WIDTHS[e8]
        def sqbmm(e, e8=e8):
            last = None
            for q4 in range(4):
                last = e.matmul(PB[q4][:, :], eyerow8_sb[:, e8 * 128:(e8 + 1) * 128],
                                sq_fm[0:8, q4 * 512:(q4 + 1) * 512],
                                start=True, stop=True)
            return last
        st("tensor", sqbmm, ["sq_fm", "eyerow8"], BT[0:4], tag=f"sqb{e8}")
        for q4 in range(4):
            st("vector", lambda e, q4=q4: e.tensor_copy(
                sqb_sb[:, q4 * 512:(q4 + 1) * 512], PB[q4][:, :]),
               [BT[q4], "sqb"], ["sqb"], tag=f"sqbev{e8}_{q4}")
        for dm8 in range(8):
            xs = xnTs[dm8 % 2]
            xst = f"xnTs{dm8 % 2}"
            dma("sync", xs[:, :], xnT_scr[:, dm8 * TW + 128: dm8 * TW + TW],
                [xst], [xst], tag=f"ldxnT{e8}_{dm8}")
            st("vector", lambda e, dm8=dm8, xs=xs: e.tensor_mul(
                xnT_e[:, dm8 * 2048:(dm8 + 1) * 2048], xs[:, :], sqb_sb[:, :]),
               [xst, "sqb", f"xnT_e{dm8}"], [f"xnT_e{dm8}"], tag=f"xe{e8}_{dm8}")
        for ch in range(we // 512):
            cw0 = int(OFFS[e8]) + ch * 512
            wb1 = w1buf[ch % 2]
            wt1 = f"w1buf{ch % 2}"
            dma("gpsimd", wb1[:, :].rearrange("p (s w) -> p s w", s=8),
                w1.rearrange("(s p) w -> p s w", p=128)[:, :, cw0:cw0 + 512],
                [wt1], [wt1], tag=f"ldw1_{e8}_{ch}")
            for wt4 in range(4):
                pbase = (wti % 2) * 4
                def mm1(e, wb1=wb1, wt4=wt4, pbase=pbase):
                    last = None
                    for q4 in range(4):
                        for dm8 in range(8):
                            last = e.matmul(
                                PB[pbase + q4][:, :],
                                wb1[:, dm8 * 512 + wt4 * 128: dm8 * 512 + (wt4 + 1) * 128],
                                xnT_e[:, dm8 * 2048 + q4 * 512: dm8 * 2048 + (q4 + 1) * 512],
                                start=(dm8 == 0), stop=(dm8 == 7))
                    return last
                st("tensor", mm1, [wt1] + [f"xnT_e{d}" for d in range(8)],
                   [BT[pbase + q] for q in range(4)], tag=f"mm1_{wti}")
                hb = Hsqb[wti % 2]
                htok = f"Hsqb{wti % 2}"
                for q4 in range(4):
                    st("scalar", lambda e, pbase=pbase, q4=q4: e.activation(
                        Hrelu[:, q4 * 512:(q4 + 1) * 512], PB[pbase + q4][:, :], AF.Relu),
                       [BT[pbase + q4], "Hrelu"], ["Hrelu"], tag=f"rel{wti}_{q4}")
                    st("vector", lambda e, q4=q4, hb=hb: e.tensor_mul(
                        hb[:, q4 * 512:(q4 + 1) * 512],
                        Hrelu[:, q4 * 512:(q4 + 1) * 512],
                        Hrelu[:, q4 * 512:(q4 + 1) * 512]),
                       ["Hrelu", htok], [htok], tag=f"hsq{wti}_{q4}")
                dma("sync", H_scr[wti * 128:(wti + 1) * 128, :], hb[:, :],
                    [htok], [f"Hrow{wti}"], tag=f"sth{wti}")
                wti += 1
    assert wti == 96

    p.barrier()

    bankc = 0
    for wb8 in range(12):
        wb2 = w2buf[wb8 % 2]
        wt2 = f"w2buf{wb8 % 2}"
        dma("gpsimd", wb2[:, :].rearrange("p (s d) -> p s d", s=8),
            w2[wb8 * 1024:(wb8 + 1) * 1024, :].rearrange("(s p) d -> p s d", p=128),
            [wt2], [wt2], tag=f"ldw2_{wb8}")
        for th in range(2):
            hb2 = hbuf4[(wb8 * 2 + th) % 2]
            ht2 = f"hbuf4_{(wb8 * 2 + th) % 2}"
            dma("sync", hb2[:, :].rearrange("p (s t) -> p s t", s=8),
                H_scr[wb8 * 1024:(wb8 + 1) * 1024,
                      th * 1024:(th + 1) * 1024].rearrange("(s p) t -> p s t", p=128),
                [ht2], [ht2], tag=f"ldh{wb8}_{th}")
            for tt in range(8):
                for hf in range(2):
                    bank = bankc % 8
                    bankc += 1
                    def mm2(e, hb2=hb2, wb2=wb2, tt=tt, hf=hf, bank=bank):
                        last = None
                        for sl8 in range(8):
                            last = e.matmul(
                                PB[bank][:, :],
                                hb2[:, sl8 * 1024 + tt * 128:
                                    sl8 * 1024 + (tt + 1) * 128],
                                wb2[:, sl8 * 1024 + hf * 512:
                                    sl8 * 1024 + (hf + 1) * 512],
                                start=(sl8 == 0), stop=(sl8 == 7))
                        return last
                    st("tensor", mm2, [ht2, wt2], [BT[bank]],
                       tag=f"mm2_{wb8}_{th}_{tt}_{hf}")
                    t16 = th * 8 + tt
                    ab = accs[t16]
                    atok = f"acc{t16}"
                    a0 = hf * 512
                    if wb8 == 0:
                        st("vector", lambda e, bank=bank, a0=a0, ab=ab:
                           e.tensor_copy(ab[:, a0:a0 + 512], PB[bank][:, :]),
                           [BT[bank]], [atok], tag=f"acccp{th}_{tt}_{hf}")
                    else:
                        st("vector", lambda e, bank=bank, a0=a0, ab=ab:
                           e.tensor_add(ab[:, a0:a0 + 512], ab[:, a0:a0 + 512],
                                        PB[bank][:, :]),
                           [BT[bank], atok], [atok],
                           tag=f"accad{wb8}_{th}_{tt}_{hf}")
    for t16 in range(16):
        gb = gout[t16 % 2]
        gtok = f"gout{t16 % 2}"
        st("scalar", lambda e, t16=t16, gb=gb: e.activation(
            gb[:, :], accs[t16][:, :], AF.Silu),
           [f"acc{t16}", gtok], [gtok], tag=f"gsilu{t16}")
        dma("sync", g_scr[t16 * 128:(t16 + 1) * 128, :], gb[:, :],
            [gtok], [f"gscr{t16}"], tag=f"stg{t16}")

    p.barrier()
    if DEBUG_PHASE == "p2":
        p.final_wait()
        return nc, p

    st("vector", lambda e: e.memset(S_run[:, :], 0.0), [], ["S_run"])
    cc_wait = [("cc", 1)]
    hb_bufs = [cvacc[0], xmdp]
    for r8 in range(8):
        bb = hb_bufs[r8 % 2]
        btok = f"hbb{r8 % 2}"
        p.step("sync" if r8 % 2 == 0 else "gpsimd",
               lambda e, r8=r8, bb=bb: e.dma_start(
                   out=bb[:, :], in_=ag_out.ap()[r8 * 128:(r8 + 1) * 128, :]),
               reads=[btok], writes=[btok], dma=True, tag=f"ldag{r8}",
               extra_waits=cc_wait if r8 < 2 else ())
        st("vector", lambda e, r8=r8, bb=bb: e.scalar_tensor_tensor(
            S_run[:, :], bb[:, :], sel8_sb[:, r8:r8 + 1], S_run[:, :],
            ALU.mult, ALU.add),
           [btok, "sel8", "S_run"], ["S_run"], tag=f"hinit{r8}")

    dma("gpsimd", WoutT_sb[:, :].rearrange("p (s j) -> p s j", s=16),
        WoutT.rearrange("(s p) j -> p s j", p=128), [], ["WoutT_raw"], tag="ldwo")
    for sl in range(16):
        st("vector", lambda e, sl=sl: e.tensor_scalar_mul(
            WoutT_sb[:, sl * 1024:(sl + 1) * 1024],
            WoutT_sb[:, sl * 1024:(sl + 1) * 1024], nw_sb[:, sl:sl + 1]),
           ["WoutT_raw", "nw", "WoutT"], ["WoutT"], tag=f"wosc{sl}")
    dma("scalar", Dp_bcast[:, :], dp_b, [], ["Dp_bcast"], tag="lddpb")

    def main_dmas(c):
        par = c % 2
        rc0 = c * 128
        cc0 = 128 + c * 128
        dma("sync", zc[par][:, :].rearrange("p (s t) -> p s t", s=16),
            z_scr[:, cc0:cc0 + 128].rearrange("(s p) t -> p s t", p=128),
            [f"zc{par}"], [f"zc{par}"], tag=f"ldz{c}")
        dma("sync", gbuf[par][:, :], g_scr[rc0:rc0 + 128, :], [f"gbuf{par}"],
            [f"gbuf{par}"], tag=f"ldg{c}")
        dma("sync", xres[par][:, :], xin[cc0:cc0 + 128, :], [f"xres{par}"],
            [f"xres{par}"], tag=f"ldxr{c}")
        dma("scalar", xm_tok[par][:, :], xmT_scr[rc0:rc0 + 128, :],
            [f"xm_tok{par}"], [f"xm_tok{par}"], tag=f"ldxmT{c}")

    def main_state(c):
        par = c % 2
        sbf = S_run_bf[par]
        sbtok = f"S_run_bf{par}"
        st("vector", lambda e, sbf=sbf: e.tensor_copy(sbf[:, :], S_run[:, :]),
           ["S_run", sbtok], [sbtok], tag=f"srbf{c}")
        chunk_state(c)

    def main_y(c):
        par = c % 2
        sbf = S_run_bf[par]
        sbtok = f"S_run_bf{par}"
        def ymm(e, c=c, par=par, sbf=sbf):
            last = None
            for q4 in range(4):
                e.matmul(PB[q4][:, :], Cs_all[:, c * 128:(c + 1) * 128],
                         sbf[:, q4 * 512:(q4 + 1) * 512], start=True, stop=False)
                last = e.matmul(PB[q4][:, :], Wsb_all[:, c * 128:(c + 1) * 128],
                                xm_tok[par][:, q4 * 512:(q4 + 1) * 512],
                                start=False, stop=True)
            return last
        st("tensor", ymm, [sbtok, f"Cs{c}", f"xm_tok{par}", f"Wsb{c}"],
           BT[0:4], tag=f"ymm{c}")
        st("vector", lambda e, par=par: e.tensor_mul(
            xmdp[:, :], xm_tok[par][:, :], Dp_bcast[:, :]),
           [f"xm_tok{par}", "Dp_bcast", "xmdp"], ["xmdp"], tag=f"xmdp{c}")
        yt = y_fm[par]
        yttok = f"y_fm{par}"
        for q4 in range(4):
            st("vector", lambda e, q4=q4, yt=yt: e.tensor_add(
                yt[:, q4 * 512:(q4 + 1) * 512], PB[q4][:, :],
                xmdp[:, q4 * 512:(q4 + 1) * 512]),
               [BT[q4], "xmdp", yttok], [yttok], tag=f"yev{c}_{q4}")
        st("scalar", lambda e, yt=yt, par=par: e.activation(
            xmdp[:, :], yt[:, :], AF.Square, accum_out=rstd_col[par][:, :]),
           [yttok, "xmdp", f"rstd_col{par}"], ["xmdp", f"rstd_col{par}"],
           tag=f"ysq{c}")
        st("scalar", lambda e, par=par: e.activation(
            rstd_col[par][:, :], rstd_col[par][:, :], AF.Ln,
            scale=1.0 / DI, bias=cbias[:, 0:1]),
           [f"rstd_col{par}", "cbias"], [f"rstd_col{par}"], tag=f"rln{c}")
        st("scalar", lambda e, par=par: e.activation(
            rstd_col[par][:, :], rstd_col[par][:, :], AF.Exp, scale=-0.5),
           [f"rstd_col{par}"], [f"rstd_col{par}"], tag=f"rexp{c}")

    def main_gate(c):
        par = c % 2
        mb = 6 + par
        yt = y_fm[par]
        yttok = f"y_fm{par}"
        for sl in range(16):
            off = 128 * (sl % 4)
            st("tensor", lambda e, sl=sl, mb=mb, off=off, yt=yt: e.transpose(
                PB[mb][:, off:off + 128], yt[:, sl * 128:(sl + 1) * 128],
                idf_sb[:, :]),
               [yttok, "idf"], [BT[mb]], tag=f"ytr{c}_{sl}")
            st("vector", lambda e, sl=sl, mb=mb, off=off, par=par:
               e.tensor_mul(
                   yg[par][:, sl * 128:(sl + 1) * 128], PB[mb][:, off:off + 128],
                   zc[par][:, sl * 128:(sl + 1) * 128]),
               [BT[mb], f"zc{par}", f"yg{par}"], [f"yg{par}"],
               tag=f"ygev{c}_{sl}")

    def main_out(c):
        par = c % 2
        rc0 = c * 128
        def omm(e, par=par):
            last = None
            for d16 in range(16):
                for hf in range(2):
                    last = e.matmul(
                        PB[4 + hf][:, :], yg[par][:, d16 * 128:(d16 + 1) * 128],
                        WoutT_sb[:, d16 * 1024 + hf * 512:
                                 d16 * 1024 + (hf + 1) * 512],
                        start=(d16 == 0), stop=(d16 == 15))
            return last
        st("tensor", omm, [f"yg{par}", "WoutT"], [BT[4], BT[5]], tag=f"omm{c}")
        ob = outbuf[par]
        otok = f"outbuf{par}"
        for hf in range(2):
            st("vector", lambda e, hf=hf, par=par: e.scalar_tensor_tensor(
                sgbuf[par][:, hf * 512:(hf + 1) * 512], PB[4 + hf][:, :],
                rstd_col[par][:, :],
                gbuf[par][:, hf * 512:(hf + 1) * 512], ALU.mult, ALU.mult),
               [BT[4 + hf], f"rstd_col{par}", f"gbuf{par}", f"yg{par}"],
               [f"yg{par}"], tag=f"sg{c}_{hf}")
        st("vector", lambda e, ob=ob, par=par: e.tensor_add(
            ob[:, :], sgbuf[par][:, :], xres[par][:, :]),
           [f"yg{par}", f"xres{par}", otok], [otok], tag=f"fin{c}")
        dma("sync", out[rc0:rc0 + 128, :], ob[:, :], [otok], [f"out{c}"],
            tag=f"stout{c}")

    for pr in range(NCH // 2):
        cs = (2 * pr, 2 * pr + 1)
        for c in cs:
            main_dmas(c)
        for c in cs:
            main_state(c)
        for c in cs:
            main_y(c)
        for c in cs:
            main_gate(c)
        for c in cs:
            main_out(c)

    p.final_wait()
    return nc, p


_nc_cache = {}


def _get_program():
    if "nc" not in _nc_cache:
        nc, p = build_program()
        with nc.Block() as block, \
             nc.semaphore("s_t") as s_t, nc.semaphore("s_v") as s_v, \
             nc.semaphore("s_s") as s_s, nc.semaphore("s_g") as s_g, \
             nc.semaphore("s_y") as s_y, nc.semaphore("s_cc") as s_cc:
            p.emit(nc, block, {"tensor": s_t, "vector": s_v, "scalar": s_s,
                               "gpsimd": s_g, "sync": s_y, "cc": s_cc})
        _nc_cache["nc"] = nc
    return _nc_cache["nc"]


def _host_inputs(L0_attn_x, L0_attn_in_proj_w, L0_attn_conv_w, L0_attn_A_log,
                 L0_attn_Dp, L0_attn_norm_w, L0_attn_out_proj_w,
                 L0_ffn_router_w, L0_ffn_w1, L0_ffn_w2):
    f32 = np.float32
    x = np.ascontiguousarray(np.asarray(L0_attn_x, dtype=f32))
    Win = np.ascontiguousarray(np.asarray(L0_attn_in_proj_w, dtype=f32))
    WinT = np.ascontiguousarray(Win.T)
    w1 = np.ascontiguousarray(np.asarray(L0_ffn_w1, dtype=f32))
    w2 = np.ascontiguousarray(np.asarray(L0_ffn_w2, dtype=f32))
    WoutT = np.ascontiguousarray(np.asarray(L0_attn_out_proj_w, dtype=f32).T)
    small9 = np.ascontiguousarray(
        np.concatenate([np.asarray(L0_ffn_router_w, f32).T,
                        Win[4352:4353, :].T], axis=1))
    A_log_row = np.ascontiguousarray(np.asarray(L0_attn_A_log, f32).reshape(1, S))
    cw = np.asarray(L0_attn_conv_w, f32).reshape(DI, 4)
    cw_col = np.ascontiguousarray(
        cw.reshape(16, 128, 4).transpose(1, 0, 2).reshape(128, 64))
    dp_col = np.ascontiguousarray(np.asarray(L0_attn_Dp, f32).reshape(16, 128).T)
    dp_b = np.ascontiguousarray(
        np.broadcast_to(np.asarray(L0_attn_Dp, f32).reshape(1, DI), (128, DI)))
    nw_col = np.ascontiguousarray(np.asarray(L0_attn_norm_w, f32).reshape(16, 128).T)

    tt = np.arange(128)
    M8 = ((tt[:, None] // BLK == tt[None, :] // BLK) & (tt[:, None] <= tt[None, :]))
    M8r = ((tt[:, None] // BLK == tt[None, :] // BLK) & (tt[:, None] > tt[None, :]))
    Mlow = tt[:, None] <= tt[None, :]
    Mrows = np.ascontiguousarray(
        np.concatenate([np.eye(128), M8, M8r, Mlow], axis=1).astype(f32))
    maskoff = np.ascontiguousarray(
        (tt[:, None] // BLK < tt[None, :] // BLK).astype(f32))
    bandmask = np.ascontiguousarray(np.concatenate(
        [((tt[None, :] - tt[:, None] == k) & (tt[:, None] // BLK == tt[None, :] // BLK))
         for k in range(BLK)], axis=1).astype(np.uint8))
    idf32 = np.eye(128, dtype=f32)
    eyerow8 = np.zeros((8, 8 * 128), f32)
    for e in range(8):
        eyerow8[e, e * 128:(e + 1) * 128] = 1.0

    shared = dict(WinT=WinT, w1=w1, w2=w2, WoutT=WoutT, small9=small9,
                  A_log_row=A_log_row, cw_col=cw_col, dp_col=dp_col,
                  dp_b=dp_b, nw_col=nw_col, Mrows=Mrows, maskoff=maskoff,
                  bandmask=bandmask, idf32=idf32, eyerow8=eyerow8)

    in_maps = []
    for core in range(8):
        b, h = core // 2, core % 2
        xin = np.zeros((TW, DM), f32)
        if h == 1:
            xin[125:128] = x[b, T - 3:T]
        xin[128:] = x[b, h * T:(h + 1) * T]
        sel = np.zeros((128, 8), f32)
        if h == 1:
            sel[:, 2 * b] = 1.0
        m = dict(shared)
        m["xin"] = np.ascontiguousarray(xin)
        m["sel8"] = sel
        in_maps.append(m)
    return in_maps


_LAST_RES = None


def kernel(**inputs):
    global _LAST_RES
    nc = _get_program()
    in_maps = _host_inputs(**inputs)
    res = run_bass_kernel_spmd(nc, in_maps, core_ids=list(range(8)))
    _LAST_RES = res
    outp = np.empty((4, 4096, 1024), np.float32)
    for core in range(8):
        b, h = core // 2, core % 2
        outp[b, h * T:(h + 1) * T] = np.asarray(res.results[core]["out"])
    return outp


# version 21
# speedup vs baseline: 1.0526x; 1.0087x over previous
import numpy as np
import ml_dtypes
import concourse.bass as bass
import concourse.mybir as mybir
from concourse.bass_utils import run_bass_kernel_spmd

F32 = mybir.dt.float32
BF16 = mybir.dt.bfloat16
U8 = mybir.dt.uint8
AF = mybir.ActivationFunctionType
ALU = mybir.AluOpType
AX = mybir.AxisListType

DM = 1024
DI = 2048
S = 128
Q = 128
NCH = 16
BLK = 8
T = 2048
TW = 2176
NT = TW // 128
J = 4353
JT = 34
WF = 12288
E = 8
WIDTHS = [1024] * 4 + [2048] * 4
OFFS = np.concatenate([[0], np.cumsum(WIDTHS)]).astype(int)
CLIP = 8.0
EPS = 1e-5
TBATCH = [(0, 512), (512, 512), (1024, 512), (1536, 512), (2048, 128)]
MM2G = [(0, 3), (3, 3), (6, 3), (9, 3), (12, 3), (15, 1)]

ENGINES = ("tensor", "vector", "scalar", "gpsimd", "sync")
STRICT_SAME_ENGINE = True
DEBUG_PHASE = None


class Prog:

    def __init__(self):
        self.steps = []
        self.cnt = {e: 0 for e in ENGINES}
        self.cnt["cc"] = 0
        self.lastw = {}
        self.readers = {}

    def step(self, engine, emit, reads=(), writes=(), dma=False, tag="",
             done_sem=None, extra_waits=()):
        deps = {}

        def dep(sem, val, asyn):
            if sem == engine and not asyn and not STRICT_SAME_ENGINE:
                return
            if deps.get(sem, 0) < val:
                deps[sem] = val

        for t in reads:
            w = self.lastw.get(t)
            if w:
                dep(*w)
        for t in writes:
            w = self.lastw.get(t)
            if w:
                dep(*w)
            for r in self.readers.get(t, ()):
                dep(*r)
        for sem, val in extra_waits:
            if deps.get(sem, 0) < val:
                deps[sem] = val
        if done_sem is not None:
            sem, amt = done_sem
        else:
            sem, amt = engine, (16 if dma else 1)
        self.cnt[sem] = self.cnt.get(sem, 0) + amt
        rec = (sem, self.cnt[sem], dma or done_sem is not None)
        for t in reads:
            self.readers.setdefault(t, []).append(rec)
        for t in writes:
            self.lastw[t] = rec
            self.readers[t] = []
        self.steps.append((engine, emit, deps, (sem, amt), tag))
        return rec

    def barrier(self):
        snapshot = dict(self.cnt)
        for e in ENGINES:
            waits = {s: v for s, v in snapshot.items() if v > 0 and s != e}
            self.steps.append((e, None, waits, (e, 0), "barrier"))
        self.lastw.clear()
        self.readers.clear()

    def final_wait(self, engine="gpsimd"):
        waits = {s: v for s, v in self.cnt.items() if v > 0}
        self.steps.append((engine, None, waits, (engine, 0), "final"))

    def emit(self, nc, block, sems):
        per = {e: [] for e in ENGINES}
        for stp in self.steps:
            per[stp[0]].append(stp)

        def mk(engine, steps):
            def body(eng):
                satisfied = {}
                for _, emitfn, deps, (sem, amt), tag in steps:
                    for s9, v in sorted(deps.items()):
                        if satisfied.get(s9, 0) >= v:
                            continue
                        eng.wait_ge(sems[s9], v)
                        satisfied[s9] = v
                    if emitfn is None:
                        continue
                    ins = emitfn(eng)
                    if amt > 0:
                        if ins is None:
                            raise RuntimeError(f"step {tag} returned no instruction")
                        ins.then_inc(sems[sem], amt)
            return body

        for e in ENGINES:
            if per[e]:
                getattr(block, e)(mk(e, per[e]))


def build_program():
    nc = bass.Bass("TRN2", target_bir_lowering=False)
    p = Prog()

    def inp(name, shape, dtype=F32):
        return nc.dram_tensor(name, shape, dtype, kind="ExternalInput").ap()

    xin = inp("xin", [TW, DM])
    dp_b = inp("dp_b", [128, DI])
    WinT = inp("WinT", [DM, J])
    w1 = inp("w1", [DM, WF])
    w2 = inp("w2", [WF, DM])
    WoutT = inp("WoutT", [DI, DM])
    small9 = inp("small9", [DM, 9])
    A_log_row = inp("A_log_row", [1, S])
    cw_col = inp("cw_col", [128, 64])
    dp_col = inp("dp_col", [128, 16])
    nw_col = inp("nw_col", [128, 16])
    sel8 = inp("sel8", [128, 8])
    Mrows = inp("Mrows", [128, 512])
    maskoff = inp("maskoff", [128, 128])
    bandmask = inp("bandmask", [128, 8 * 128], U8)
    idf32 = inp("idf32", [128, 128])
    eyerow8 = inp("eyerow8", [8, 8 * 128])

    out = nc.dram_tensor("out", [T, DM], F32, kind="ExternalOutput").ap()

    xnhi_scr = nc.dram_tensor("xnhi_scr", [TW, DM], BF16, kind="Internal").ap()
    xnlo_scr = nc.dram_tensor("xnlo_scr", [TW, DM], BF16, kind="Internal").ap()
    xnT_scr = nc.dram_tensor("xnT_scr", [128, 8 * TW], BF16, kind="Internal").ap()
    z_scr = nc.dram_tensor("z_scr", [DI, TW], BF16, kind="Internal").ap()
    xm_scr = nc.dram_tensor("xm_scr", [NCH * 128, 2048], BF16, kind="Internal").ap()
    xmT_scr = nc.dram_tensor("xmT_scr", [T, DI], BF16, kind="Internal").ap()
    H_scr = nc.dram_tensor("H_scr", [WF, T], BF16, kind="Internal").ap()
    g_scr = nc.dram_tensor("g_scr", [T, DM], BF16, kind="Internal").ap()
    ag_in = nc.dram_tensor("ag_in", [S, DI], F32, kind="Internal")
    ag_out = nc.dram_tensor("ag_out", [8 * S, DI], F32, kind="Internal")

    PB = [nc.alloc_psum_tensor(f"PB{i}", [128, 512], F32) for i in range(8)]
    BT = [f"B{i}" for i in range(8)]

    def sball(name, shape, dtype=F32):
        return nc.alloc_sbuf_tensor(name, shape, dtype)

    dts_tok = sball("dts_tok", [128, NT])
    combine = sball("combine", [128, NT * 8])
    A_row = sball("A_row", [1, 128])
    ones1 = sball("ones1", [1, 128])
    onesc = sball("onesc", [128, 1])
    ones1b = sball("ones1b", [1, 128], BF16)
    onescb = sball("onescb", [128, 1], BF16)
    rows_sb = [sball(f"rows_sb{i}", [1, 512]) for i in range(2)]
    rQrow = [sball(f"rQrow{i}", [1, 128]) for i in range(2)]
    rstdrow = [sball(f"rstdrow{i}", [1, 128]) for i in range(2)]
    ck_sb = [sball(f"ck_sb{i}", [1, 1024], BF16) for i in range(2)]
    l8row = [sball(f"l8row{i}", [1, 16]) for i in range(2)]
    Mrows_sb = sball("Mrows_sb", [128, 512])
    idb_sb = sball("idb_sb", [128, 128], BF16)
    maskoff_sb = sball("maskoff_sb", [128, 128])
    bandmask_sb = sball("bandmask_sb", [128, 8 * 128], U8)
    idf_sb = sball("idf_sb", [128, 128])
    eyerow8_sb = sball("eyerow8_sb", [8, 8 * 128])
    cw_sb = sball("cw_sb", [128, 64])
    dp_sb = sball("dp_sb", [128, 16])
    nw_sb = sball("nw_sb", [128, 16])
    sel8_sb = sball("sel8_sb", [128, 8])
    small_f = sball("small_f", [128, 8 * 9])
    small_hi = sball("small_hi", [128, 8 * 9], BF16)
    small_lo = sball("small_lo", [128, 8 * 9], BF16)
    smallhl = sball("smallhl", [128, 8 * 18], BF16)
    lnc = [sball(f"lnc{i}", [128, 16]) for i in range(2)]
    headf = sball("headf", [128, 16])
    sqc_t = sball("sqc_t", [128, 8])
    dccol = [sball(f"dccol{i}", [128, 1]) for i in range(2)]
    d8 = [sball(f"d8{i}", [128, 16]) for i in range(2)]
    srow_t = sball("srow_t", [1, 128])
    cbias = sball("cbias", [128, 2])
    rstd_col = [sball(f"rstd_col{i}", [128, 1]) for i in range(2)]

    arena_pp = 178 * 1024
    rem_pp = nc.sbuf_bytes_remaining // 128
    got = nc.bump_sbuf(arena_pp)
    assert got is not None, f"arena alloc failed rem={rem_pp}"
    abase = got[0]

    def at(name, off_pp, shape, dtype=F32):
        return nc.alloc_sbuf_tensor_at(name, shape, dtype, offset=abase + off_pp)

    K = 1024
    xnT_hi = at("xnT_hi", 0, [128, 8 * TW], BF16)
    w1buf = [at("w1buf0", 0, [128, 8 * 512], BF16),
             at("w1buf1", 8 * K, [128, 8 * 512], BF16)]
    Hrelu = at("Hrelu", 16 * K, [128, 2048], BF16)
    Hsqb = [at("Hsqb0", 20 * K, [128, 2048], BF16),
            at("Hsqb1", 24 * K, [128, 2048], BF16)]
    Hsq2 = at("Hsq2", 28 * K, [128, 2048], BF16)
    hbuf4 = [at("hbuf4_0", 0, [128, 8 * 1024], BF16),
             at("hbuf4_1", 16 * K, [128, 8 * 1024], BF16)]
    WoutT_sb = at("WoutT_sb", 0, [128, 16 * 1024], BF16)

    B0 = 34 * K
    xnT_lo = at("xnT_lo", B0, [128, 8 * TW], BF16)
    prebuf = [at("prebuf0", B0, [128, TW], BF16),
              at("prebuf1", B0 + 4352, [128, TW], BF16)]
    wbuf = [at("wbuf0", B0 + 8704, [128, 8 * 128], BF16),
            at("wbuf1", B0 + 10752, [128, 8 * 128], BF16)]
    zbuf = [at("zbuf0", B0 + 12800, [128, 512], BF16),
            at("zbuf1", B0 + 13824, [128, 512], BF16)]
    cvt0 = [at("cvt0a", B0 + 14848, [128, 512]),
            at("cvt0b", B0 + 16896, [128, 512])]
    cvt1 = [at("cvt1a", B0 + 19 * K, [128, 1024]),
            at("cvt1b", B0 + 23 * K, [128, 1024])]
    xnTs = [at("xnTs0", B0, [128, 2048], BF16),
            at("xnTs1", B0 + 4 * K, [128, 2048], BF16)]
    sq_fm = at("sq_fm", B0 + 8 * K, [8, 2048])
    sqb_sb = at("sqb_sb", B0 + 16 * K, [128, 2048], BF16)
    w2buf = [at("w2buf0", B0, [128, 8 * 1024], BF16),
             at("w2buf1", B0 + 16 * K, [128, 8 * 1024], BF16)]
    zc = [at("zc0", B0, [128, 2048], BF16),
          at("zc1", B0 + 4 * K, [128, 2048], BF16)]
    gbuf = [at("gbuf0", B0 + 8 * K, [128, 1024], BF16),
            at("gbuf1", B0 + 10 * K, [128, 1024], BF16)]
    xres = [at("xres0", B0 + 12 * K, [128, 1024]),
            at("xres1", B0 + 16 * K, [128, 1024])]
    outbuf = [at("outbuf0", B0 + 20 * K, [128, 1024]),
              at("outbuf1", B0 + 24 * K, [128, 1024])]
    dccol_all = at("dccol_all", B0 + 32 * K, [128, 16])

    C0 = 68 * K
    cvacc = [at("cvacc0", C0, [128, 2048]),
             at("cvacc1", C0 + 8 * K, [128, 2048])]
    cvt2 = [at("cvt2a", C0 + 16 * K, [128, 1024]),
            at("cvt2b", C0 + 20 * K, [128, 1024])]
    xmout = [at("xmout0", C0 + 24 * K, [128, 2048], BF16),
             at("xmout1", C0 + 28 * K, [128, 2048], BF16)]
    y_fm = [at("y_fm0", C0 + 32 * K, [128, 2048]),
            at("y_fm1", C0 + 40 * K, [128, 2048])]
    ysqh = [at("ysqh0", C0 + 16 * K, [128, 1024]),
            at("ysqh1", C0 + 20 * K, [128, 1024])]
    xmc = [at("xmc0", C0 + 24 * K, [128, 2048], BF16),
           at("xmc1", C0 + 28 * K, [128, 2048], BF16)]
    xm_tok = [at("xm_tok0", C0 + 48 * K, [128, 2048], BF16),
              at("xm_tok1", C0 + 52 * K, [128, 2048], BF16)]
    yg = [at("yg0", C0 + 0 * K, [128, 2048], BF16),
          at("yg1", C0 + 4 * K, [128, 2048], BF16)]
    sgbuf = [at("sgbuf0", C0 + 0 * K, [128, 1024]),
             at("sgbuf1", C0 + 4 * K, [128, 1024])]
    Zbuf = [at("Zbuf0", C0 + 56 * K, [128, 1024], BF16),
            at("Zbuf1", C0 + 58 * K, [128, 1024], BF16)]
    Bendtok_all = at("Bendtok_all", C0 + 8 * K, [128, 2048], BF16)
    ckb = [at("ckb0", C0 + 12 * K, [128, 1024], BF16),
           at("ckb1", C0 + 14 * K, [128, 1024], BF16)]
    Wsb_all = at("Wsb_all", C0 + 16 * K, [128, 2048], BF16)
    Cs_all = at("Cs_all", C0 + 20 * K, [128, 2048], BF16)
    Dp_bcast = at("Dp_bcast", C0 + 24 * K, [128, 2048])
    xmdp = at("xmdp", C0 + 56 * K, [128, 2048])
    xmst = [at("xmst0", C0 + 32 * K, [128, 2048], BF16),
            at("xmst1", C0 + 36 * K, [128, 2048], BF16)]

    D0 = 132 * K
    S_run = at("S_run", D0, [128, 2048])
    B_fm = at("B_fm", D0 + 8 * K, [128, TW], BF16)
    C_fm = at("C_fm", D0 + 8 * K + 4352, [128, TW], BF16)
    S_run_bf = [at("S_run_bf0", D0 + 8 * K, [128, 2048], BF16),
                at("S_run_bf1", D0 + 12 * K, [128, 2048], BF16)]
    DS = D0 + 17 * K
    xnT_e = at("xnT_e", DS, [128, 8 * 2048], BF16)
    gout = [at("gout0", DS, [128, 1024], BF16),
            at("gout1", DS + 2 * K, [128, 1024], BF16)]
    MS = []
    for par in range(2):
        base = DS + par * 7 * K
        MS.append(dict(
            dtsb=at(f"dtsb{par}", base, [128, 128]),
            ebufA=at(f"ebufA{par}", base + 512, [128, 128]),
            ebufB=at(f"ebufB{par}", base + 1024, [128, 128]),
            F1=at(f"F1_{par}", base + 1536, [128, 128], BF16),
            F2base=at(f"F2base{par}", base + 2048, [128, 128]),
            F2run=at(f"F2run{par}", base + 2560, [128, 128]),
            Btld=at(f"Btld{par}", base + 3072, [128, 128]),
            Bend=at(f"Bend{par}", base + 3584, [128, 128]),
            Bendtok=at(f"Bendtok{par}", base + 4096, [128, 128], BF16),
            Cs=at(f"Cs{par}", base + 4352, [128, 128], BF16),
            A_bar=at(f"A_bar{par}", base + 4608, [128, 128]),
            Ppp=[at(f"Ppp{par}_0", base + 5120, [128, 128]),
                 at(f"Ppp{par}_1", base + 5632, [128, 128])],
            zcw=at(f"zcw{par}", base + 6144, [128, 128]),
            Wsb=at(f"Wsb{par}", base + 6656, [128, 128], BF16),
        ))
    F2runB = [at("F2runB0", DS + 14 * K, [128, 128]),
              at("F2runB1", DS + 14 * K + 512, [128, 128])]
    F2hist = [at("F2hist0", DS + 16 * K, [128, 15 * 128], BF16),
              at("F2hist1", DS + 20 * K, [128, 15 * 128], BF16)]
    ACC_OFFS = [D0, D0 + 4 * K, D0 + 8 * K, D0 + 12 * K,
                C0, C0 + 4 * K, C0 + 12 * K, C0 + 24 * K,
                C0 + 28 * K, C0 + 32 * K, C0 + 36 * K, C0 + 40 * K,
                C0 + 44 * K, C0 + 48 * K, C0 + 52 * K, C0 + 56 * K]
    accs = [at(f"accs{t}", off, [128, 1024]) for t, off in enumerate(ACC_OFFS)]
    PS1 = DS
    xq = [at("xq0", PS1, [128, 1024]), at("xq1", PS1 + 4 * K, [128, 1024])]
    lnjunk = at("lnjunk", PS1 + 8 * K, [128, 1024])
    xnf = [at("xnf0", PS1 + 12 * K, [128, 1024]),
           at("xnf1", PS1 + 16 * K, [128, 1024])]
    xnhib = [at("xnhib0", PS1 + 20 * K, [128, 1024], BF16),
             at("xnhib1", PS1 + 22 * K, [128, 1024], BF16)]
    xnlof = [at("xnlof0", PS1 + 24 * K, [128, 1024]),
             at("xnlof1", PS1 + 28 * K, [128, 1024])]

    def dma(queue, o, i, reads, writes, tag="", **kw):
        return p.step(queue, lambda e, o=o, i=i, kw=kw: e.dma_start(out=o, in_=i, **kw),
                      reads=reads, writes=writes, dma=True, tag=tag)

    def dmat(queue, o, i, reads, writes, tag=""):
        return p.step(queue, lambda e, o=o, i=i: e.dma_start_transpose(o, i),
                      reads=reads, writes=writes, dma=True, tag=tag)

    def st(engine, fn, reads=(), writes=(), tag="", extra_waits=()):
        return p.step(engine, fn, reads=reads, writes=writes, tag=tag,
                      extra_waits=extra_waits)

    dma("sync", Mrows_sb[:, :], Mrows, [], ["Mrows"])
    dma("sync", maskoff_sb[:, :], maskoff, [], ["maskoff"])
    dma("sync", bandmask_sb[:, :], bandmask, [], ["bandmask"])
    dma("sync", idf_sb[:, :], idf32, [], ["idf"])
    st("vector", lambda e: e.tensor_copy(idb_sb[:, :], idf_sb[:, :]),
       ["idf"], ["idb"])
    dma("sync", eyerow8_sb[:, :], eyerow8, [], ["eyerow8"])
    dma("sync", cw_sb[:, :], cw_col, [], ["cw"])
    dma("sync", dp_sb[:, :], dp_col, [], ["dp"])
    dma("sync", nw_sb[:, :], nw_col, [], ["nw"])
    dma("sync", sel8_sb[:, :], sel8, [], ["sel8"])
    dma("sync", small_f[:, :].rearrange("p (s e) -> p s e", s=8),
        small9.rearrange("(s p) e -> p s e", p=128),
        [], ["small_f"], allow_slow_non_contiguous=True)
    dma("sync", srow_t[:, :], A_log_row, [], ["alog"])
    st("vector", lambda e: e.memset(cbias[:, 0:1], EPS), [], ["cbias"])
    st("vector", lambda e: e.memset(cbias[:, 1:2], 1.0), [], ["cbias"])
    st("vector", lambda e: e.memset(ones1[:, :], 1.0), [], ["ones1"])
    st("vector", lambda e: e.memset(onesc[:, :], 1.0), [], ["onesc"])
    st("vector", lambda e: e.memset(ones1b[:, :], 1.0), [], ["ones1b"])
    st("vector", lambda e: e.memset(onescb[:, :], 1.0), [], ["onescb"])
    for par in range(2):
        st("vector", lambda e, par=par: e.memset(Zbuf[par][:, :], 0.0),
           [], [f"Zbuf{par}"])
    st("scalar", lambda e: e.activation(srow_t[:, :], srow_t[:, :], AF.Exp),
       ["alog"], ["alog"])
    st("vector", lambda e: e.tensor_scalar_mul(A_row[:, :], srow_t[:, :], -1.0),
       ["alog"], ["A_row"])
    st("vector", lambda e: e.tensor_copy(small_hi[:, :], small_f[:, :]),
       ["small_f"], ["small_hi"])
    st("vector", lambda e: e.tensor_sub(small_lo[:, :], small_f[:, :], small_hi[:, :]),
       ["small_f", "small_hi"], ["small_lo"])
    for s8 in range(8):
        st("vector", lambda e, s8=s8: e.tensor_copy(
            smallhl[:, s8 * 18:s8 * 18 + 9], small_hi[:, s8 * 9:(s8 + 1) * 9]),
           ["small_hi"], ["smallhl"])
        st("vector", lambda e, s8=s8: e.tensor_copy(
            smallhl[:, s8 * 18 + 9:s8 * 18 + 18], small_lo[:, s8 * 9:(s8 + 1) * 9]),
           ["small_lo"], ["smallhl"])

    for i in range(NT):
        pr = i % 2
        xb = xq[pr]
        xtok = f"xq{pr}"
        ln = lnc[pr]
        ltok = f"lnc{pr}"
        dma("sync", xb[:, :], xin[i * 128:(i + 1) * 128, :], [], [xtok], tag=f"ldx{i}")
        st("scalar", lambda e, xb=xb, ln=ln: e.activation(
            lnjunk[:, :], xb[:, :], AF.Square, accum_out=ln[:, 0:1]),
           [xtok, "lnjunk"], ["lnjunk", ltok], tag=f"ssq{i}")
        st("scalar", lambda e, xb=xb, ln=ln: e.activation(
            lnjunk[:, :], xb[:, :], AF.Copy, accum_out=ln[:, 1:2]),
           [xtok, "lnjunk"], ["lnjunk", ltok], tag=f"msum{i}")
        st("vector", lambda e, ln=ln: e.tensor_scalar_mul(ln[:, 2:3], ln[:, 1:2], 1.0 / DM),
           [ltok], [ltok])
        st("vector", lambda e, ln=ln: e.tensor_scalar_mul(ln[:, 3:4], ln[:, 0:1], 1.0 / DM),
           [ltok], [ltok])
        st("vector", lambda e, ln=ln: e.scalar_tensor_tensor(
            ln[:, 4:5], ln[:, 2:3], ln[:, 2:3], ln[:, 3:4], ALU.mult, ALU.subtract),
           [ltok], [ltok])
        st("scalar", lambda e, ln=ln: e.activation(ln[:, 5:6], ln[:, 4:5], AF.Ln,
                                                   scale=-1.0, bias=cbias[:, 0:1]),
           [ltok, "cbias"], [ltok])
        st("scalar", lambda e, ln=ln: e.activation(ln[:, 6:7], ln[:, 5:6], AF.Exp,
                                                   scale=-0.5),
           [ltok], [ltok])
        st("vector", lambda e, xb=xb, ln=ln, pr=pr: e.tensor_scalar(
            xnhib[pr][:, :], xb[:, :], ln[:, 2:3], ln[:, 6:7], ALU.subtract, ALU.mult),
           [xtok, ltok], [f"xnhib{pr}"], tag=f"xnhi{i}")
        st("vector", lambda e, xb=xb, ln=ln, pr=pr: e.tensor_scalar(
            xnf[pr][:, :], xb[:, :], ln[:, 2:3], ln[:, 6:7], ALU.subtract, ALU.mult),
           [xtok, ltok], [f"xnf{pr}"], tag=f"xnf{i}")
        st("vector", lambda e, pr=pr: e.tensor_sub(
            xnlof[pr][:, :], xnf[pr][:, :], xnhib[pr][:, :]),
           [f"xnf{pr}", f"xnhib{pr}"], [f"xnlof{pr}"], tag=f"xnlo{i}")
        for sl in range(8):
            bank = sl % 4
            st("tensor", lambda e, pr=pr, sl=sl, bank=bank: e.transpose(
                PB[bank][:, 0:128], xnf[pr][:, sl * 128:(sl + 1) * 128], idf_sb[:, :]),
               [f"xnf{pr}", "idf"], [BT[bank]], tag=f"trh{i}_{sl}")
            st("vector", lambda e, i=i, sl=sl, bank=bank: e.tensor_copy(
                xnT_hi[:, sl * TW + i * 128: sl * TW + (i + 1) * 128],
                PB[bank][:, 0:128]),
               [BT[bank], f"xnT_hi_{sl}"], [f"xnT_hi_{sl}"], tag=f"trhev{i}_{sl}")
            st("tensor", lambda e, pr=pr, sl=sl, bank=bank: e.transpose(
                PB[bank][:, 128:256], xnlof[pr][:, sl * 128:(sl + 1) * 128],
                idf_sb[:, :]),
               [f"xnlof{pr}", "idf"], [BT[bank]], tag=f"trl{i}_{sl}")
            st("vector", lambda e, i=i, sl=sl, bank=bank: e.tensor_copy(
                xnT_lo[:, sl * TW + i * 128: sl * TW + (i + 1) * 128],
                PB[bank][:, 128:256]),
               [BT[bank], f"xnT_lo_{sl}"], [f"xnT_lo_{sl}"], tag=f"trlev{i}_{sl}")

    dma("sync", xnT_scr, xnT_hi[:, :],
        [f"xnT_hi_{sl}" for sl in range(8)], ["xnT_scr"], tag="xnTscr")

    for i in range(1, NT):
        def head_mm(e, i=i):
            last = None
            for dm8 in range(8):
                lh = xnT_hi[:, dm8 * TW + i * 128: dm8 * TW + (i + 1) * 128]
                last = e.matmul(PB[6][:, 0:18], lh, smallhl[:, dm8 * 18:(dm8 + 1) * 18],
                                start=(dm8 == 0), stop=False)
            for dm8 in range(8):
                ll = xnT_lo[:, dm8 * TW + i * 128: dm8 * TW + (i + 1) * 128]
                last = e.matmul(PB[6][:, 0:9], ll, small_hi[:, dm8 * 9:(dm8 + 1) * 9],
                                start=False, stop=(dm8 == 7))
            return last
        st("tensor", head_mm,
           [f"xnT_hi_{d}" for d in range(8)] + [f"xnT_lo_{d}" for d in range(8)]
           + ["smallhl", "small_hi"], ["B6"], tag=f"headmm{i}")
        st("vector", lambda e: e.tensor_copy(lnjunk[:, 0:18], PB[6][:, 0:18]),
           ["B6", "lnjunk"], ["lnjunk"], tag=f"headcp{i}")
        st("vector", lambda e: e.tensor_add(headf[:, 0:9], lnjunk[:, 0:9],
                                            lnjunk[:, 9:18]),
           ["lnjunk", "headf"], ["headf"], tag=f"headadd{i}")
        st("scalar", lambda e: e.activation(headf[:, 9:10], headf[:, 8:9], AF.Abs),
           ["headf"], ["headf"])
        st("scalar", lambda e: e.activation(headf[:, 10:11], headf[:, 9:10], AF.Exp,
                                            scale=-1.0),
           ["headf"], ["headf"])
        st("scalar", lambda e: e.activation(headf[:, 11:12], headf[:, 10:11], AF.Ln,
                                            bias=cbias[:, 1:2]),
           ["headf", "cbias"], ["headf"])
        st("vector", lambda e: e.tensor_scalar_max(headf[:, 12:13], headf[:, 8:9], 0.0),
           ["headf"], ["headf"])
        st("vector", lambda e, i=i: e.tensor_add(
            dts_tok[:, i:i + 1], headf[:, 11:12], headf[:, 12:13]),
           ["headf"], ["dts_tok"], tag=f"dts{i}")
        csl = combine[:, i * 8:(i + 1) * 8]
        st("scalar", lambda e, csl=csl: e.activation(csl, headf[:, 0:8], AF.Sigmoid),
           ["headf"], ["probs"], tag=f"sig{i}")
        st("vector", lambda e, csl=csl: e.reduce_max(headf[:, 13:14], csl, AX.X),
           ["probs", "headf"], ["headf"])
        st("vector", lambda e, csl=csl: e.tensor_scalar(
            lnjunk[:, 0:8], csl, headf[:, 13:14], None, ALU.is_equal),
           ["probs", "headf", "lnjunk"], ["lnjunk"])
        st("vector", lambda e, csl=csl: e.scalar_tensor_tensor(
            lnjunk[:, 8:16], lnjunk[:, 0:8], -1e30, csl, ALU.mult, ALU.add),
           ["lnjunk", "probs"], ["lnjunk"])
        st("vector", lambda e: e.reduce_max(headf[:, 14:15], lnjunk[:, 8:16], AX.X),
           ["lnjunk", "headf"], ["headf"])
        st("vector", lambda e: e.scalar_tensor_tensor(
            headf[:, 15:16], headf[:, 13:14], 1.0, headf[:, 14:15], ALU.mult, ALU.add),
           ["headf"], ["headf"])
        st("vector", lambda e: e.tensor_scalar_add(headf[:, 15:16], headf[:, 15:16], 1e-20),
           ["headf"], ["headf"])
        st("vector", lambda e: e.reciprocal(headf[:, 13:14], headf[:, 15:16]),
           ["headf"], ["headf"])
        st("vector", lambda e, csl=csl: e.tensor_scalar(
            lnjunk[:, 0:8], csl, headf[:, 14:15], None, ALU.is_ge),
           ["probs", "headf", "lnjunk"], ["lnjunk"])
        st("vector", lambda e, csl=csl: e.tensor_mul(csl, csl, lnjunk[:, 0:8]),
           ["probs", "lnjunk"], ["probs"])
        st("vector", lambda e, csl=csl: e.tensor_scalar_mul(csl, csl, headf[:, 13:14]),
           ["probs", "headf"], ["probs"], tag=f"comb{i}")

    p.barrier()
    if DEBUG_PHASE == "p1":
        p.final_wait()
        return nc, p

    def chunk_rows(c):
        par = c % 2
        mb = 6 + par
        mt = BT[mb]
        rows = rows_sb[par]
        rt = f"rows{par}"
        st("tensor", lambda e, c=c, mb=mb: e.matmul(
            PB[mb][0:1, 0:512], dts_tok[:, c + 1:c + 2], Mrows_sb[:, :],
            start=True, stop=True),
           ["dts_tok", "Mrows"], [mt], tag=f"rows{c}")
        st("vector", lambda e, rows=rows, mb=mb: e.tensor_copy(
            rows[:, :], PB[mb][0:1, 0:512]),
           [mt], [rt], tag=f"rowsev{c}")
        st("vector", lambda e, rows=rows, par=par: e.tensor_scalar(
            rQrow[par][:, :], rows[:, 384:512], -1.0, rows[0:1, 511:512],
            ALU.mult, ALU.add),
           [rt], [f"rQ{par}"], tag=f"rq{c}")
        st("vector", lambda e, rows=rows, par=par: e.tensor_copy(
            l8row[par][:, :], rows[0:1, 135:256:8]),
           [rt], [f"l8{par}"], tag=f"l8{c}")
        st("tensor", lambda e, rows=rows, mb=mb: e.matmul(
            PB[mb][:, 0:128], ones1[:, :], rows[0:1, 0:128], start=True, stop=True),
           [rt, "ones1"], [mt], tag=f"dtsbmm{c}")
        st("vector", lambda e, par=par, mb=mb: e.tensor_copy(
            MS[par]["dtsb"][:, :], PB[mb][:, 0:128]),
           [mt], [f"dtsb{par}"], tag=f"dtsbev{c}")
        st("tensor", lambda e, par=par, mb=mb: e.matmul(
            PB[mb][:, 128:144], A_row[:, :], l8row[par][:, :], start=True, stop=True),
           [f"l8{par}", "A_row"], [mt], tag=f"d8mm{c}")
        st("scalar", lambda e, par=par, mb=mb: e.activation(
            d8[par][:, :], PB[mb][:, 128:144], AF.Exp),
           [mt], [f"d8{par}"], tag=f"d8e{c}")
        st("tensor", lambda e, rows=rows, mb=mb: e.matmul(
            PB[mb][:, 144:145], A_row[:, :], rows[0:1, 511:512], start=True, stop=True),
           [rt, "A_row"], [mt], tag=f"dcmm{c}")
        st("scalar", lambda e, c=c, mb=mb: e.activation(
            dccol_all[:, c:c + 1], PB[mb][:, 144:145], AF.Exp),
           [mt], [f"dcc{c}"], tag=f"dce{c}")

    def chunk_exp(c, rowap, rowtok, outsb, outtok, slot, tag):
        ab = 4 + c % 2
        abt = BT[ab]
        off = 0 if slot == 0 else 128
        st("tensor", lambda e, rowap=rowap, ab=ab, off=off: e.matmul(
            PB[ab][:, off:off + 128], A_row[:, :], rowap, start=True, stop=True),
           [rowtok, "A_row"], [abt], tag=f"{tag}mm{c}")
        st("scalar", lambda e, outsb=outsb, ab=ab, off=off: e.activation(
            outsb[:, :], PB[ab][:, off:off + 128], AF.Exp),
           [abt], [outtok], tag=f"{tag}e{c}")

    def chunk_btld(c):
        par = c % 2
        cc0 = 128 + c * 128
        st("vector", lambda e, cc0=cc0, par=par: e.tensor_mul(
            MS[par]["Btld"][:, :], B_fm[:, cc0:cc0 + 128], MS[par]["dtsb"][:, :]),
           ["B_fm", f"dtsb{par}"], [f"Btld{par}"], tag=f"btld{c}")

    def chunk_bend(c):
        par = c % 2
        mb = 6 + par
        mt = BT[mb]
        m = MS[par]
        chunk_exp(c, rQrow[par][:, :], f"rQ{par}", m["ebufA"], f"ebufA{par}", 0, "erq")
        st("vector", lambda e, m=m, par=par: e.tensor_mul(
            m["Bend"][:, :], m["Btld"][:, :], m["ebufA"][:, :]),
           [f"Btld{par}", f"ebufA{par}"], [f"Bend{par}"], tag=f"bend{c}")
        st("tensor", lambda e, m=m, mb=mb: e.transpose(
            PB[mb][:, 384:512], m["Bend"][:, :], idf_sb[:, :]),
           [f"Bend{par}", "idf"], [mt], tag=f"bendtr{c}")
        st("vector", lambda e, c=c, mb=mb: e.tensor_copy(
            Bendtok_all[:, c * 128:(c + 1) * 128], PB[mb][:, 384:512]),
           [mt], [f"Bet{c}", "cvacc1"], tag=f"bendev{c}")

    def chunk_xmtok(c):
        par = c % 2
        xf = y_fm[par]
        xftok = f"y_fm{par}"
        st("vector", lambda e, par=par, xf=xf: e.tensor_copy(xf[:, :], xmc[par][:, :]),
           [f"xmc{par}", xftok], [xftok], tag=f"xmf32_{c}")
        for sl in range(16):
            bk = sl % 4
            off = 128 * par
            st("tensor", lambda e, sl=sl, bk=bk, off=off, xf=xf: e.transpose(
                PB[bk][:, off:off + 128], xf[:, sl * 128:(sl + 1) * 128], idf_sb[:, :]),
               [xftok, "idf"], [BT[bk]], tag=f"xtr{c}_{sl}")
            st("vector", lambda e, sl=sl, bk=bk, off=off, par=par: e.tensor_copy(
                xm_tok[par][:, sl * 128:(sl + 1) * 128], PB[bk][:, off:off + 128]),
               [BT[bk], f"xm_tok{par}"], [f"xm_tok{par}"], tag=f"xtrev{c}_{sl}")

    def chunk_state(c):
        par = c % 2
        def smm(e, c=c, par=par):
            last = None
            for q4 in range(4):
                last = e.matmul(PB[q4][:, :], Bendtok_all[:, c * 128:(c + 1) * 128],
                                xm_tok[par][:, q4 * 512:(q4 + 1) * 512],
                                start=True, stop=True)
            return last
        st("tensor", smm, [f"Bet{c}", f"xm_tok{par}"], BT[0:4], tag=f"smm{c}")
        for q4 in range(4):
            st("vector", lambda e, q4=q4, c=c: e.scalar_tensor_tensor(
                S_run[:, q4 * 512:(q4 + 1) * 512], S_run[:, q4 * 512:(q4 + 1) * 512],
                dccol_all[:, c:c + 1], PB[q4][:, :], ALU.mult, ALU.add),
               ["S_run", f"dcc{c}", BT[q4]], ["S_run"], tag=f"sup{c}_{q4}")

    def prep_exps(c):
        par = c % 2
        m = MS[par]
        cc0 = 128 + c * 128
        chunk_exp(c, rows_sb[par][0:1, 0:128], f"rows{par}", m["A_bar"],
                  f"A_bar{par}", 0, "abar")
        chunk_exp(c, rows_sb[par][0:1, 128:256], f"rows{par}", m["ebufB"],
                  f"ebufB{par}", 1, "elc")
        st("vector", lambda e, cc0=cc0, m=m, par=par: e.tensor_mul(
            m["F1"][:, :], C_fm[:, cc0:cc0 + 128], m["ebufB"][:, :]),
           ["C_fm", f"ebufB{par}", f"F1{par}"], [f"F1{par}"], tag=f"f1_{c}")
        chunk_exp(c, rows_sb[par][0:1, 256:384], f"rows{par}", m["ebufA"],
                  f"ebufA{par}", 0, "er8")
        st("vector", lambda e, m=m, par=par: e.tensor_mul(
            m["F2base"][:, :], m["Btld"][:, :], m["ebufA"][:, :]),
           [f"Btld{par}", f"ebufA{par}", f"F2base{par}"], [f"F2base{par}"],
           tag=f"f2b{c}")
        chunk_exp(c, rows_sb[par][0:1, 384:512], f"rows{par}", m["ebufB"],
                  f"ebufB{par}", 1, "eg")
        st("vector", lambda e, cc0=cc0, m=m, par=par, c=c: e.tensor_mul(
            Cs_all[:, c * 128:(c + 1) * 128], C_fm[:, cc0:cc0 + 128],
            m["ebufB"][:, :]),
           ["C_fm", f"ebufB{par}", f"Cs{c}"], [f"Cs{c}", "cvt21"],
           tag=f"csx{c}")

    def f2_chain(c):
        par = c % 2
        m = MS[par]
        h = F2hist[par]
        htok = f"F2hist{par}"
        st("vector", lambda e, h=h: e.memset(h[:, 8:128], 0.0),
           [htok], [htok], tag=f"f2z{c}")
        st("vector", lambda e, h=h, m=m: e.tensor_copy(
            h[:, 0:8], m["F2base"][:, 0:8]),
           [f"F2base{par}", htok], [htok], tag=f"f2c{c}_1")
        for I in range(2, 16):
            st("vector", lambda e, h=h, I=I, par=par:
               e.tensor_scalar_mul(h[:, (I - 1) * 128:I * 128],
                                   h[:, (I - 2) * 128:(I - 1) * 128],
                                   d8[par][:, I - 1:I]),
               [htok, f"d8{par}"], [htok], tag=f"f2r{c}_{I}")
            st("vector", lambda e, h=h, I=I, m=m: e.tensor_copy(
                h[:, (I - 1) * 128 + (I - 1) * 8:(I - 1) * 128 + I * 8],
                m["F2base"][:, (I - 1) * 8:I * 8]),
               [f"F2base{par}", htok], [htok], tag=f"f2c{c}_{I}")

    def w_fin(c):
        par = c % 2
        ab = 4 + par
        abt = BT[ab]
        m = MS[par]
        h = F2hist[par]
        htok = f"F2hist{par}"
        for I in range(1, 16):
            st("tensor", lambda e, I=I, h=h, m=m, ab=ab: e.matmul(
                PB[ab][:, 256 + I * 8:256 + (I + 1) * 8],
                h[:, (I - 1) * 128:I * 128],
                m["F1"][:, I * 8:(I + 1) * 8], start=True, stop=True),
               [htok, f"F1{par}"], [abt], tag=f"wmm{c}_{I}")
        st("vector", lambda e, c=c, ab=ab: e.tensor_mul(
            Wsb_all[:, c * 128:(c + 1) * 128], PB[ab][:, 256:384],
            maskoff_sb[:, :]),
           [abt, "maskoff", f"Wsb{c}"], [f"Wsb{c}", "cvt20"], tag=f"wev{c}")

    def band_chain(c):
        par = c % 2
        ab = 4 + par
        abt = BT[ab]
        m = MS[par]
        cc0 = 128 + c * 128
        st("vector", lambda e, cc0=cc0, m=m, par=par: e.tensor_mul(
            Zbuf[par][:, 0:128], C_fm[:, cc0:cc0 + 128], m["Btld"][:, :]),
           ["C_fm", f"Btld{par}", f"Zbuf{par}"], [f"Zbuf{par}"], tag=f"z0_{c}")
        for k9 in range(1, BLK):
            src = m["Ppp"][(k9 - 1) % 2]
            dst = m["Ppp"][k9 % 2]
            stok, dtok = f"Ppp{par}_{(k9 - 1) % 2}", f"Ppp{par}_{k9 % 2}"
            if k9 == 1:
                st("vector", lambda e, dst=dst, m=m: e.tensor_copy(
                    dst[:, :], m["A_bar"][:, :]),
                   [f"A_bar{par}", dtok], [dtok], tag=f"p1_{c}")
            else:
                st("vector", lambda e, src=src, dst=dst, m=m: e.tensor_mul(
                    dst[:, 1:128], m["A_bar"][:, 1:128], src[:, 0:127]),
                   [f"A_bar{par}", stok, dtok], [dtok], tag=f"p{k9}_{c}")
            st("vector", lambda e, dst=dst, cc0=cc0, m=m: e.tensor_mul(
                m["zcw"][:, :], C_fm[:, cc0:cc0 + 128], dst[:, :]),
               ["C_fm", dtok, f"zcw{par}"], [f"zcw{par}"], tag=f"zcw{k9}_{c}")
            st("vector", lambda e, k9=k9, m=m, par=par: e.tensor_mul(
                Zbuf[par][:, k9 * 128 + k9:(k9 + 1) * 128],
                m["zcw"][:, k9:128], m["Btld"][:, 0:128 - k9]),
               [f"zcw{par}", f"Btld{par}", f"Zbuf{par}"], [f"Zbuf{par}"],
               tag=f"z{k9}_{c}")

    def band_fin(c):
        par = c % 2
        ab = 4 + par
        abt = BT[ab]
        for h2 in range(2):
            st("tensor", lambda e, h2=h2, par=par, ab=ab: e.matmul(
                PB[ab][0:1, 0:512], onescb[:, :],
                Zbuf[par][:, h2 * 512:(h2 + 1) * 512], start=True, stop=True),
               [f"Zbuf{par}", "onescb"], [abt], tag=f"ckmm{c}_{h2}")
            st("vector", lambda e, h2=h2, par=par, ab=ab: e.tensor_copy(
                ck_sb[par][:, h2 * 512:(h2 + 1) * 512], PB[ab][0:1, 0:512]),
               [abt, f"ck{par}"], [f"ck{par}"], tag=f"ckev{c}_{h2}")
        for h2 in range(2):
            st("tensor", lambda e, h2=h2, par=par, ab=ab: e.matmul(
                PB[ab][:, 0:512], ones1b[:, :],
                ck_sb[par][0:1, h2 * 512:(h2 + 1) * 512], start=True, stop=True),
               [f"ck{par}", "ones1b"], [abt], tag=f"ckbmm{c}_{h2}")
            st("vector", lambda e, h2=h2, par=par, ab=ab: e.tensor_copy(
                ckb[par][:, h2 * 512:(h2 + 1) * 512], PB[ab][:, 0:512]),
               [abt, f"ckb{par}"], [f"ckb{par}", "cvacc1"],
               tag=f"ckbev{c}_{h2}")
        for k9 in range(BLK):
            st("vector", lambda e, k9=k9, par=par, c=c: e.copy_predicated(
                Wsb_all[:, c * 128:(c + 1) * 128],
                bandmask_sb[:, k9 * 128:(k9 + 1) * 128],
                ckb[par][:, k9 * 128:(k9 + 1) * 128]),
               ["bandmask", f"ckb{par}", f"Wsb{c}"], [f"Wsb{c}"],
               tag=f"band{c}_{k9}")

    st("vector", lambda e: e.memset(S_run[:, :], 0.0), [], ["S_run"])
    for par in range(2):
        st("vector", lambda e, par=par: e.memset(MS[par]["F2run"][:, :], 0.0),
           [f"F2run{par}"], [f"F2run{par}"], tag=f"msf2_{par}")
        st("vector", lambda e, par=par: e.memset(F2runB[par][:, :], 0.0),
           [f"F2runB{par}"], [f"F2runB{par}"], tag=f"msf2b_{par}")
        st("vector", lambda e, par=par: e.memset(MS[par]["Ppp"][0][:, :], 1.0),
           [f"Ppp{par}_0"], [f"Ppp{par}_0"], tag=f"msp0_{par}")
        st("vector", lambda e, par=par: e.memset(MS[par]["Ppp"][1][:, :], 1.0),
           [f"Ppp{par}_1"], [f"Ppp{par}_1"], tag=f"msp1_{par}")

    def evac_z(jt, tb_i, t0, tn, bank):
        pz = tb_i % 2
        zb = zbuf[pz]
        ztok = f"zbuf{pz}"
        ct = cvt0[pz]
        ctok = f"cvt0{pz}"
        st("scalar", lambda e, tn=tn, bank=bank, ct=ct: e.activation(
            ct[:, 0:tn], PB[bank][:, 0:tn], AF.Silu),
           [BT[bank], ctok], [ctok], tag=f"zsilu{jt}_{tb_i}")
        st("vector", lambda e, tn=tn, zb=zb, ct=ct: e.tensor_scalar_min(
            zb[:, 0:tn], ct[:, 0:tn], CLIP),
           [ctok], [ztok])
        dma("sync", z_scr[jt * 128:(jt + 1) * 128, t0:t0 + tn], zb[:, 0:tn],
            [ztok], [f"zscr{jt}_{tb_i}"], tag=f"zst{jt}_{tb_i}")

    def evac_xm(jt, tb_i, t0, tn, bank):
        sl = jt - 16
        pb = prebuf[sl % 2]
        ptok = f"prebuf{sl % 2}"
        ct = cvt1[tb_i % 2]
        ctok = f"cvt1{tb_i % 2}"
        st("scalar", lambda e, tn=tn, bank=bank, ct=ct: e.activation(
            ct[:, 0:tn], PB[bank][:, 0:tn], AF.Silu),
           [BT[bank], ctok], [ctok], tag=f"xmsilu{jt}_{tb_i}")
        st("vector", lambda e, t0=t0, tn=tn, pb=pb, ct=ct: e.tensor_scalar_min(
            pb[:, t0:t0 + tn], ct[:, 0:tn], CLIP),
           [ctok, ptok], [ptok])

    def evac_bc(jt, tb_i, t0, tn, bank):
        dst = B_fm if jt == 32 else C_fm
        tok = "B_fm" if jt == 32 else "C_fm"
        st("vector", lambda e, dst=dst, t0=t0, tn=tn, bank=bank: e.tensor_copy(
            dst[:, t0:t0 + tn], PB[bank][:, 0:tn]),
           [BT[bank]], [tok], tag=f"bc{jt}_{tb_i}")

    def conv_slot(sl):
        pb = prebuf[sl % 2]
        ptok = f"prebuf{sl % 2}"
        acc = cvacc[sl % 2]
        atok = f"cvacc{sl % 2}"
        xo = xmout[sl % 2]
        xotok = f"xmout{sl % 2}"
        st("vector", lambda e, sl=sl, pb=pb, acc=acc: e.tensor_scalar(
            acc[:, :], pb[:, 125:125 + T], cw_sb[:, sl * 4:sl * 4 + 1], None, ALU.mult),
           [ptok, "cw", atok], [atok], tag=f"cv0_{sl}")
        for k4 in range(1, 4):
            st("vector", lambda e, sl=sl, k4=k4, pb=pb, acc=acc: e.scalar_tensor_tensor(
                acc[:, :], pb[:, 125 + k4:125 + k4 + T],
                cw_sb[:, sl * 4 + k4:sl * 4 + k4 + 1], acc[:, :], ALU.mult, ALU.add),
               [ptok, "cw", atok], [atok], tag=f"cv{k4}_{sl}")
        for hh in range(2):
            cv = cvt2[hh]
            cvtok = f"cvt2{hh}"
            st("scalar", lambda e, acc=acc, hh=hh, cv=cv: e.activation(
                cv[:, :], acc[:, hh * 1024:(hh + 1) * 1024], AF.Silu),
               [atok, cvtok], [cvtok], tag=f"cs{hh}_{sl}")
        xs = xmst[sl % 2]
        xstok = f"xmst{sl % 2}"
        for cc in range(NCH):
            bk = 4 + cc % 4
            cv = cvt2[cc // 8]
            cvtok = f"cvt2{cc // 8}"
            st("tensor", lambda e, cc=cc, bk=bk, cv=cv: e.transpose(
                PB[bk][:, 0:128], cv[:, (cc % 8) * 128:(cc % 8 + 1) * 128],
                idf_sb[:, :]),
               [cvtok, "idf"], [BT[bk]], tag=f"cvtr{sl}_{cc}")
            st("vector", lambda e, cc=cc, bk=bk, xs=xs: e.tensor_scalar_min(
                xs[:, cc * 128:(cc + 1) * 128], PB[bk][:, 0:128], CLIP),
               [BT[bk], xstok], [xstok], tag=f"cvtrev{sl}_{cc}")
        dma("sync",
            xmT_scr[:, sl * 128:(sl + 1) * 128].rearrange("(c t) f -> t c f",
                                                          t=128),
            xs[:, :].rearrange("p (c f) -> p c f", c=16),
            [xstok], [f"xmT_{sl}"], tag=f"stxmT{sl}")

    ipq = 0
    for jti, jt in enumerate(list(range(16, 34)) + list(range(16))):
        wb = wbuf[jti % 2]
        wtok = f"wbuf{jti % 2}"
        dma("gpsimd", wb[:, :].rearrange("p (s j) -> p s j", s=8),
            WinT.rearrange("(s p) j -> p s j", p=128)[:, :, jt * 128:(jt + 1) * 128],
            [], [wtok], tag=f"ldw{jt}")
        for tb_i, (t0, tn) in enumerate(TBATCH):
            bank = ipq % 4
            ipq += 1
            def mmj(e, wb=wb, t0=t0, tn=tn, bank=bank):
                last = None
                for dm8 in range(8):
                    last = e.matmul(
                        PB[bank][:, 0:tn],
                        wb[:, dm8 * 128:(dm8 + 1) * 128],
                        xnT_hi[:, dm8 * TW + t0: dm8 * TW + t0 + tn],
                        start=(dm8 == 0), stop=(dm8 == 7))
                return last
            st("tensor", mmj, [wtok], [BT[bank]], tag=f"ip{jt}_{tb_i}")
            if jt < 16:
                evac_z(jt, tb_i, t0, tn, bank)
            elif jt < 32:
                evac_xm(jt, tb_i, t0, tn, bank)
            else:
                evac_bc(jt, tb_i, t0, tn, bank)
        if 16 <= jt < 32:
            conv_slot(jt - 16)
        def stage_B(c):
            chunk_rows(c)
            chunk_btld(c)
            prep_exps(c)
            f2_chain(c)
            band_chain(c)

        def stage_C(c):
            w_fin(c)
            band_fin(c)
            chunk_bend(c)

        def stage_D(c):
            par = c % 2
            dma("gpsimd", xm_tok[par][:, :], xmT_scr[c * 128:(c + 1) * 128, :],
                [f"xm_tok{par}"] + [f"xmT_{sl}" for sl in range(16)],
                [f"xm_tok{par}"], tag=f"ldxmT3a_{c}")
            chunk_state(c)

        if 0 <= jti - 18 < NCH:
            stage_B(jti - 18)
        if 0 <= jti - 19 < NCH:
            stage_C(jti - 19)
        if 0 <= jti - 20 < NCH:
            stage_D(jti - 20)

    stage_C(15)
    stage_D(14)
    stage_D(15)

    dma("sync", ag_in.ap(), S_run[:, :], ["S_run"], ["ag_in"], tag="agst")
    p.step("gpsimd", lambda e: e.collective_compute(
        "AllGather", ALU.bypass, replica_groups=[list(range(8))],
        ins=[ag_in.ap().opt()], outs=[ag_out.ap().opt()]),
        reads=["ag_in"], writes=["ag_out"], done_sem=("cc", 1), tag="allgather")

    p.barrier()
    if DEBUG_PHASE == "p3a":
        p.final_wait()
        return nc, p

    for i in range(16):
        st("tensor", lambda e, i=i: e.transpose(
            PB[6][0:8, 0:128], combine[:, (i + 1) * 8:(i + 2) * 8], idf_sb[:, :]),
           ["combine", "idf"], ["B6"], tag=f"sqtr{i}")
        st("vector", lambda e, i=i: e.tensor_copy(
            sq_fm[0:8, i * 128:(i + 1) * 128], PB[6][0:8, 0:128]),
           ["B6"], ["sq_fm"], tag=f"sqev{i}")
    for dm8 in range(8):
        xs = xnTs[dm8 % 2]
        xst = f"xnTs{dm8 % 2}"
        dma("sync", xnT_e[:, dm8 * 2048:(dm8 + 1) * 2048],
            xnT_scr[:, dm8 * TW + 128: dm8 * TW + TW],
            [f"xnT_e{dm8}"], [f"xnT_e{dm8}"], tag=f"ldxnT{dm8}")

    wti = 0
    for e8 in range(E):
        we = WIDTHS[e8]
        def sqbmm(e, e8=e8):
            last = None
            for q4 in range(4):
                last = e.matmul(PB[q4][:, :], eyerow8_sb[:, e8 * 128:(e8 + 1) * 128],
                                sq_fm[0:8, q4 * 512:(q4 + 1) * 512],
                                start=True, stop=True)
            return last
        st("tensor", sqbmm, ["sq_fm", "eyerow8"], BT[0:4], tag=f"sqb{e8}")
        for q4 in range(4):
            st("vector", lambda e, q4=q4: e.tensor_copy(
                sqb_sb[:, q4 * 512:(q4 + 1) * 512], PB[q4][:, :]),
               [BT[q4], "sqb"], ["sqb"], tag=f"sqbev{e8}_{q4}")
        for ch in range(we // 512):
            cw0 = int(OFFS[e8]) + ch * 512
            wb1 = w1buf[ch % 2]
            wt1 = f"w1buf{ch % 2}"
            dma("gpsimd", wb1[:, :].rearrange("p (s w) -> p s w", s=8),
                w1.rearrange("(s p) w -> p s w", p=128)[:, :, cw0:cw0 + 512],
                [wt1], [wt1], tag=f"ldw1_{e8}_{ch}")
            for wt4 in range(4):
                pbase = (wti % 2) * 4
                def mm1(e, wb1=wb1, wt4=wt4, pbase=pbase):
                    last = None
                    for q4 in range(4):
                        for dm8 in range(8):
                            last = e.matmul(
                                PB[pbase + q4][:, :],
                                wb1[:, dm8 * 512 + wt4 * 128: dm8 * 512 + (wt4 + 1) * 128],
                                xnT_e[:, dm8 * 2048 + q4 * 512: dm8 * 2048 + (q4 + 1) * 512],
                                start=(dm8 == 0), stop=(dm8 == 7))
                    return last
                st("tensor", mm1, [wt1] + [f"xnT_e{d}" for d in range(8)],
                   [BT[pbase + q] for q in range(4)], tag=f"mm1_{wti}")
                hb = Hsqb[wti % 2]
                htok = f"Hsqb{wti % 2}"
                for q4 in range(4):
                    st("scalar", lambda e, pbase=pbase, q4=q4: e.activation(
                        Hrelu[:, q4 * 512:(q4 + 1) * 512], PB[pbase + q4][:, :], AF.Relu),
                       [BT[pbase + q4], "Hrelu"], ["Hrelu"], tag=f"rel{wti}_{q4}")
                    st("vector", lambda e, q4=q4: e.tensor_mul(
                        Hsq2[:, q4 * 512:(q4 + 1) * 512],
                        Hrelu[:, q4 * 512:(q4 + 1) * 512],
                        Hrelu[:, q4 * 512:(q4 + 1) * 512]),
                       ["Hrelu", "Hsq2"], ["Hsq2"], tag=f"hsq{wti}_{q4}")
                    st("vector", lambda e, q4=q4, hb=hb: e.tensor_mul(
                        hb[:, q4 * 512:(q4 + 1) * 512],
                        Hsq2[:, q4 * 512:(q4 + 1) * 512],
                        sqb_sb[:, q4 * 512:(q4 + 1) * 512]),
                       ["Hsq2", "sqb", htok], [htok], tag=f"hsc{wti}_{q4}")
                dma("sync", H_scr[wti * 128:(wti + 1) * 128, :], hb[:, :],
                    [htok], [f"Hrow{wti}"], tag=f"sth{wti}")
                wti += 1
    assert wti == 96

    p.barrier()

    bankc = 0
    for wb8 in range(12):
        wb2 = w2buf[wb8 % 2]
        wt2 = f"w2buf{wb8 % 2}"
        dma("gpsimd", wb2[:, :].rearrange("p (s d) -> p s d", s=8),
            w2[wb8 * 1024:(wb8 + 1) * 1024, :].rearrange("(s p) d -> p s d", p=128),
            [wt2], [wt2], tag=f"ldw2_{wb8}")
        for th in range(2):
            hb2 = hbuf4[(wb8 * 2 + th) % 2]
            ht2 = f"hbuf4_{(wb8 * 2 + th) % 2}"
            dma("sync", hb2[:, :].rearrange("p (s t) -> p s t", s=8),
                H_scr[wb8 * 1024:(wb8 + 1) * 1024,
                      th * 1024:(th + 1) * 1024].rearrange("(s p) t -> p s t", p=128),
                [ht2], [ht2], tag=f"ldh{wb8}_{th}")
            for tt in range(8):
                for hf in range(2):
                    bank = bankc % 8
                    bankc += 1
                    def mm2(e, hb2=hb2, wb2=wb2, tt=tt, hf=hf, bank=bank):
                        last = None
                        for sl8 in range(8):
                            last = e.matmul(
                                PB[bank][:, :],
                                hb2[:, sl8 * 1024 + tt * 128:
                                    sl8 * 1024 + (tt + 1) * 128],
                                wb2[:, sl8 * 1024 + hf * 512:
                                    sl8 * 1024 + (hf + 1) * 512],
                                start=(sl8 == 0), stop=(sl8 == 7))
                        return last
                    st("tensor", mm2, [ht2, wt2], [BT[bank]],
                       tag=f"mm2_{wb8}_{th}_{tt}_{hf}")
                    t16 = th * 8 + tt
                    ab = accs[t16]
                    atok = f"acc{t16}"
                    a0 = hf * 512
                    if wb8 == 0:
                        st("vector", lambda e, bank=bank, a0=a0, ab=ab:
                           e.tensor_copy(ab[:, a0:a0 + 512], PB[bank][:, :]),
                           [BT[bank]], [atok], tag=f"acccp{th}_{tt}_{hf}")
                    else:
                        st("vector", lambda e, bank=bank, a0=a0, ab=ab:
                           e.tensor_add(ab[:, a0:a0 + 512], ab[:, a0:a0 + 512],
                                        PB[bank][:, :]),
                           [BT[bank], atok], [atok],
                           tag=f"accad{wb8}_{th}_{tt}_{hf}")
    for t16 in range(16):
        gb = gout[t16 % 2]
        gtok = f"gout{t16 % 2}"
        st("scalar", lambda e, t16=t16, gb=gb: e.activation(
            gb[:, :], accs[t16][:, :], AF.Silu),
           [f"acc{t16}", gtok], [gtok], tag=f"gsilu{t16}")
        dma("sync", g_scr[t16 * 128:(t16 + 1) * 128, :], gb[:, :],
            [gtok], [f"gscr{t16}"], tag=f"stg{t16}")

    p.barrier()
    if DEBUG_PHASE == "p2":
        p.final_wait()
        return nc, p

    st("vector", lambda e: e.memset(S_run[:, :], 0.0), [], ["S_run"])
    cc_wait = [("cc", 1)]
    hb_bufs = [cvacc[0], xmdp]
    for r8 in range(8):
        bb = hb_bufs[r8 % 2]
        btok = f"hbb{r8 % 2}"
        p.step("sync" if r8 % 2 == 0 else "gpsimd",
               lambda e, r8=r8, bb=bb: e.dma_start(
                   out=bb[:, :], in_=ag_out.ap()[r8 * 128:(r8 + 1) * 128, :]),
               reads=[btok], writes=[btok], dma=True, tag=f"ldag{r8}",
               extra_waits=cc_wait if r8 < 2 else ())
        st("vector", lambda e, r8=r8, bb=bb: e.scalar_tensor_tensor(
            S_run[:, :], bb[:, :], sel8_sb[:, r8:r8 + 1], S_run[:, :],
            ALU.mult, ALU.add),
           [btok, "sel8", "S_run"], ["S_run"], tag=f"hinit{r8}")

    dma("gpsimd", WoutT_sb[:, :].rearrange("p (s j) -> p s j", s=16),
        WoutT.rearrange("(s p) j -> p s j", p=128), [], ["WoutT_raw"], tag="ldwo")
    for sl in range(16):
        st("vector", lambda e, sl=sl: e.tensor_scalar_mul(
            WoutT_sb[:, sl * 1024:(sl + 1) * 1024],
            WoutT_sb[:, sl * 1024:(sl + 1) * 1024], nw_sb[:, sl:sl + 1]),
           ["WoutT_raw", "nw", "WoutT"], ["WoutT"], tag=f"wosc{sl}")
    dma("scalar", Dp_bcast[:, :], dp_b, [], ["Dp_bcast"], tag="lddpb")

    def main_dmas(c):
        par = c % 2
        rc0 = c * 128
        cc0 = 128 + c * 128
        dma("sync", zc[par][:, :].rearrange("p (s t) -> p s t", s=16),
            z_scr[:, cc0:cc0 + 128].rearrange("(s p) t -> p s t", p=128),
            [f"zc{par}"], [f"zc{par}"], tag=f"ldz{c}")
        dma("sync", gbuf[par][:, :], g_scr[rc0:rc0 + 128, :], [f"gbuf{par}"],
            [f"gbuf{par}"], tag=f"ldg{c}")
        dma("sync", xres[par][:, :], xin[cc0:cc0 + 128, :], [f"xres{par}"],
            [f"xres{par}"], tag=f"ldxr{c}")
        dma("scalar", xm_tok[par][:, :], xmT_scr[rc0:rc0 + 128, :],
            [f"xm_tok{par}"], [f"xm_tok{par}"], tag=f"ldxmT{c}")

    def main_state(c):
        par = c % 2
        sbf = S_run_bf[par]
        sbtok = f"S_run_bf{par}"
        st("vector", lambda e, sbf=sbf: e.tensor_copy(sbf[:, :], S_run[:, :]),
           ["S_run", sbtok], [sbtok], tag=f"srbf{c}")
        chunk_state(c)

    def main_y(c):
        par = c % 2
        sbf = S_run_bf[par]
        sbtok = f"S_run_bf{par}"
        def ymm(e, c=c, par=par, sbf=sbf):
            last = None
            for q4 in range(4):
                e.matmul(PB[q4][:, :], Cs_all[:, c * 128:(c + 1) * 128],
                         sbf[:, q4 * 512:(q4 + 1) * 512], start=True, stop=False)
                last = e.matmul(PB[q4][:, :], Wsb_all[:, c * 128:(c + 1) * 128],
                                xm_tok[par][:, q4 * 512:(q4 + 1) * 512],
                                start=False, stop=True)
            return last
        st("tensor", ymm, [sbtok, f"Cs{c}", f"xm_tok{par}", f"Wsb{c}"],
           BT[0:4], tag=f"ymm{c}")
        st("vector", lambda e, par=par: e.tensor_mul(
            xmdp[:, :], xm_tok[par][:, :], Dp_bcast[:, :]),
           [f"xm_tok{par}", "Dp_bcast", "xmdp"], ["xmdp"], tag=f"xmdp{c}")
        yt = y_fm[par]
        yttok = f"y_fm{par}"
        for q4 in range(4):
            st("vector", lambda e, q4=q4, yt=yt: e.tensor_add(
                yt[:, q4 * 512:(q4 + 1) * 512], PB[q4][:, :],
                xmdp[:, q4 * 512:(q4 + 1) * 512]),
               [BT[q4], "xmdp", yttok], [yttok], tag=f"yev{c}_{q4}")
        st("scalar", lambda e, yt=yt, par=par: e.activation(
            xmdp[:, :], yt[:, :], AF.Square, accum_out=rstd_col[par][:, :]),
           [yttok, "xmdp", f"rstd_col{par}"], ["xmdp", f"rstd_col{par}"],
           tag=f"ysq{c}")
        st("scalar", lambda e, par=par: e.activation(
            rstd_col[par][:, :], rstd_col[par][:, :], AF.Ln,
            scale=1.0 / DI, bias=cbias[:, 0:1]),
           [f"rstd_col{par}", "cbias"], [f"rstd_col{par}"], tag=f"rln{c}")
        st("scalar", lambda e, par=par: e.activation(
            rstd_col[par][:, :], rstd_col[par][:, :], AF.Exp, scale=-0.5),
           [f"rstd_col{par}"], [f"rstd_col{par}"], tag=f"rexp{c}")

    def main_gate(c):
        par = c % 2
        mb = 6 + par
        yt = y_fm[par]
        yttok = f"y_fm{par}"
        for sl in range(16):
            off = 128 * (sl % 4)
            st("tensor", lambda e, sl=sl, mb=mb, off=off, yt=yt: e.transpose(
                PB[mb][:, off:off + 128], yt[:, sl * 128:(sl + 1) * 128],
                idf_sb[:, :]),
               [yttok, "idf"], [BT[mb]], tag=f"ytr{c}_{sl}")
            st("vector", lambda e, sl=sl, mb=mb, off=off, par=par:
               e.tensor_mul(
                   yg[par][:, sl * 128:(sl + 1) * 128], PB[mb][:, off:off + 128],
                   zc[par][:, sl * 128:(sl + 1) * 128]),
               [BT[mb], f"zc{par}", f"yg{par}"], [f"yg{par}"],
               tag=f"ygev{c}_{sl}")

    def main_out(c):
        par = c % 2
        rc0 = c * 128
        def omm(e, par=par):
            last = None
            for d16 in range(16):
                for hf in range(2):
                    last = e.matmul(
                        PB[4 + hf][:, :], yg[par][:, d16 * 128:(d16 + 1) * 128],
                        WoutT_sb[:, d16 * 1024 + hf * 512:
                                 d16 * 1024 + (hf + 1) * 512],
                        start=(d16 == 0), stop=(d16 == 15))
            return last
        st("tensor", omm, [f"yg{par}", "WoutT"], [BT[4], BT[5]], tag=f"omm{c}")
        ob = outbuf[par]
        otok = f"outbuf{par}"
        for hf in range(2):
            st("vector", lambda e, hf=hf, par=par: e.scalar_tensor_tensor(
                sgbuf[par][:, hf * 512:(hf + 1) * 512], PB[4 + hf][:, :],
                rstd_col[par][:, :],
                gbuf[par][:, hf * 512:(hf + 1) * 512], ALU.mult, ALU.mult),
               [BT[4 + hf], f"rstd_col{par}", f"gbuf{par}", f"yg{par}"],
               [f"yg{par}"], tag=f"sg{c}_{hf}")
        st("vector", lambda e, ob=ob, par=par: e.tensor_add(
            ob[:, :], sgbuf[par][:, :], xres[par][:, :]),
           [f"yg{par}", f"xres{par}", otok], [otok], tag=f"fin{c}")
        dma("sync", out[rc0:rc0 + 128, :], ob[:, :], [otok], [f"out{c}"],
            tag=f"stout{c}")

    for pr in range(NCH // 2):
        cs = (2 * pr, 2 * pr + 1)
        for c in cs:
            main_dmas(c)
        for c in cs:
            main_state(c)
        for c in cs:
            main_y(c)
        for c in cs:
            main_gate(c)
        for c in cs:
            main_out(c)

    p.final_wait()
    return nc, p


_nc_cache = {}


def _get_program():
    if "nc" not in _nc_cache:
        nc, p = build_program()
        with nc.Block() as block, \
             nc.semaphore("s_t") as s_t, nc.semaphore("s_v") as s_v, \
             nc.semaphore("s_s") as s_s, nc.semaphore("s_g") as s_g, \
             nc.semaphore("s_y") as s_y, nc.semaphore("s_cc") as s_cc:
            p.emit(nc, block, {"tensor": s_t, "vector": s_v, "scalar": s_s,
                               "gpsimd": s_g, "sync": s_y, "cc": s_cc})
        _nc_cache["nc"] = nc
    return _nc_cache["nc"]


def _host_inputs(L0_attn_x, L0_attn_in_proj_w, L0_attn_conv_w, L0_attn_A_log,
                 L0_attn_Dp, L0_attn_norm_w, L0_attn_out_proj_w,
                 L0_ffn_router_w, L0_ffn_w1, L0_ffn_w2):
    f32 = np.float32
    x = np.ascontiguousarray(np.asarray(L0_attn_x, dtype=f32))
    Win = np.ascontiguousarray(np.asarray(L0_attn_in_proj_w, dtype=f32))
    WinT = np.ascontiguousarray(Win.T)
    w1 = np.ascontiguousarray(np.asarray(L0_ffn_w1, dtype=f32))
    w2 = np.ascontiguousarray(np.asarray(L0_ffn_w2, dtype=f32))
    WoutT = np.ascontiguousarray(np.asarray(L0_attn_out_proj_w, dtype=f32).T)
    small9 = np.ascontiguousarray(
        np.concatenate([np.asarray(L0_ffn_router_w, f32).T,
                        Win[4352:4353, :].T], axis=1))
    A_log_row = np.ascontiguousarray(np.asarray(L0_attn_A_log, f32).reshape(1, S))
    cw = np.asarray(L0_attn_conv_w, f32).reshape(DI, 4)
    cw_col = np.ascontiguousarray(
        cw.reshape(16, 128, 4).transpose(1, 0, 2).reshape(128, 64))
    dp_col = np.ascontiguousarray(np.asarray(L0_attn_Dp, f32).reshape(16, 128).T)
    dp_b = np.ascontiguousarray(
        np.broadcast_to(np.asarray(L0_attn_Dp, f32).reshape(1, DI), (128, DI)))
    nw_col = np.ascontiguousarray(np.asarray(L0_attn_norm_w, f32).reshape(16, 128).T)

    tt = np.arange(128)
    M8 = ((tt[:, None] // BLK == tt[None, :] // BLK) & (tt[:, None] <= tt[None, :]))
    M8r = ((tt[:, None] // BLK == tt[None, :] // BLK) & (tt[:, None] > tt[None, :]))
    Mlow = tt[:, None] <= tt[None, :]
    Mrows = np.ascontiguousarray(
        np.concatenate([np.eye(128), M8, M8r, Mlow], axis=1).astype(f32))
    maskoff = np.ascontiguousarray(
        (tt[:, None] // BLK < tt[None, :] // BLK).astype(f32))
    bandmask = np.ascontiguousarray(np.concatenate(
        [((tt[None, :] - tt[:, None] == k) & (tt[:, None] // BLK == tt[None, :] // BLK))
         for k in range(BLK)], axis=1).astype(np.uint8))
    idf32 = np.eye(128, dtype=f32)
    eyerow8 = np.zeros((8, 8 * 128), f32)
    for e in range(8):
        eyerow8[e, e * 128:(e + 1) * 128] = 1.0

    shared = dict(WinT=WinT, w1=w1, w2=w2, WoutT=WoutT, small9=small9,
                  A_log_row=A_log_row, cw_col=cw_col, dp_col=dp_col,
                  dp_b=dp_b, nw_col=nw_col, Mrows=Mrows, maskoff=maskoff,
                  bandmask=bandmask, idf32=idf32, eyerow8=eyerow8)

    in_maps = []
    for core in range(8):
        b, h = core // 2, core % 2
        xin = np.zeros((TW, DM), f32)
        if h == 1:
            xin[125:128] = x[b, T - 3:T]
        xin[128:] = x[b, h * T:(h + 1) * T]
        sel = np.zeros((128, 8), f32)
        if h == 1:
            sel[:, 2 * b] = 1.0
        m = dict(shared)
        m["xin"] = np.ascontiguousarray(xin)
        m["sel8"] = sel
        in_maps.append(m)
    return in_maps


_LAST_RES = None


def kernel(**inputs):
    global _LAST_RES
    nc = _get_program()
    in_maps = _host_inputs(**inputs)
    res = run_bass_kernel_spmd(nc, in_maps, core_ids=list(range(8)))
    _LAST_RES = res
    outp = np.empty((4, 4096, 1024), np.float32)
    for core in range(8):
        b, h = core // 2, core % 2
        outp[b, h * T:(h + 1) * T] = np.asarray(res.results[core]["out"])
    return outp
